# version 06
# baseline (speedup 1.0000x reference)
import contextlib
import os
import numpy as np
import concourse.bass as bass
import concourse.mybir as mybir
from concourse.bass_utils import run_bass_kernel_spmd

F32 = mybir.dt.float32
AF = mybir.ActivationFunctionType
ALU = mybir.AluOpType
AX = mybir.AxisListType

D = 1024
DI = 2048
FZ = 4096
HID = 512
NE = 16
TO = 2048
TA = 4096
TAL = 4097
WP = 256
NT2 = TA // WP
WA = 512
NTA = TA // WA
TH = 1024
EPS = 1e-5
BIG = 1.0e5
SET_LNEXP = 6
SET_SILU = 18
KC_D = D // 128
KC_I = DI // 128
WPP = WP + 2


def _load_act_set(nc, eng, idx):
    eng.add_instruction(mybir.InstLoadActFuncSet(
        name=nc.get_next_instruction_name(), act_func_set_id=idx,
        ins=[], outs=[]))


def build_program():
    phases = int(os.environ.get("ATH_PHASES", "4"))
    skips = set(os.environ.get("ATH_SKIP", "").split(","))
    nc = bass.Bass("TRN2", target_bir_lowering=False,
                   detect_race_conditions=False)

    def din(name, shape):
        return nc.dram_tensor(name, shape, F32, kind="ExternalInput").ap()

    x_fm = din("x_fm", [D, TAL])
    x_tm = din("x_tm", [TO, D])
    w_in = din("w_in", [D, FZ])
    conv_w = din("conv_w", [128, KC_I * 3])
    wxp = din("wxp", [DI, 97])
    w_dt = din("w_dt", [64, DI])
    b_dt_c = din("b_dt_c", [128, KC_I])
    w_out = din("w_out", [DI, D])
    ln_g_c = din("ln_g_c", [128, KC_I])
    ln_b_c = din("ln_b_c", [128, KC_I])
    gate_wT = din("gate_wT", [D, NE])
    w13 = din("w13", [NE, D, 1024])
    w2t = din("w2t", [NE, HID, D])
    sw13 = din("sw13", [D, 1024])
    sw2t = din("sw2t", [HID, D])
    consts = din("consts", [128, 8])
    ident = din("ident", [128, 128])
    iota16 = din("iota16", [128, NE])
    iotamb = din("iotamb", [128, NE])
    p16 = din("p16", [16, 128])
    sel16 = din("sel16", [16, NE * 128])
    ones_r = din("ones_r", [1, 128])

    dbg = os.environ.get("ATH_DBGOUT", "0") == "1"
    ikind = "ExternalOutput" if dbg else "Internal"
    xn_sp = nc.dram_tensor("xn_sp", [D, TO], F32, kind=ikind).ap()
    ffn_sp = nc.dram_tensor("ffn_sp", [TO, D], F32, kind=ikind).ap()
    dbg_t = (nc.dram_tensor("dbg_t", [128, 8192], F32, kind="ExternalOutput").ap()
             if dbg else None)
    out = nc.dram_tensor("out", [TO, D], F32, kind="ExternalOutput").ap()

    ctx = contextlib.ExitStack()
    with ctx:
        def sbuf(name, shape):
            return ctx.enter_context(nc.sbuf_tensor(name, shape, F32))

        s_consts = sbuf("s_consts", [128, 8])
        s_ident = sbuf("s_ident", [128, 128])
        s_iota = sbuf("s_iota", [128, NE])
        s_iotamb = sbuf("s_iotamb", [128, NE])
        s_p16 = sbuf("s_p16", [16, 128])
        s_ones_r = sbuf("s_ones_r", [1, 128])
        s_convw = sbuf("s_convw", [128, KC_I * 3])
        s_bdt = sbuf("s_bdt", [128, KC_I])
        s_lng = sbuf("s_lng", [128, KC_I])
        s_lnb = sbuf("s_lnb", [128, KC_I])

        SBTOP = nc.SBUF_PARTITION_SIZE_BYTES
        MBASE = 26624
        assert nc.sbuf_base <= MBASE, nc.sbuf_base
        cur = [MBASE]

        def at(name, shape, off=None):
            if off is None:
                off = cur[0]
                nbytes = int(np.prod(shape[1:])) * 4
                cur[0] = off + ((nbytes + 31) // 32) * 32
            h = nc.alloc_sbuf_tensor_at(name, shape, F32, offset=off)
            assert off + int(np.prod(shape[1:])) * 4 <= SBTOP, (name, off)
            return h

        yrep = at("yrep", [128, TO])
        comb_fm = at("comb_fm", [16, TO])
        y16 = at("y16", [16, WA])
        y16b = at("y16b", [16, WA])
        tkl = at("tkl", [128, NE])
        tkcur = at("tkcur", [128, NE])
        tkeq = at("tkeq", [128, NE])
        tkcand = at("tkcand", [128, NE])
        tksel = at("tksel", [128, NE])
        tkw = at("tkw", [128, NE])
        tkm = at("tkm", [128, 8])
        PERM_END = cur[0]
        r_dec = at("r_dec", [1, TA])
        r_c = at("r_c", [1, TA])
        dt_lo = at("dt_lo", [128, TO])
        cp_buf = at("cp_buf", [16, TO])
        lg_fm = at("lg_fm", [16, TO])
        MID_END = cur[0]
        r_s = r_c
        w_in_h = at("w_in_h", [128, KC_D * DI])
        w_xp = at("w_xp", [128, KC_I * 97])
        w_gate = at("w_gate", [128, KC_D * NE])
        WPS = 512
        xt = [at(f"xt{i}", [128, KC_D * WPS]) for i in range(2)]
        xzb = at("xzb", [128, WPS])
        xsb = [at(f"xsb{i}", [128, WPS]) for i in range(2)]
        cacc = [at(f"cacc{i}", [128, WPS]) for i in range(2)]
        bpcp = cacc[0]
        r_mu = at("r_mu", [1, WPS])
        r_var = at("r_var", [1, WPS])
        r_inv = at("r_inv", [1, WPS])
        r_xm = at("r_xm", [1, WPS])
        assert cur[0] <= SBTOP, cur[0]
        r_bps = at("r_bps", [1, WPS], off=_off(nc, xzb))
        w_dt_s = at("w_dt_s", [128, DI], off=_off(nc, xt[0]))
        r_dtm = at("r_dtm", [1, WA], off=_off(nc, xt[0]) + 8192)
        dtscr = [at(f"dtscr{i}", [128, WA], off=_off(nc, xt[1]) + i * 2048)
                 for i in range(2)]
        r_muy = at("r_muy", [1, TO], off=_off(nc, xt[1]) + 4096)
        r_invy = at("r_invy", [1, TO], off=_off(nc, w_in_h))
        r_tmp = at("r_tmp", [1, TO], off=_off(nc, w_in_h) + 8192)
        p3 = [PERM_END]

        def at3(name, shape):
            off = p3[0]
            nbytes = int(np.prod(shape[1:])) * 4
            p3[0] = off + ((nbytes + 31) // 32) * 32
            h = nc.alloc_sbuf_tensor_at(name, shape, F32, offset=off)
            assert off + nbytes <= SBTOP, (name, off)
            return h

        xn_h = at3("xn_h", [128, KC_D * TH])
        acc = at3("acc", [128, 8 * D])
        w13_s = [at3(f"w13_s{i}", [128, KC_D * 1024]) for i in range(2)]
        w2_s = at3("w2_s", [128, 4 * D])
        act_b = at3("act_b", [128, 4 * TH])
        silu_b = at3("silu_b", [128, 512])
        h3c_b = at3("h3c_b", [128, 512])
        cbc_b = at3("cbc_b", [128, TH])
        s_sel = at3("s_sel", [16, NE * 128])
        assert p3[0] <= SBTOP, p3[0]
        p4 = [PERM_END]

        def at4(name, shape):
            off = p4[0]
            nbytes = int(np.prod(shape[1:])) * 4
            p4[0] = off + ((nbytes + 31) // 32) * 32
            h = nc.alloc_sbuf_tensor_at(name, shape, F32, offset=off)
            assert off + nbytes <= SBTOP, (name, off)
            return h

        w_in_h4 = at4("w_in_h4", [128, KC_D * DI])
        w_out_s = at4("w_out_s", [128, KC_I * D])
        xn_t = [at4(f"xn_t{i}", [128, KC_D * WP]) for i in range(2)]
        xtm_t = [at4(f"xtm_t{i}", [128, D]) for i in range(2)]
        ffn_t = [at4(f"ffn_t{i}", [128, D]) for i in range(2)]
        szb = [at4(f"szb{i}", [128, WP]) for i in range(2)]
        ab = [at4(f"ab{i}", [128, WP]) for i in range(2)]
        ob = at4("ob", [128, 2 * D])
        assert p4[0] <= SBTOP, p4[0]

        pb = [ctx.enter_context(nc.psum_tensor(f"pb{i}", [128, 512], F32))
              for i in range(8)]

        sems = {}
        for sname in ["s_x", "s_w", "s_sp", "s_mm", "s_ve", "s_se"]:
            sems[sname] = ctx.enter_context(nc.semaphore(sname))
        cnt = {k: 0 for k in sems}

        block = ctx.enter_context(nc.Block())
        plan = {e: [] for e in ("sync", "tensor", "vector", "scalar", "gpsimd")}
        marks = {}

        def mark(name):
            marks[name] = {e: len(plan[e]) for e in plan}

        def emit(eng, fn):
            plan[eng].append(fn)

        def dma(q, out_ap, in_ap, sem, slow=False):
            cnt[sem] += 16
            s = sems[sem]
            if slow:
                emit(q, lambda e, o=out_ap, i=in_ap, s=s:
                     e.dma_start(out=o, in_=i, allow_slow_non_contiguous=True)
                     .then_inc(s, 16))
            else:
                emit(q, lambda e, o=out_ap, i=in_ap, s=s:
                     e.dma_start(out=o, in_=i).then_inc(s, 16))
            return cnt[sem]

        def wait(eng, sem, val=None):
            v = cnt[sem] if val is None else val
            s = sems[sem]
            emit(eng, lambda e, s=s, v=v: e.wait_ge(s, v))

        def snap(sem):
            return cnt[sem]

        def op(eng, fn, sem=None):
            if sem is None:
                emit(eng, fn)
                return None
            cnt[sem] += 1
            s = sems[sem]
            emit(eng, lambda e, f=fn, s=s: f(e).then_inc(s, 1))
            return cnt[sem]

        def marker(eng, sem):
            cnt[sem] += 1
            s = sems[sem]
            emit(eng, lambda e, s=s: e.drain().then_inc(s, 1))
            return cnt[sem]

        ENG_SEM = {"tensor": "s_mm", "vector": "s_ve", "scalar": "s_se"}

        def barrier():
            for e in ("tensor", "vector", "scalar"):
                marker(e, ENG_SEM[e])
            tgt = {s: cnt[s] for s in ("s_mm", "s_ve", "s_se")}
            for e in ("tensor", "vector", "scalar", "gpsimd"):
                for s in ("s_mm", "s_ve", "s_se"):
                    wait(e, s, tgt[s])
            return tgt

        for dst, src in [(s_consts, consts), (s_ident, ident),
                         (s_iota, iota16), (s_iotamb, iotamb),
                         (s_p16, p16), (s_ones_r, ones_r),
                         (s_convw, conv_w), (s_bdt, b_dt_c),
                         (s_lng, ln_g_c), (s_lnb, ln_b_c)]:
            dma("sync", dst[:], src, "s_x")
        consts_ld = snap("s_x")
        for k in range(KC_D):
            dma("sync", w_in_h[:, k * DI:(k + 1) * DI],
                w_in[k * 128:(k + 1) * 128, 0:DI], "s_w")
            dma("sync", w_gate[:, k * NE:(k + 1) * NE],
                gate_wT[k * 128:(k + 1) * 128, :], "s_w")
        for k in range(KC_I):
            dma("sync", w_xp[:, k * 97:(k + 1) * 97],
                wxp[k * 128:(k + 1) * 128, :], "s_w")
        w_p2_ld = snap("s_w")
        mark("pre")

        ONE = s_consts[:, 0:1]
        EPS_AP = s_consts[0:1, 1:2]

        wait("tensor", "s_x", consts_ld)
        wait("vector", "s_x", consts_ld)
        wait("scalar", "s_x", consts_ld)
        wait("tensor", "s_w", w_p2_ld)
        xt_done = [0, 0]
        xt_sp = [0, 0]
        sq_done = [0, 0]
        cp_dma_prev = 0
        widths = [410, 410, 410, 410, 408]
        p2_tiles = []
        for h in range(2):
            t = h * TO
            for wdt in widths:
                p2_tiles.append((t, wdt))
                t += wdt
        for ti, (t0, W) in enumerate(p2_tiles):
            own = t0 >= TO
            lo = max(t0 - 1, 0)
            hi = t0 + W + 1
            wlo = t0 - lo
            off = 1 - wlo
            wwin = hi - lo
            buf = xt[ti % 2]
            if ti >= 2:
                wait("sync", "s_ve", xt_done[ti % 2])
                wait("sync", "s_sp", xt_sp[ti % 2])
            for k in range(KC_D):
                dma("sync", buf[:, k * WPS:k * WPS + wwin],
                    x_fm[k * 128:(k + 1) * 128, lo:hi], "s_x")
            ld = snap("s_x")
            wait("tensor", "s_x", ld)
            wait("scalar", "s_x", ld)
            for k in range(KC_D):
                op("tensor", lambda e, k=k, buf=buf, wwin=wwin:
                   e.matmul(pb[0][0:1, 0:wwin], ONE,
                            buf[:, k * WPS:k * WPS + wwin],
                            start=(k == 0), stop=(k == KC_D - 1)))
            for k in range(KC_D):
                sqb = cacc[k % 2]
                if k >= 2:
                    wait("scalar", "s_mm", sq_done[k % 2])
                se_id = op("scalar",
                           lambda e, k=k, buf=buf, wwin=wwin, sqb=sqb:
                           e.activation(sqb[:, 0:wwin],
                                        buf[:, k * WPS:k * WPS + wwin],
                                        AF.Square), "s_se")
                wait("tensor", "s_se", se_id)
                sq_done[k % 2] = op(
                    "tensor", lambda e, k=k, wwin=wwin, sqb=sqb:
                    e.matmul(pb[1][0:1, 0:wwin], ONE, sqb[:, 0:wwin],
                             start=(k == 0), stop=(k == KC_D - 1)), "s_mm")
            wait("vector", "s_mm", snap("s_mm"))
            op("vector", lambda e, wwin=wwin:
               e.tensor_scalar(r_mu[:, 0:wwin], pb[0][0:1, 0:wwin],
                               1.0 / D, None, ALU.mult))
            op("vector", lambda e, wwin=wwin:
               e.tensor_scalar(r_var[:, 0:wwin], pb[1][0:1, 0:wwin],
                               1.0 / D, None, ALU.mult))
            op("vector", lambda e, wwin=wwin: e.tensor_tensor(
                r_inv[:, 0:wwin], r_mu[:, 0:wwin], r_mu[:, 0:wwin],
                ALU.mult))
            ve_id = op("vector", lambda e, wwin=wwin: e.tensor_tensor(
                r_var[:, 0:wwin], r_var[:, 0:wwin], r_inv[:, 0:wwin],
                ALU.subtract), "s_ve")
            wait("scalar", "s_ve", ve_id)
            emit("scalar", lambda e: _load_act_set(nc, e, SET_LNEXP))
            op("scalar", lambda e, wwin=wwin: e.activation(
                r_inv[:, 0:wwin], r_var[:, 0:wwin], AF.Ln, bias=EPS_AP))
            se_id = op("scalar", lambda e, wwin=wwin: e.activation(
                r_inv[:, 0:wwin], r_inv[:, 0:wwin], AF.Exp, scale=-0.5),
                "s_se")
            emit("scalar", lambda e: _load_act_set(nc, e, SET_SILU))
            wait("tensor", "s_se", se_id)
            op("tensor", lambda e, wwin=wwin:
               e.matmul(pb[4][:, 0:wwin], s_ones_r[0:1, :],
                        r_mu[:, 0:wwin], start=True, stop=True))
            op("tensor", lambda e, wwin=wwin:
               e.matmul(pb[5][:, 0:wwin], s_ones_r[0:1, :],
                        r_inv[:, 0:wwin], start=True, stop=True), "s_mm")
            wait("vector", "s_mm", snap("s_mm"))
            for k in range(KC_D):
                op("vector", lambda e, k=k, buf=buf, wwin=wwin:
                   e.tensor_tensor(buf[:, k * WPS:k * WPS + wwin],
                                   buf[:, k * WPS:k * WPS + wwin],
                                   pb[4][:, 0:wwin], ALU.subtract))
                op("vector", lambda e, k=k, buf=buf, wwin=wwin:
                   e.tensor_tensor(buf[:, k * WPS:k * WPS + wwin],
                                   buf[:, k * WPS:k * WPS + wwin],
                                   pb[5][:, 0:wwin], ALU.mult),
                   "s_ve" if k == KC_D - 1 else None)
            xn_ready = cnt["s_ve"]
            if own:
                wait("gpsimd", "s_ve", xn_ready)
                for k in range(KC_D):
                    dma("gpsimd",
                        xn_sp[k * 128:(k + 1) * 128, t0 - TO:t0 - TO + W],
                        buf[:, k * WPS + wlo:k * WPS + wlo + W], "s_sp")
            wait("tensor", "s_ve", xn_ready)
            xz_evac = [0, 0, 0, 0]
            silu_mark = [0, 0]
            xp_mark = [0, 0]

            def emit_xz(m):
                bank = pb[m % 4]
                if m >= 4:
                    wait("tensor", "s_ve", xz_evac[m % 4])
                for k in range(KC_D):
                    op("tensor",
                       lambda e, m=m, k=k, bank=bank, wwin=wwin, buf=buf:
                       e.matmul(bank[:, 0:wwin],
                                w_in_h[:, k * DI + m * 128:k * DI + (m + 1) * 128],
                                buf[:, k * WPS:k * WPS + wwin],
                                start=(k == 0), stop=(k == KC_D - 1)),
                       "s_mm" if k == KC_D - 1 else None)
                wait("vector", "s_mm", snap("s_mm"))
                if ti == 0:
                    op("vector", lambda e: e.tensor_scalar(
                        xzb[:, 0:1], s_consts[:, 0:1], 0.0, None, ALU.mult))
                xz_evac[m % 4] = op(
                    "vector", lambda e, off=off, wwin=wwin, bank=bank:
                    e.tensor_copy(xzb[:, off:off + wwin], bank[:, 0:wwin]),
                    "s_ve")
                cb = cacc[m % 2]
                xb2 = xsb[m % 2]
                if m >= 2:
                    wait("vector", "s_se", silu_mark[m % 2])
                op("vector", lambda e, m=m, cb=cb, W=W: e.tensor_scalar(
                    cb[:, 0:W], xzb[:, 0:W],
                    s_convw[:, m * 3:m * 3 + 1], None, ALU.mult))
                op("vector", lambda e, m=m, cb=cb, W=W: e.scalar_tensor_tensor(
                    cb[:, 0:W], xzb[:, 1:1 + W],
                    s_convw[:, m * 3 + 1:m * 3 + 2], cb[:, 0:W],
                    ALU.mult, ALU.add))
                ve_id = op("vector", lambda e, m=m, cb=cb, W=W:
                           e.scalar_tensor_tensor(
                               cb[:, 0:W], xzb[:, 2:2 + W],
                               s_convw[:, m * 3 + 2:m * 3 + 3], cb[:, 0:W],
                               ALU.mult, ALU.add), "s_ve")
                wait("scalar", "s_ve", ve_id)
                if m >= 2:
                    wait("scalar", "s_mm", xp_mark[m % 2])
                silu_mark[m % 2] = op(
                    "scalar", lambda e, m=m, cb=cb, xb2=xb2, W=W:
                    e.activation(xb2[:, 0:W], cb[:, 0:W], AF.Silu), "s_se")

            def emit_xproj(m):
                xb2 = xsb[m % 2]
                wait("tensor", "s_se", silu_mark[m % 2])
                xp_mark[m % 2] = op(
                    "tensor", lambda e, m=m, xb2=xb2, W=W: e.matmul(
                        pb[6][0:97, 0:W], w_xp[:, m * 97:(m + 1) * 97],
                        xb2[:, 0:W], start=(m == 0), stop=(m == KC_I - 1)),
                    "s_mm")

            for m in range(KC_I):
                emit_xz(m)
                if m >= 1:
                    emit_xproj(m - 1)
            emit_xproj(KC_I - 1)
            wait("vector", "s_mm", snap("s_mm"))
            if own:
                wait("vector", "s_sp", cp_dma_prev)
            hlf = 64 * (t0 // TO)
            tc = t0 % TO
            op("vector", lambda e, W=W: e.tensor_copy(
                bpcp[0:32, 0:W], pb[6][0:32, 0:W]))
            op("vector", lambda e, hlf=hlf, tc=tc, W=W: e.tensor_copy(
                dt_lo[hlf:hlf + 32, tc:tc + W], pb[6][32:64, 0:W]))
            op("vector", lambda e, hlf=hlf, tc=tc, W=W: e.tensor_copy(
                dt_lo[hlf + 32:hlf + 64, tc:tc + W], pb[6][64:96, 0:W]))
            ve_id = op("vector", lambda e, W=W: e.tensor_copy(
                r_xm[:, 0:W], pb[6][96:97, 0:W]), "s_ve")
            wait("tensor", "s_ve", ve_id)
            mm_id = op("tensor", lambda e, W=W: e.matmul(
                pb[7][0:1, 0:W], s_consts[0:16, 0:1], bpcp[0:16, 0:W],
                start=True, stop=True), "s_mm")
            wait("vector", "s_mm", mm_id)
            op("vector", lambda e, W=W: e.tensor_copy(
                r_bps[:, 0:W], pb[7][0:1, 0:W]))
            op("vector", lambda e, t0=t0, W=W: e.tensor_tensor(
                r_c[:, t0:t0 + W], r_xm[:, 0:W], r_bps[:, 0:W],
                ALU.mult), "s_ve")
            if own:
                wait("gpsimd", "s_ve", snap("s_ve"))
                cp_dma_prev = dma("gpsimd", cp_buf[:, tc:tc + W],
                                  bpcp[16:32, 0:W], "s_sp")
                for k in range(KC_D):
                    op("tensor", lambda e, k=k, wlo=wlo, W=W, buf=buf:
                       e.matmul(
                           pb[4][0:16, 0:W],
                           w_gate[:, k * NE:(k + 1) * NE],
                           buf[:, k * WPS + wlo:k * WPS + wlo + W],
                           start=(k == 0), stop=(k == KC_D - 1)),
                       "s_mm" if k == KC_D - 1 else None)
                wait("vector", "s_mm", snap("s_mm"))
                op("vector", lambda e, tc=tc, W=W: e.tensor_copy(
                    lg_fm[:, tc:tc + W], pb[4][0:16, 0:W]), "s_ve")
            bt = barrier()
            xt_done[ti % 2] = bt["s_ve"]
            xt_sp[ti % 2] = snap("s_sp")
            wait("vector", "s_sp", snap("s_sp"))
            wait("scalar", "s_sp", snap("s_sp"))
            mark(f"t{ti}")
        p2_end = barrier()
        sp_after_p2 = snap("s_sp")
        mark("p2")
        emit("scalar", lambda e: _load_act_set(nc, e, SET_LNEXP))
        wait("vector", "s_sp", sp_after_p2)
        wait("sync", "s_mm", p2_end["s_mm"])
        wait("sync", "s_sp", sp_after_p2)
        dma("sync", w_dt_s[0:64, :], w_dt, "s_w")
        dma("sync", w_dt_s[64:128, :], w_dt, "s_w")
        w_dt_ld = snap("s_w")
        wait("tensor", "s_w", w_dt_ld)
        for ti in range(NTA):
            t0 = ti * WA
            hlf = 64 * (t0 // TO)
            tc = t0 % TO
            dt_sum = [0, 0]
            for m in range(KC_I):
                bank = pb[m % 2]
                scr = dtscr[m % 2]
                if m >= 2:
                    wait("tensor", "s_mm", dt_sum[m % 2])
                    wait("scalar", "s_mm", dt_sum[m % 2])
                op("tensor", lambda e, m=m, hlf=hlf, tc=tc, bank=bank:
                   e.matmul(bank[:, 0:WA],
                            w_dt_s[hlf:hlf + 64, m * 128:(m + 1) * 128],
                            dt_lo[hlf:hlf + 64, tc:tc + WA],
                            start=True, stop=True), "s_mm")
                wait("scalar", "s_mm", snap("s_mm"))
                op("scalar", lambda e, m=m, bank=bank, scr=scr: e.activation(
                    scr[:], bank[:, 0:WA], AF.Exp, bias=s_bdt[:, m:m + 1]))
                se_id = op("scalar", lambda e, scr=scr: e.activation(
                    scr[:], scr[:], AF.Ln, bias=1.0), "s_se")
                wait("tensor", "s_se", se_id)
                dt_sum[m % 2] = op("tensor", lambda e, m=m, scr=scr: e.matmul(
                    pb[4][0:1, 0:WA], ONE, scr[:],
                    start=(m == 0), stop=(m == KC_I - 1)), "s_mm")
            wait("vector", "s_mm", snap("s_mm"))
            ve_id = op("vector", lambda e: e.tensor_scalar(
                r_dtm[:, 0:WA], pb[4][0:1, 0:WA],
                1.0 / DI, None, ALU.mult), "s_ve")
            wait("scalar", "s_ve", ve_id)
            se_id = op("scalar", lambda e, t0=t0: e.activation(
                r_dec[:, t0:t0 + WA], r_dtm[:, 0:WA], AF.Exp, scale=-1.0),
                "s_se")
            wait("tensor", "s_se", se_id)
        wait("vector", "s_se", snap("s_se"))
        op("vector", lambda e: e.tensor_tensor_scan(
            r_s[:], r_dec[:], r_c[:], 0.0, ALU.mult, ALU.add), "s_ve")
        for ti in range(TO // WA):
            t0 = ti * WA
            wait("tensor", "s_ve", snap("s_ve"))
            op("tensor", lambda e, t0=t0: e.matmul(
                pb[0][0:16, 0:WA], s_ones_r[0:1, 0:16],
                r_s[:, TO + t0:TO + t0 + WA], start=True, stop=True), "s_mm")
            wait("vector", "s_mm", snap("s_mm"))
            op("vector", lambda e, t0=t0: e.tensor_tensor(
                y16[:], cp_buf[:, t0:t0 + WA], pb[0][0:16, 0:WA], ALU.mult),
                "s_ve")
            wait("scalar", "s_ve", snap("s_ve"))
            se_id = op("scalar", lambda e: e.activation(
                y16b[:], y16[:], AF.Square), "s_se")
            wait("tensor", "s_se", se_id)
            op("tensor", lambda e: e.matmul(
                pb[1][0:1, 0:WA], s_consts[0:16, 0:1], y16[:],
                start=True, stop=True))
            op("tensor", lambda e: e.matmul(
                pb[2][0:1, 0:WA], s_consts[0:16, 0:1], y16b[:],
                start=True, stop=True), "s_mm")
            wait("vector", "s_mm", snap("s_mm"))
            op("vector", lambda e, t0=t0: e.tensor_scalar(
                r_muy[:, t0:t0 + WA], pb[1][0:1, 0:WA], 1.0 / 16, None,
                ALU.mult))
            op("vector", lambda e, t0=t0: e.tensor_scalar(
                r_invy[:, t0:t0 + WA], pb[2][0:1, 0:WA], 1.0 / 16, None,
                ALU.mult))
            op("vector", lambda e, t0=t0: e.tensor_tensor(
                r_tmp[:, t0:t0 + WA], r_muy[:, t0:t0 + WA],
                r_muy[:, t0:t0 + WA], ALU.mult))
            ve_id = op("vector", lambda e, t0=t0: e.tensor_tensor(
                r_invy[:, t0:t0 + WA], r_invy[:, t0:t0 + WA],
                r_tmp[:, t0:t0 + WA], ALU.subtract), "s_ve")
            wait("scalar", "s_ve", ve_id)
            op("scalar", lambda e, t0=t0: e.activation(
                r_invy[:, t0:t0 + WA], r_invy[:, t0:t0 + WA],
                AF.Ln, bias=EPS_AP))
            se_id = op("scalar", lambda e, t0=t0: e.activation(
                r_invy[:, t0:t0 + WA], r_invy[:, t0:t0 + WA],
                AF.Exp, scale=-0.5), "s_se")
            wait("tensor", "s_se", se_id)
            op("tensor", lambda e, t0=t0: e.matmul(
                pb[1][0:16, 0:WA], s_ones_r[0:1, 0:16],
                r_muy[:, t0:t0 + WA], start=True, stop=True))
            op("tensor", lambda e, t0=t0: e.matmul(
                pb[2][0:16, 0:WA], s_ones_r[0:1, 0:16],
                r_invy[:, t0:t0 + WA], start=True, stop=True), "s_mm")
            wait("vector", "s_mm", snap("s_mm"))
            op("vector", lambda e: e.tensor_tensor(
                y16[:], y16[:], pb[1][0:16, 0:WA], ALU.subtract))
            ve_id = op("vector", lambda e: e.tensor_tensor(
                y16[:], y16[:], pb[2][0:16, 0:WA], ALU.mult), "s_ve")
            wait("tensor", "s_ve", ve_id)
            op("tensor", lambda e: e.matmul(
                pb[3][:, 0:WA], s_p16[:], y16[:], start=True, stop=True),
                "s_mm")
            wait("vector", "s_mm", snap("s_mm"))
            op("vector", lambda e, t0=t0: e.tensor_copy(
                yrep[:, t0:t0 + WA], pb[3][:, 0:WA]), "s_ve")
            wait("tensor", "s_ve", snap("s_ve"))
        def vsync():
            v = marker("vector", "s_ve")
            wait("vector", "s_ve", v)

        def opv(fn):
            op("vector", fn)
            vsync()

        for g in range(TO // 128):
            c0 = g * 128
            op("tensor", lambda e, c0=c0: e.transpose(
                pb[0][:, 0:16], lg_fm[:, c0:c0 + 128],
                s_ident[0:16, 0:16]), "s_mm")
            wait("vector", "s_mm", snap("s_mm"))
            opv(lambda e: e.tensor_copy(tkl[:], pb[0][:, 0:16]))
            opv(lambda e: e.tensor_reduce(
                tkm[:, 0:1], tkl[:], AX.X, ALU.max))
            opv(lambda e: e.tensor_scalar(
                tkl[:], tkl[:], tkm[:, 0:1], None, ALU.subtract))
            opv(lambda e: e.tensor_copy(tkcur[:], tkl[:]))
            for r in range(4):
                opv(lambda e: e.tensor_reduce(
                    tkm[:, 1:2], tkcur[:], AX.X, ALU.max))
                opv(lambda e: e.tensor_scalar(
                    tkeq[:], tkcur[:], tkm[:, 1:2], None, ALU.is_ge))
                opv(lambda e: e.tensor_tensor(
                    tkcand[:], tkeq[:], s_iotamb[:], ALU.mult))
                opv(lambda e: e.tensor_scalar(
                    tkcand[:], tkcand[:], BIG * 10, None, ALU.add))
                opv(lambda e: e.tensor_reduce(
                    tkm[:, 2:3], tkcand[:], AX.X, ALU.min))
                opv(lambda e: e.tensor_scalar(
                    tkeq[:], s_iota[:], tkm[:, 2:3], None, ALU.is_equal))
                if r == 0:
                    opv(lambda e: e.tensor_copy(tksel[:], tkeq[:]))
                else:
                    opv(lambda e: e.tensor_tensor(
                        tksel[:], tksel[:], tkeq[:], ALU.add))
                opv(lambda e: e.scalar_tensor_tensor(
                    tkcur[:], tkeq[:], -BIG, tkcur[:], ALU.mult, ALU.add))
            ve_id = snap("s_ve")
            wait("scalar", "s_ve", ve_id)
            se_id = op("scalar", lambda e: e.activation(
                tkw[:], tkl[:], AF.Exp), "s_se")
            wait("vector", "s_se", se_id)
            opv(lambda e: e.tensor_tensor(
                tkw[:], tkw[:], tksel[:], ALU.mult))
            opv(lambda e: e.tensor_reduce(
                tkm[:, 3:4], tkw[:], AX.X, ALU.add))
            opv(lambda e: e.reciprocal(tkm[:, 0:1], tkm[:, 3:4]))
            ve_id = op("vector", lambda e: e.tensor_scalar(
                tkw[:], tkw[:], tkm[:, 0:1], None, ALU.mult), "s_ve")
            wait("tensor", "s_ve", ve_id)
            op("tensor", lambda e: e.transpose(
                pb[1][0:16, 0:128], tkw[:], s_ident[:]), "s_mm")
            wait("vector", "s_mm", snap("s_mm"))
            op("vector", lambda e, c0=c0: e.tensor_copy(
                comb_fm[:, c0:c0 + 128], pb[1][0:16, 0:128]), "s_ve")
            wait("tensor", "s_ve", snap("s_ve"))
        aii_end = barrier()
        if dbg:
            for s in ("s_mm", "s_ve", "s_se"):
                wait("gpsimd", s, aii_end[s])
            dma("gpsimd", dbg_t[0:1, 0:TA], r_dec[:], "s_sp")
            dma("gpsimd", dbg_t[1:2, 0:TA], r_s[:], "s_sp")
            dma("gpsimd", dbg_t[16:32, 0:TO], cp_buf[:], "s_sp")
            dma("gpsimd", dbg_t[32:48, 0:TO], comb_fm[:], "s_sp")
            dma("gpsimd", dbg_t[48:64, 0:TO], lg_fm[:], "s_sp")
            dma("gpsimd", dbg_t[0:128, TO * 2:TO * 3], dt_lo[:], "s_sp")
            dma("gpsimd", dbg_t[0:128, TO * 3:TO * 4], yrep[:], "s_sp")
            dma("gpsimd", dbg_t[0:128, 100:116], tkl[:], "s_sp")
            dma("gpsimd", dbg_t[0:128, 116:132], tkw[:], "s_sp")
            dma("gpsimd", dbg_t[0:128, 132:148], tksel[:], "s_sp")
            dma("gpsimd", dbg_t[0:128, 148:152], tkm[:, 0:4], "s_sp")
            wait("sync", "s_sp", snap("s_sp"))
        mark("aii")

        emit("scalar", lambda e: _load_act_set(nc, e, SET_SILU))
        wait("sync", "s_mm", aii_end["s_mm"])
        wait("sync", "s_ve", aii_end["s_ve"])
        wait("sync", "s_sp", sp_after_p2)
        dma("sync", s_sel[:], sel16, "s_w")

        exps = [NE] + list(range(NE))
        w13_free = [0, 0]
        w2_free = 0
        last_act = 0
        prev_eo = None
        first = True
        for hf in range(2):
            h0 = hf * TH
            if hf > 0:
                wait("sync", "s_mm", snap("s_mm"))
            for k in range(KC_D):
                dma("sync", xn_h[:, k * TH:(k + 1) * TH],
                    xn_sp[k * 128:(k + 1) * 128, h0:h0 + TH], "s_x")
            xn_ld = snap("s_x")
            wait("tensor", "s_x", xn_ld)
            for ei, ex in enumerate(exps):
                wb = w13_s[ei % 2]
                if not first or ei >= 2:
                    wait("sync", "s_ve", w13_free[ei % 2])
                src = sw13 if ex == NE else w13[ex]
                for k in range(KC_D):
                    dma("sync", wb[:, k * 1024:(k + 1) * 1024],
                        src[k * 128:(k + 1) * 128, :], "s_w")
                wait("sync", "s_mm", w2_free)
                src2 = sw2t if ex == NE else w2t[ex]
                for k in range(4):
                    dma("sync", w2_s[:, k * D:(k + 1) * D],
                        src2[k * 128:(k + 1) * 128, :], "s_w")
                w_ld = snap("s_w")
                if ex != NE:
                    if prev_eo is not None:
                        wait("tensor", "s_ve", prev_eo[1])
                    for tt in range(2):
                        op("tensor", lambda e, ex=ex, h0=h0, tt=tt: e.matmul(
                            pb[4 + tt][:, 0:512],
                            s_sel[:, ex * 128:(ex + 1) * 128],
                            comb_fm[:, h0 + tt * 512:h0 + (tt + 1) * 512],
                            start=True, stop=True),
                            "s_mm" if tt == 1 else None)
                    wait("vector", "s_mm", snap("s_mm"))
                    for tt in range(2):
                        op("vector", lambda e, tt=tt: e.tensor_copy(
                            cbc_b[:, tt * 512:(tt + 1) * 512],
                            pb[4 + tt][:, 0:512]),
                            "s_ve" if tt == 1 else None)
                cbc_ready = cnt["s_ve"]
                wait("tensor", "s_w", w_ld)
                h_free = [0, 0]
                for hh in range(4):
                    for tt in range(2):
                        bank_a = pb[(hh * 2 + tt) % 2]
                        bank_b = pb[2 + (hh * 2 + tt) % 2]
                        if hh >= 1:
                            wait("tensor", "s_ve", h_free[tt])
                        for k in range(KC_D):
                            op("tensor",
                               lambda e, k=k, hh=hh, tt=tt, bank_a=bank_a,
                               wb=wb: e.matmul(
                                   bank_a[:, 0:512],
                                   wb[:, k * 1024 + hh * 128:
                                      k * 1024 + (hh + 1) * 128],
                                   xn_h[:, k * TH + tt * 512:
                                        k * TH + (tt + 1) * 512],
                                   start=(k == 0), stop=(k == KC_D - 1)))
                        for k in range(KC_D):
                            op("tensor",
                               lambda e, k=k, hh=hh, tt=tt, bank_b=bank_b,
                               wb=wb: e.matmul(
                                   bank_b[:, 0:512],
                                   wb[:, k * 1024 + 512 + hh * 128:
                                      k * 1024 + 512 + (hh + 1) * 128],
                                   xn_h[:, k * TH + tt * 512:
                                        k * TH + (tt + 1) * 512],
                                   start=(k == 0), stop=(k == KC_D - 1)),
                               "s_mm" if k == KC_D - 1 else None)
                        mm_done = cnt["s_mm"]
                        wait("scalar", "s_mm", mm_done)
                        if last_act:
                            wait("scalar", "s_ve", last_act)
                        se_id = op("scalar",
                                   lambda e, bank_a=bank_a: e.activation(
                                       silu_b[:], bank_a[:, 0:512], AF.Silu),
                                   "s_se")
                        wait("vector", "s_mm", mm_done)
                        wait("vector", "s_ve", cbc_ready)
                        if ex != NE:
                            op("vector", lambda e, tt=tt, bank_b=bank_b:
                               e.tensor_tensor(
                                   h3c_b[:], bank_b[:, 0:512],
                                   cbc_b[:, tt * 512:(tt + 1) * 512],
                                   ALU.mult))
                        else:
                            op("vector", lambda e, bank_b=bank_b:
                               e.tensor_copy(h3c_b[:], bank_b[:, 0:512]))
                        wait("vector", "s_se", se_id)
                        last_act = op(
                            "vector", lambda e, hh=hh, tt=tt: e.tensor_tensor(
                                act_b[:, hh * TH + tt * 512:
                                      hh * TH + (tt + 1) * 512],
                                silu_b[:], h3c_b[:], ALU.mult), "s_ve")
                        h_free[tt] = last_act
                act_ready = cnt["s_ve"]
                wait("tensor", "s_ve", act_ready)
                eo_free = [0, 0, 0, 0]
                for tg in range(8):
                    for dh in range(2):
                        bank = pb[4 + (tg * 2 + dh) % 4]
                        if tg * 2 + dh >= 4:
                            wait("tensor", "s_ve", eo_free[(tg * 2 + dh) % 4])
                        elif tg * 2 + dh < 2 and ex != NE:
                            wait("tensor", "s_ve", cbc_ready)
                        for k in range(4):
                            op("tensor",
                               lambda e, k=k, tg=tg, dh=dh, bank=bank:
                               e.matmul(
                                   bank[:, 0:512],
                                   act_b[:, k * TH + tg * 128:
                                         k * TH + (tg + 1) * 128],
                                   w2_s[:, k * D + dh * 512:
                                        k * D + (dh + 1) * 512],
                                   start=(k == 0), stop=(k == 3)),
                               "s_mm" if k == 3 else None)
                        wait("vector", "s_mm", snap("s_mm"))
                        if ei == 0:
                            eo_free[(tg * 2 + dh) % 4] = op(
                                "vector", lambda e, tg=tg, dh=dh, bank=bank:
                                e.tensor_copy(
                                    acc[:, tg * D + dh * 512:
                                        tg * D + (dh + 1) * 512],
                                    bank[:, 0:512]), "s_ve")
                        else:
                            eo_free[(tg * 2 + dh) % 4] = op(
                                "vector", lambda e, tg=tg, dh=dh, bank=bank:
                                e.tensor_tensor(
                                    acc[:, tg * D + dh * 512:
                                        tg * D + (dh + 1) * 512],
                                    acc[:, tg * D + dh * 512:
                                        tg * D + (dh + 1) * 512],
                                    bank[:, 0:512], ALU.add), "s_ve")
                w2_free = cnt["s_mm"]
                w13_free[ei % 2] = cnt["s_ve"]
                prev_eo = list(eo_free)
                first = False
            wait("gpsimd", "s_ve", snap("s_ve"))
            for tg in range(8):
                dma("gpsimd", ffn_sp[h0 + tg * 128:h0 + (tg + 1) * 128, :],
                    acc[:, tg * D:(tg + 1) * D], "s_sp")
            wait("vector", "s_sp", snap("s_sp"))
        p3_end = barrier()
        mark("p3")

        wait("sync", "s_mm", p3_end["s_mm"])
        wait("sync", "s_ve", p3_end["s_ve"])
        for k in range(KC_D):
            dma("sync", w_in_h4[:, k * DI:(k + 1) * DI],
                w_in[k * 128:(k + 1) * 128, DI:FZ], "s_w")
        for k in range(KC_I):
            dma("sync", w_out_s[:, k * D:(k + 1) * D],
                w_out[k * 128:(k + 1) * 128, :], "s_w")
        w_p4_ld = snap("s_w")
        wait("tensor", "s_w", w_p4_ld)
        xn_done = [0, 0]
        aux_done = [0, 0]
        for ti in range(TO // WP):
            t0 = ti * WP
            xb = xn_t[ti % 2]
            xtb = xtm_t[ti % 2]
            ftb = ffn_t[ti % 2]
            if ti >= 2:
                wait("sync", "s_mm", xn_done[ti % 2])
                wait("gpsimd", "s_ve", aux_done[ti % 2])
            for k in range(KC_D):
                dma("sync", xb[:, k * WP:(k + 1) * WP],
                    xn_sp[k * 128:(k + 1) * 128, t0:t0 + WP], "s_x")
            dma("gpsimd", xtb[:], x_tm[t0:t0 + 128, :], "s_sp")
            dma("gpsimd", ftb[:], ffn_sp[t0:t0 + 128, :], "s_sp")
            ld_x = snap("s_x")
            ld_aux = snap("s_sp")
            wait("tensor", "s_x", ld_x)
            z_free = [0, 0]
            ab_mark = [0, 0]
            wout_mark = [0, 0]

            def emit_z(m):
                bank = pb[m % 2]
                szm = szb[m % 2]
                if m >= 2:
                    wait("tensor", "s_se", z_free[m % 2])
                for k in range(KC_D):
                    op("tensor", lambda e, m=m, k=k, bank=bank, xb=xb:
                       e.matmul(
                           bank[:, 0:WP],
                           w_in_h4[:, k * DI + m * 128:k * DI + (m + 1) * 128],
                           xb[:, k * WP:(k + 1) * WP],
                           start=(k == 0), stop=(k == KC_D - 1)),
                       "s_mm" if k == KC_D - 1 else None)
                wait("scalar", "s_mm", snap("s_mm"))
                if m >= 2:
                    wait("scalar", "s_ve", ab_mark[m % 2])
                z_free[m % 2] = op(
                    "scalar", lambda e, bank=bank, szm=szm: e.activation(
                        szm[:], bank[:, 0:WP], AF.Silu), "s_se")
                wait("vector", "s_se", z_free[m % 2])
                if m >= 2:
                    wait("vector", "s_mm", wout_mark[m % 2])
                abm = ab[m % 2]
                op("vector", lambda e, m=m, t0=t0, abm=abm: e.tensor_scalar(
                    abm[:], yrep[:, t0:t0 + WP],
                    s_lng[:, m:m + 1], s_lnb[:, m:m + 1],
                    ALU.mult, ALU.add))
                ab_mark[m % 2] = op(
                    "vector", lambda e, abm=abm, szm=szm: e.tensor_tensor(
                        abm[:], abm[:], szm[:], ALU.mult), "s_ve")

            def emit_wout(m):
                abm = ab[m % 2]
                wait("tensor", "s_ve", ab_mark[m % 2])
                for tg in range(2):
                    for dh in range(2):
                        wout_mark[m % 2] = op(
                            "tensor", lambda e, m=m, tg=tg, dh=dh, abm=abm:
                            e.matmul(
                                pb[4 + tg * 2 + dh][:, 0:512],
                                abm[:, tg * 128:(tg + 1) * 128],
                                w_out_s[:, m * D + dh * 512:
                                        m * D + (dh + 1) * 512],
                                start=(m == 0), stop=(m == KC_I - 1)),
                            "s_mm")

            for m in range(KC_I):
                emit_z(m)
                if m >= 1:
                    emit_wout(m - 1)
            emit_wout(KC_I - 1)
            mm_fin = cnt["s_mm"]
            xn_done[ti % 2] = mm_fin
            wait("vector", "s_mm", mm_fin)
            wait("vector", "s_sp", ld_aux)
            for dh in range(2):
                op("vector", lambda e, dh=dh, xtb=xtb: e.tensor_tensor(
                    ob[:, dh * 512:(dh + 1) * 512],
                    pb[4 + dh][:, 0:512],
                    xtb[:, dh * 512:(dh + 1) * 512], ALU.add))
                op("vector", lambda e, dh=dh, ftb=ftb: e.tensor_tensor(
                    ob[:, dh * 512:(dh + 1) * 512],
                    ob[:, dh * 512:(dh + 1) * 512],
                    ftb[:, dh * 512:(dh + 1) * 512], ALU.add),
                    "s_ve" if dh == 1 else None)
            ve1 = cnt["s_ve"]
            wait("gpsimd", "s_ve", ve1)
            dma("gpsimd", out[t0:t0 + 128, :], ob[:, 0:D], "s_sp")
            dma("gpsimd", xtb[:], x_tm[t0 + 128:t0 + 256, :], "s_sp")
            dma("gpsimd", ftb[:], ffn_sp[t0 + 128:t0 + 256, :], "s_sp")
            ld_aux2 = snap("s_sp")
            wait("vector", "s_sp", ld_aux2)
            for dh in range(2):
                op("vector", lambda e, dh=dh, xtb=xtb: e.tensor_tensor(
                    ob[:, D + dh * 512:D + (dh + 1) * 512],
                    pb[6 + dh][:, 0:512],
                    xtb[:, dh * 512:(dh + 1) * 512], ALU.add))
                op("vector", lambda e, dh=dh, ftb=ftb: e.tensor_tensor(
                    ob[:, D + dh * 512:D + (dh + 1) * 512],
                    ob[:, D + dh * 512:D + (dh + 1) * 512],
                    ftb[:, dh * 512:(dh + 1) * 512], ALU.add),
                    "s_ve" if dh == 1 else None)
            ve2 = cnt["s_ve"]
            aux_done[ti % 2] = ve2
            wait("gpsimd", "s_ve", ve2)
            dma("gpsimd", out[t0 + 128:t0 + 256, :], ob[:, D:2 * D], "s_sp")
            wait("tensor", "s_ve", ve2)
        wait("gpsimd", "s_sp", snap("s_sp"))
        wait("sync", "s_sp", snap("s_sp"))

        cutname = {1: "p2", 2: "aii", 3: "p3"}.get(phases)
        envcut = os.environ.get("ATH_CUT")
        if envcut:
            cutname = envcut
        if cutname is not None:
            for e in plan:
                plan[e] = plan[e][:marks[cutname][e]]

        @block.sync
        def _(eng):
            for fn in plan["sync"]:
                fn(eng)

        @block.tensor
        def _(eng):
            for fn in plan["tensor"]:
                fn(eng)

        @block.vector
        def _(eng):
            for fn in plan["vector"]:
                fn(eng)

        @block.scalar
        def _(eng):
            for fn in plan["scalar"]:
                fn(eng)

        @block.gpsimd
        def _(eng):
            for fn in plan["gpsimd"]:
                fn(eng)

    return nc


def _off(nc, handle):
    return nc.lookup_mloc(handle).addr


def _host_inputs(inputs):
    x = np.ascontiguousarray(np.asarray(inputs["L0_attn_x"], np.float32))
    w_in = np.ascontiguousarray(np.asarray(inputs["L0_attn_W_in"], np.float32))
    w_conv = np.asarray(inputs["L0_attn_W_conv"], np.float32)
    w_xproj = np.asarray(inputs["L0_attn_W_xproj"], np.float32)
    w_dt = np.ascontiguousarray(np.asarray(inputs["L0_attn_W_dt"], np.float32))
    b_dt = np.asarray(inputs["L0_attn_b_dt"], np.float32)
    w_out = np.ascontiguousarray(np.asarray(inputs["L0_attn_W_out"], np.float32))
    ln_g = np.asarray(inputs["L0_attn_ln_g"], np.float32)
    ln_b = np.asarray(inputs["L0_attn_ln_b"], np.float32)
    gate_w = np.asarray(inputs["L0_ffn_gate_w"], np.float32)
    w1 = np.asarray(inputs["L0_ffn_w1"], np.float32)
    w3 = np.asarray(inputs["L0_ffn_w3"], np.float32)
    w2 = np.asarray(inputs["L0_ffn_w2"], np.float32)
    sw1 = np.asarray(inputs["L0_ffn_sw1"], np.float32)
    sw3 = np.asarray(inputs["L0_ffn_sw3"], np.float32)
    sw2 = np.asarray(inputs["L0_ffn_sw2"], np.float32)

    conv3 = w_conv.reshape(DI, 3)
    conv_c = np.ascontiguousarray(
        conv3.reshape(KC_I, 128, 3).transpose(1, 0, 2).reshape(128, KC_I * 3))
    wxp_h = np.ascontiguousarray(np.concatenate([
        w_xproj[:, 64:80], w_xproj[:, 80:96], w_xproj[:, 0:64],
        np.full((DI, 1), 1.0 / DI, np.float32)], axis=1))
    b_dt_c = np.ascontiguousarray(b_dt.reshape(KC_I, 128).T)
    ln_g_c = np.ascontiguousarray(ln_g.reshape(KC_I, 128).T)
    ln_b_c = np.ascontiguousarray(ln_b.reshape(KC_I, 128).T)
    gate_wT = np.ascontiguousarray(gate_w.T)
    w13 = np.ascontiguousarray(np.concatenate(
        [w1.transpose(0, 2, 1), w3.transpose(0, 2, 1)], axis=2))
    w2t = np.ascontiguousarray(w2.transpose(0, 2, 1))
    sw13 = np.ascontiguousarray(np.concatenate([sw1.T, sw3.T], axis=1))
    sw2t = np.ascontiguousarray(sw2.T)

    consts = np.zeros((128, 8), np.float32)
    consts[:, 0] = 1.0
    consts[:, 1] = EPS
    ident = np.eye(128, dtype=np.float32)
    iota16 = np.tile(np.arange(NE, dtype=np.float32), (128, 1))
    iotamb = np.ascontiguousarray(iota16 - BIG * 10)
    iota16 = np.ascontiguousarray(iota16)
    p16 = np.zeros((16, 128), np.float32)
    p16[np.arange(128) % 16, np.arange(128)] = 1.0
    sel16 = np.zeros((16, NE * 128), np.float32)
    for e in range(NE):
        sel16[e, e * 128:(e + 1) * 128] = 1.0
    ones_r = np.ones((1, 128), np.float32)

    shared = dict(w_in=w_in, conv_w=conv_c, wxp=wxp_h, w_dt=w_dt,
                  b_dt_c=b_dt_c, w_out=w_out, ln_g_c=ln_g_c, ln_b_c=ln_b_c,
                  gate_wT=gate_wT, w13=w13, w2t=w2t, sw13=sw13, sw2t=sw2t,
                  consts=consts, ident=ident, iota16=iota16, iotamb=iotamb,
                  p16=p16, sel16=sel16, ones_r=ones_r)

    in_maps = []
    for c in range(8):
        seq, half = c // 2, c % 2
        xs = x[seq]
        x_fmc = np.zeros((D, TAL), np.float32)
        x_fmc[:, 2048:4096] = xs[half * 2048:(half + 1) * 2048].T
        if half == 1:
            x_fmc[:, 0:2048] = xs[0:2048].T
        else:
            x_fmc[:, 4096] = xs[2048]
        m = dict(shared)
        m["x_fm"] = np.ascontiguousarray(x_fmc)
        m["x_tm"] = np.ascontiguousarray(xs[half * 2048:(half + 1) * 2048])
        in_maps.append(m)
    return in_maps


def kernel(**inputs):
    in_maps = _host_inputs(inputs)
    nc = build_program()
    res = run_bass_kernel_spmd(nc, in_maps, core_ids=list(range(8)))
    outa = np.zeros((4, 4096, D), np.float32)
    for c in range(8):
        seq, half = c // 2, c % 2
        outa[seq, half * 2048:(half + 1) * 2048, :] = res.results[c]["out"]
    return outa


# version 08
# speedup vs baseline: 1.0255x; 1.0255x over previous
import contextlib
import os
import numpy as np
import concourse.bass as bass
import concourse.mybir as mybir
from concourse.bass_utils import run_bass_kernel_spmd

F32 = mybir.dt.float32
AF = mybir.ActivationFunctionType
ALU = mybir.AluOpType
AX = mybir.AxisListType

D = 1024
DI = 2048
FZ = 4096
HID = 512
NE = 16
TO = 2048
TA = 4096
TAL = 4097
WP = 256
NT2 = TA // WP
WA = 512
NTA = TA // WA
TH = 1024
EPS = 1e-5
BIG = 1.0e5
SET_LNEXP = 6
SET_SILU = 18
KC_D = D // 128
KC_I = DI // 128
WPP = WP + 2


def _load_act_set(nc, eng, idx):
    eng.add_instruction(mybir.InstLoadActFuncSet(
        name=nc.get_next_instruction_name(), act_func_set_id=idx,
        ins=[], outs=[]))


def build_program():
    phases = int(os.environ.get("ATH_PHASES", "4"))
    skips = set(os.environ.get("ATH_SKIP", "").split(","))
    nc = bass.Bass("TRN2", target_bir_lowering=False,
                   detect_race_conditions=False)

    def din(name, shape):
        return nc.dram_tensor(name, shape, F32, kind="ExternalInput").ap()

    x_fm = din("x_fm", [D, TAL])
    x_tm = din("x_tm", [TO, D])
    w_in = din("w_in", [D, FZ])
    conv_w = din("conv_w", [128, KC_I * 3])
    wxp = din("wxp", [DI, 97])
    w_dt = din("w_dt", [64, DI])
    b_dt_c = din("b_dt_c", [128, KC_I])
    w_out = din("w_out", [DI, D])
    ln_g_c = din("ln_g_c", [128, KC_I])
    ln_b_c = din("ln_b_c", [128, KC_I])
    gate_wT = din("gate_wT", [D, NE])
    w13 = din("w13", [NE, D, 1024])
    w2t = din("w2t", [NE, HID, D])
    sw13 = din("sw13", [D, 1024])
    sw2t = din("sw2t", [HID, D])
    consts = din("consts", [128, 8])
    ident = din("ident", [128, 128])
    iota16 = din("iota16", [128, NE])
    iotamb = din("iotamb", [128, NE])
    p16 = din("p16", [16, 128])
    sel16 = din("sel16", [16, NE * 128])
    ones_r = din("ones_r", [1, 128])

    dbg = os.environ.get("ATH_DBGOUT", "0") == "1"
    ikind = "ExternalOutput" if dbg else "Internal"
    xn_sp = nc.dram_tensor("xn_sp", [D, TO], F32, kind=ikind).ap()
    ffn_sp = nc.dram_tensor("ffn_sp", [TO, D], F32, kind=ikind).ap()
    dbg_t = (nc.dram_tensor("dbg_t", [128, 8192], F32, kind="ExternalOutput").ap()
             if dbg else None)
    out = nc.dram_tensor("out", [TO, D], F32, kind="ExternalOutput").ap()

    ctx = contextlib.ExitStack()
    with ctx:
        def sbuf(name, shape):
            return ctx.enter_context(nc.sbuf_tensor(name, shape, F32))

        s_consts = sbuf("s_consts", [128, 8])
        s_ident = sbuf("s_ident", [128, 128])
        s_iota = sbuf("s_iota", [128, NE])
        s_iotamb = sbuf("s_iotamb", [128, NE])
        s_p16 = sbuf("s_p16", [16, 128])
        s_ones_r = sbuf("s_ones_r", [1, 128])
        s_convw = sbuf("s_convw", [128, KC_I * 3])
        s_bdt = sbuf("s_bdt", [128, KC_I])
        s_lng = sbuf("s_lng", [128, KC_I])
        s_lnb = sbuf("s_lnb", [128, KC_I])

        SBTOP = nc.SBUF_PARTITION_SIZE_BYTES
        MBASE = 26624
        assert nc.sbuf_base <= MBASE, nc.sbuf_base
        cur = [MBASE]

        def at(name, shape, off=None):
            if off is None:
                off = cur[0]
                nbytes = int(np.prod(shape[1:])) * 4
                cur[0] = off + ((nbytes + 31) // 32) * 32
            h = nc.alloc_sbuf_tensor_at(name, shape, F32, offset=off)
            assert off + int(np.prod(shape[1:])) * 4 <= SBTOP, (name, off)
            return h

        yrep = at("yrep", [128, TO])
        comb_fm = at("comb_fm", [16, TO])
        y16 = at("y16", [16, WA])
        y16b = at("y16b", [16, WA])
        tkl = at("tkl", [128, NE])
        tkcur = at("tkcur", [128, NE])
        tkeq = at("tkeq", [128, NE])
        tkcand = at("tkcand", [128, NE])
        tksel = at("tksel", [128, NE])
        tkw = at("tkw", [128, NE])
        tkm = at("tkm", [128, 8])
        PERM_END = cur[0]
        r_dec = at("r_dec", [1, TA])
        r_c = at("r_c", [1, TA])
        dt_lo = at("dt_lo", [128, TO])
        cp_buf = at("cp_buf", [16, TO])
        lg_fm = at("lg_fm", [16, TO])
        MID_END = cur[0]
        r_s = r_c
        w_in_h = at("w_in_h", [128, KC_D * DI])
        w_xp = at("w_xp", [128, KC_I * 97])
        w_gate = at("w_gate", [128, KC_D * NE])
        WPS = 512
        xt = [at(f"xt{i}", [128, KC_D * WPS]) for i in range(2)]
        xzb = at("xzb", [128, WPS])
        xsb = [at(f"xsb{i}", [128, WPS]) for i in range(2)]
        cacc = [at(f"cacc{i}", [128, WPS]) for i in range(2)]
        bpcp = cacc[0]
        r_mu = at("r_mu", [1, WPS])
        r_var = at("r_var", [1, WPS])
        r_inv = at("r_inv", [1, WPS])
        r_xm = at("r_xm", [1, WPS])
        assert cur[0] <= SBTOP, cur[0]
        r_bps = at("r_bps", [1, WPS], off=_off(nc, xzb))
        w_dt_s = at("w_dt_s", [128, DI], off=_off(nc, xt[0]))
        r_dtm = at("r_dtm", [1, WA], off=_off(nc, xt[0]) + 8192)
        dtscr = [at(f"dtscr{i}", [128, WA], off=_off(nc, xt[1]) + i * 2048)
                 for i in range(2)]
        r_muy = at("r_muy", [1, TO], off=_off(nc, xt[1]) + 4096)
        r_invy = at("r_invy", [1, TO], off=_off(nc, w_in_h))
        r_tmp = at("r_tmp", [1, TO], off=_off(nc, w_in_h) + 8192)
        p3 = [PERM_END]

        def at3(name, shape):
            off = p3[0]
            nbytes = int(np.prod(shape[1:])) * 4
            p3[0] = off + ((nbytes + 31) // 32) * 32
            h = nc.alloc_sbuf_tensor_at(name, shape, F32, offset=off)
            assert off + nbytes <= SBTOP, (name, off)
            return h

        xn_h = at3("xn_h", [128, KC_D * TH])
        acc = at3("acc", [128, 8 * D])
        w13_s = [at3(f"w13_s{i}", [128, KC_D * 1024]) for i in range(2)]
        w2_s = at3("w2_s", [128, 4 * D])
        act_b = at3("act_b", [128, 4 * TH])
        silu_b = at3("silu_b", [128, 512])
        h3c_b = at3("h3c_b", [128, 512])
        cbc_b = at3("cbc_b", [128, TH])
        s_sel = at3("s_sel", [16, NE * 128])
        assert p3[0] <= SBTOP, p3[0]
        p4 = [PERM_END]

        def at4(name, shape):
            off = p4[0]
            nbytes = int(np.prod(shape[1:])) * 4
            p4[0] = off + ((nbytes + 31) // 32) * 32
            h = nc.alloc_sbuf_tensor_at(name, shape, F32, offset=off)
            assert off + nbytes <= SBTOP, (name, off)
            return h

        w_in_h4 = at4("w_in_h4", [128, KC_D * DI])
        w_out_s = at4("w_out_s", [128, KC_I * D])
        xn_t = [at4(f"xn_t{i}", [128, KC_D * WP]) for i in range(2)]
        xtm_t = [at4(f"xtm_t{i}", [128, D]) for i in range(2)]
        ffn_t = [at4(f"ffn_t{i}", [128, D]) for i in range(2)]
        szb = [at4(f"szb{i}", [128, WP]) for i in range(2)]
        ab = [at4(f"ab{i}", [128, WP]) for i in range(2)]
        ob = at4("ob", [128, 2 * D])
        assert p4[0] <= SBTOP, p4[0]

        pb = [ctx.enter_context(nc.psum_tensor(f"pb{i}", [128, 512], F32))
              for i in range(8)]

        sems = {}
        for sname in ["s_x", "s_w", "s_sp", "s_mm", "s_ve", "s_se"]:
            sems[sname] = ctx.enter_context(nc.semaphore(sname))
        cnt = {k: 0 for k in sems}

        block = ctx.enter_context(nc.Block())
        plan = {e: [] for e in ("sync", "tensor", "vector", "scalar", "gpsimd")}
        marks = {}

        def mark(name):
            marks[name] = {e: len(plan[e]) for e in plan}

        def emit(eng, fn):
            plan[eng].append(fn)

        def dma(q, out_ap, in_ap, sem, slow=False):
            cnt[sem] += 16
            s = sems[sem]
            if slow:
                emit(q, lambda e, o=out_ap, i=in_ap, s=s:
                     e.dma_start(out=o, in_=i, allow_slow_non_contiguous=True)
                     .then_inc(s, 16))
            else:
                emit(q, lambda e, o=out_ap, i=in_ap, s=s:
                     e.dma_start(out=o, in_=i).then_inc(s, 16))
            return cnt[sem]

        def wait(eng, sem, val=None):
            v = cnt[sem] if val is None else val
            s = sems[sem]
            emit(eng, lambda e, s=s, v=v: e.wait_ge(s, v))

        def snap(sem):
            return cnt[sem]

        def op(eng, fn, sem=None):
            if sem is None:
                emit(eng, fn)
                return None
            cnt[sem] += 1
            s = sems[sem]
            emit(eng, lambda e, f=fn, s=s: f(e).then_inc(s, 1))
            return cnt[sem]

        def marker(eng, sem):
            cnt[sem] += 1
            s = sems[sem]
            emit(eng, lambda e, s=s: e.drain().then_inc(s, 1))
            return cnt[sem]

        ENG_SEM = {"tensor": "s_mm", "vector": "s_ve", "scalar": "s_se"}

        def barrier():
            for e in ("tensor", "vector", "scalar"):
                marker(e, ENG_SEM[e])
            tgt = {s: cnt[s] for s in ("s_mm", "s_ve", "s_se")}
            for e in ("tensor", "vector", "scalar", "gpsimd"):
                for s in ("s_mm", "s_ve", "s_se"):
                    wait(e, s, tgt[s])
            return tgt

        for dst, src in [(s_consts, consts), (s_ident, ident),
                         (s_iota, iota16), (s_iotamb, iotamb),
                         (s_p16, p16), (s_ones_r, ones_r),
                         (s_convw, conv_w), (s_bdt, b_dt_c),
                         (s_lng, ln_g_c), (s_lnb, ln_b_c)]:
            dma("sync", dst[:], src, "s_x")
        consts_ld = snap("s_x")
        for k in range(KC_D):
            dma("sync", w_in_h[:, k * DI:(k + 1) * DI],
                w_in[k * 128:(k + 1) * 128, 0:DI], "s_w")
            dma("sync", w_gate[:, k * NE:(k + 1) * NE],
                gate_wT[k * 128:(k + 1) * 128, :], "s_w")
        for k in range(KC_I):
            dma("sync", w_xp[:, k * 97:(k + 1) * 97],
                wxp[k * 128:(k + 1) * 128, :], "s_w")
        w_p2_ld = snap("s_w")
        mark("pre")

        ONE = s_consts[:, 0:1]
        EPS_AP = s_consts[0:1, 1:2]

        wait("tensor", "s_x", consts_ld)
        wait("vector", "s_x", consts_ld)
        wait("scalar", "s_x", consts_ld)
        wait("tensor", "s_w", w_p2_ld)
        xt_done = [0, 0]
        xt_sp = [0, 0]
        sq_done = [0, 0]
        cp_dma_prev = 0
        widths = [410, 410, 410, 410, 408]
        p2_tiles = []
        for h in range(2):
            t = h * TO
            for wdt in widths:
                p2_tiles.append((t, wdt))
                t += wdt
        for ti, (t0, W) in enumerate(p2_tiles):
            own = t0 >= TO
            lo = max(t0 - 1, 0)
            hi = t0 + W + 1
            wlo = t0 - lo
            off = 1 - wlo
            wwin = hi - lo
            buf = xt[ti % 2]
            if ti >= 2:
                wait("sync", "s_ve", xt_done[ti % 2])
                wait("sync", "s_sp", xt_sp[ti % 2])
            for k in range(KC_D):
                dma("sync", buf[:, k * WPS:k * WPS + wwin],
                    x_fm[k * 128:(k + 1) * 128, lo:hi], "s_x")
            ld = snap("s_x")
            wait("tensor", "s_x", ld)
            wait("scalar", "s_x", ld)
            for k in range(KC_D):
                op("tensor", lambda e, k=k, buf=buf, wwin=wwin:
                   e.matmul(pb[0][0:1, 0:wwin], ONE,
                            buf[:, k * WPS:k * WPS + wwin],
                            start=(k == 0), stop=(k == KC_D - 1)))
            for k in range(KC_D):
                sqb = cacc[k % 2]
                if k >= 2:
                    wait("scalar", "s_mm", sq_done[k % 2])
                se_id = op("scalar",
                           lambda e, k=k, buf=buf, wwin=wwin, sqb=sqb:
                           e.activation(sqb[:, 0:wwin],
                                        buf[:, k * WPS:k * WPS + wwin],
                                        AF.Square), "s_se")
                wait("tensor", "s_se", se_id)
                sq_done[k % 2] = op(
                    "tensor", lambda e, k=k, wwin=wwin, sqb=sqb:
                    e.matmul(pb[1][0:1, 0:wwin], ONE, sqb[:, 0:wwin],
                             start=(k == 0), stop=(k == KC_D - 1)), "s_mm")
            wait("vector", "s_mm", snap("s_mm"))
            op("vector", lambda e, wwin=wwin:
               e.tensor_scalar(r_mu[:, 0:wwin], pb[0][0:1, 0:wwin],
                               1.0 / D, None, ALU.mult))
            op("vector", lambda e, wwin=wwin:
               e.tensor_scalar(r_var[:, 0:wwin], pb[1][0:1, 0:wwin],
                               1.0 / D, None, ALU.mult))
            op("vector", lambda e, wwin=wwin: e.tensor_tensor(
                r_inv[:, 0:wwin], r_mu[:, 0:wwin], r_mu[:, 0:wwin],
                ALU.mult))
            ve_id = op("vector", lambda e, wwin=wwin: e.tensor_tensor(
                r_var[:, 0:wwin], r_var[:, 0:wwin], r_inv[:, 0:wwin],
                ALU.subtract), "s_ve")
            wait("scalar", "s_ve", ve_id)
            emit("scalar", lambda e: _load_act_set(nc, e, SET_LNEXP))
            op("scalar", lambda e, wwin=wwin: e.activation(
                r_inv[:, 0:wwin], r_var[:, 0:wwin], AF.Ln, bias=EPS_AP))
            se_id = op("scalar", lambda e, wwin=wwin: e.activation(
                r_inv[:, 0:wwin], r_inv[:, 0:wwin], AF.Exp, scale=-0.5),
                "s_se")
            emit("scalar", lambda e: _load_act_set(nc, e, SET_SILU))
            wait("tensor", "s_se", se_id)
            op("tensor", lambda e, wwin=wwin:
               e.matmul(pb[4][:, 0:wwin], s_ones_r[0:1, :],
                        r_mu[:, 0:wwin], start=True, stop=True))
            op("tensor", lambda e, wwin=wwin:
               e.matmul(pb[5][:, 0:wwin], s_ones_r[0:1, :],
                        r_inv[:, 0:wwin], start=True, stop=True), "s_mm")
            wait("vector", "s_mm", snap("s_mm"))
            for k in range(KC_D):
                op("vector", lambda e, k=k, buf=buf, wwin=wwin:
                   e.tensor_tensor(buf[:, k * WPS:k * WPS + wwin],
                                   buf[:, k * WPS:k * WPS + wwin],
                                   pb[4][:, 0:wwin], ALU.subtract))
                op("vector", lambda e, k=k, buf=buf, wwin=wwin:
                   e.tensor_tensor(buf[:, k * WPS:k * WPS + wwin],
                                   buf[:, k * WPS:k * WPS + wwin],
                                   pb[5][:, 0:wwin], ALU.mult),
                   "s_ve" if k == KC_D - 1 else None)
            xn_ready = cnt["s_ve"]
            if own:
                wait("gpsimd", "s_ve", xn_ready)
                for k in range(KC_D):
                    dma("gpsimd",
                        xn_sp[k * 128:(k + 1) * 128, t0 - TO:t0 - TO + W],
                        buf[:, k * WPS + wlo:k * WPS + wlo + W], "s_sp")
            wait("tensor", "s_ve", xn_ready)
            xz_evac = [0, 0, 0, 0]
            silu_mark = [0, 0]
            xp_mark = [0, 0]

            def emit_xz(m):
                bank = pb[m % 4]
                if m >= 4:
                    wait("tensor", "s_ve", xz_evac[m % 4])
                for k in range(KC_D):
                    op("tensor",
                       lambda e, m=m, k=k, bank=bank, wwin=wwin, buf=buf:
                       e.matmul(bank[:, 0:wwin],
                                w_in_h[:, k * DI + m * 128:k * DI + (m + 1) * 128],
                                buf[:, k * WPS:k * WPS + wwin],
                                start=(k == 0), stop=(k == KC_D - 1)),
                       "s_mm" if k == KC_D - 1 else None)
                wait("vector", "s_mm", snap("s_mm"))
                if ti == 0:
                    op("vector", lambda e: e.tensor_scalar(
                        xzb[:, 0:1], s_consts[:, 0:1], 0.0, None, ALU.mult))
                xz_evac[m % 4] = op(
                    "vector", lambda e, off=off, wwin=wwin, bank=bank:
                    e.tensor_copy(xzb[:, off:off + wwin], bank[:, 0:wwin]),
                    "s_ve")
                cb = cacc[m % 2]
                xb2 = xsb[m % 2]
                if m >= 2:
                    wait("vector", "s_se", silu_mark[m % 2])
                op("vector", lambda e, m=m, cb=cb, W=W: e.tensor_scalar(
                    cb[:, 0:W], xzb[:, 0:W],
                    s_convw[:, m * 3:m * 3 + 1], None, ALU.mult))
                op("vector", lambda e, m=m, cb=cb, W=W: e.scalar_tensor_tensor(
                    cb[:, 0:W], xzb[:, 1:1 + W],
                    s_convw[:, m * 3 + 1:m * 3 + 2], cb[:, 0:W],
                    ALU.mult, ALU.add))
                ve_id = op("vector", lambda e, m=m, cb=cb, W=W:
                           e.scalar_tensor_tensor(
                               cb[:, 0:W], xzb[:, 2:2 + W],
                               s_convw[:, m * 3 + 2:m * 3 + 3], cb[:, 0:W],
                               ALU.mult, ALU.add), "s_ve")
                wait("scalar", "s_ve", ve_id)
                if m >= 2:
                    wait("scalar", "s_mm", xp_mark[m % 2])
                silu_mark[m % 2] = op(
                    "scalar", lambda e, m=m, cb=cb, xb2=xb2, W=W:
                    e.activation(xb2[:, 0:W], cb[:, 0:W], AF.Silu), "s_se")

            def emit_xproj(m):
                xb2 = xsb[m % 2]
                wait("tensor", "s_se", silu_mark[m % 2])
                xp_mark[m % 2] = op(
                    "tensor", lambda e, m=m, xb2=xb2, W=W: e.matmul(
                        pb[6][0:97, 0:W], w_xp[:, m * 97:(m + 1) * 97],
                        xb2[:, 0:W], start=(m == 0), stop=(m == KC_I - 1)),
                    "s_mm")

            for m in range(KC_I):
                emit_xz(m)
                if m >= 1:
                    emit_xproj(m - 1)
            emit_xproj(KC_I - 1)
            wait("vector", "s_mm", snap("s_mm"))
            if own:
                wait("vector", "s_sp", cp_dma_prev)
            hlf = 64 * (t0 // TO)
            tc = t0 % TO
            op("vector", lambda e, W=W: e.tensor_copy(
                bpcp[0:32, 0:W], pb[6][0:32, 0:W]))
            op("vector", lambda e, hlf=hlf, tc=tc, W=W: e.tensor_copy(
                dt_lo[hlf:hlf + 32, tc:tc + W], pb[6][32:64, 0:W]))
            op("vector", lambda e, hlf=hlf, tc=tc, W=W: e.tensor_copy(
                dt_lo[hlf + 32:hlf + 64, tc:tc + W], pb[6][64:96, 0:W]))
            ve_id = op("vector", lambda e, W=W: e.tensor_copy(
                r_xm[:, 0:W], pb[6][96:97, 0:W]), "s_ve")
            wait("tensor", "s_ve", ve_id)
            mm_id = op("tensor", lambda e, W=W: e.matmul(
                pb[7][0:1, 0:W], s_consts[0:16, 0:1], bpcp[0:16, 0:W],
                start=True, stop=True), "s_mm")
            wait("vector", "s_mm", mm_id)
            op("vector", lambda e, W=W: e.tensor_copy(
                r_bps[:, 0:W], pb[7][0:1, 0:W]))
            op("vector", lambda e, t0=t0, W=W: e.tensor_tensor(
                r_c[:, t0:t0 + W], r_xm[:, 0:W], r_bps[:, 0:W],
                ALU.mult), "s_ve")
            if own:
                wait("gpsimd", "s_ve", snap("s_ve"))
                cp_dma_prev = dma("gpsimd", cp_buf[:, tc:tc + W],
                                  bpcp[16:32, 0:W], "s_sp")
                for k in range(KC_D):
                    op("tensor", lambda e, k=k, wlo=wlo, W=W, buf=buf:
                       e.matmul(
                           pb[4][0:16, 0:W],
                           w_gate[:, k * NE:(k + 1) * NE],
                           buf[:, k * WPS + wlo:k * WPS + wlo + W],
                           start=(k == 0), stop=(k == KC_D - 1)),
                       "s_mm" if k == KC_D - 1 else None)
                wait("vector", "s_mm", snap("s_mm"))
                op("vector", lambda e, tc=tc, W=W: e.tensor_copy(
                    lg_fm[:, tc:tc + W], pb[4][0:16, 0:W]), "s_ve")
            bt = barrier()
            xt_done[ti % 2] = bt["s_ve"]
            xt_sp[ti % 2] = snap("s_sp")
            wait("vector", "s_sp", snap("s_sp"))
            wait("scalar", "s_sp", snap("s_sp"))
            mark(f"t{ti}")
        p2_end = barrier()
        sp_after_p2 = snap("s_sp")
        mark("p2")
        emit("scalar", lambda e: _load_act_set(nc, e, SET_LNEXP))
        wait("vector", "s_sp", sp_after_p2)
        wait("sync", "s_mm", p2_end["s_mm"])
        wait("sync", "s_sp", sp_after_p2)
        dma("sync", w_dt_s[0:64, :], w_dt, "s_w")
        dma("sync", w_dt_s[64:128, :], w_dt, "s_w")
        w_dt_ld = snap("s_w")
        wait("tensor", "s_w", w_dt_ld)
        for ti in range(NTA):
            t0 = ti * WA
            hlf = 64 * (t0 // TO)
            tc = t0 % TO
            dt_sum = [0, 0]
            se_done = [0, 0]

            def emit_dt(m):
                bank = pb[m % 2]
                scr = dtscr[m % 2]
                if m >= 2:
                    wait("tensor", "s_mm", dt_sum[m % 2])
                op("tensor", lambda e, m=m, hlf=hlf, tc=tc, bank=bank:
                   e.matmul(bank[:, 0:WA],
                            w_dt_s[hlf:hlf + 64, m * 128:(m + 1) * 128],
                            dt_lo[hlf:hlf + 64, tc:tc + WA],
                            start=True, stop=True), "s_mm")
                wait("scalar", "s_mm", snap("s_mm"))
                if m >= 2:
                    wait("scalar", "s_mm", dt_sum[m % 2])
                op("scalar", lambda e, m=m, bank=bank, scr=scr: e.activation(
                    scr[:], bank[:, 0:WA], AF.Exp, bias=s_bdt[:, m:m + 1]))
                op("scalar", lambda e, scr=scr: e.activation(
                    scr[:], scr[:], AF.Ln, bias=1.0))
                se_done[m % 2] = marker("scalar", "s_se")

            def emit_dtsum(m):
                scr = dtscr[m % 2]
                wait("tensor", "s_se", se_done[m % 2])
                dt_sum[m % 2] = op("tensor", lambda e, m=m, scr=scr: e.matmul(
                    pb[4][0:1, 0:WA], ONE, scr[:],
                    start=(m == 0), stop=(m == KC_I - 1)), "s_mm")

            for m in range(KC_I):
                emit_dt(m)
                if m >= 1:
                    emit_dtsum(m - 1)
            emit_dtsum(KC_I - 1)
            wait("vector", "s_mm", snap("s_mm"))
            ve_id = op("vector", lambda e: e.tensor_scalar(
                r_dtm[:, 0:WA], pb[4][0:1, 0:WA],
                1.0 / DI, None, ALU.mult), "s_ve")
            wait("scalar", "s_ve", ve_id)
            se_id = op("scalar", lambda e, t0=t0: e.activation(
                r_dec[:, t0:t0 + WA], r_dtm[:, 0:WA], AF.Exp, scale=-1.0),
                "s_se")
            wait("tensor", "s_se", se_id)
        wait("vector", "s_se", snap("s_se"))
        op("vector", lambda e: e.tensor_tensor_scan(
            r_s[:], r_dec[:], r_c[:], 0.0, ALU.mult, ALU.add), "s_ve")
        for ti in range(TO // WA):
            t0 = ti * WA
            wait("tensor", "s_ve", snap("s_ve"))
            op("tensor", lambda e, t0=t0: e.matmul(
                pb[0][0:16, 0:WA], s_ones_r[0:1, 0:16],
                r_s[:, TO + t0:TO + t0 + WA], start=True, stop=True), "s_mm")
            wait("vector", "s_mm", snap("s_mm"))
            op("vector", lambda e, t0=t0: e.tensor_tensor(
                y16[:], cp_buf[:, t0:t0 + WA], pb[0][0:16, 0:WA], ALU.mult),
                "s_ve")
            wait("scalar", "s_ve", snap("s_ve"))
            se_id = op("scalar", lambda e: e.activation(
                y16b[:], y16[:], AF.Square), "s_se")
            wait("tensor", "s_se", se_id)
            op("tensor", lambda e: e.matmul(
                pb[1][0:1, 0:WA], s_consts[0:16, 0:1], y16[:],
                start=True, stop=True))
            op("tensor", lambda e: e.matmul(
                pb[2][0:1, 0:WA], s_consts[0:16, 0:1], y16b[:],
                start=True, stop=True), "s_mm")
            wait("vector", "s_mm", snap("s_mm"))
            op("vector", lambda e, t0=t0: e.tensor_scalar(
                r_muy[:, t0:t0 + WA], pb[1][0:1, 0:WA], 1.0 / 16, None,
                ALU.mult))
            op("vector", lambda e, t0=t0: e.tensor_scalar(
                r_invy[:, t0:t0 + WA], pb[2][0:1, 0:WA], 1.0 / 16, None,
                ALU.mult))
            op("vector", lambda e, t0=t0: e.tensor_tensor(
                r_tmp[:, t0:t0 + WA], r_muy[:, t0:t0 + WA],
                r_muy[:, t0:t0 + WA], ALU.mult))
            ve_id = op("vector", lambda e, t0=t0: e.tensor_tensor(
                r_invy[:, t0:t0 + WA], r_invy[:, t0:t0 + WA],
                r_tmp[:, t0:t0 + WA], ALU.subtract), "s_ve")
            wait("scalar", "s_ve", ve_id)
            op("scalar", lambda e, t0=t0: e.activation(
                r_invy[:, t0:t0 + WA], r_invy[:, t0:t0 + WA],
                AF.Ln, bias=EPS_AP))
            se_id = op("scalar", lambda e, t0=t0: e.activation(
                r_invy[:, t0:t0 + WA], r_invy[:, t0:t0 + WA],
                AF.Exp, scale=-0.5), "s_se")
            wait("tensor", "s_se", se_id)
            op("tensor", lambda e, t0=t0: e.matmul(
                pb[1][0:16, 0:WA], s_ones_r[0:1, 0:16],
                r_muy[:, t0:t0 + WA], start=True, stop=True))
            op("tensor", lambda e, t0=t0: e.matmul(
                pb[2][0:16, 0:WA], s_ones_r[0:1, 0:16],
                r_invy[:, t0:t0 + WA], start=True, stop=True), "s_mm")
            wait("vector", "s_mm", snap("s_mm"))
            op("vector", lambda e: e.tensor_tensor(
                y16[:], y16[:], pb[1][0:16, 0:WA], ALU.subtract))
            ve_id = op("vector", lambda e: e.tensor_tensor(
                y16[:], y16[:], pb[2][0:16, 0:WA], ALU.mult), "s_ve")
            wait("tensor", "s_ve", ve_id)
            op("tensor", lambda e: e.matmul(
                pb[3][:, 0:WA], s_p16[:], y16[:], start=True, stop=True),
                "s_mm")
            wait("vector", "s_mm", snap("s_mm"))
            op("vector", lambda e, t0=t0: e.tensor_copy(
                yrep[:, t0:t0 + WA], pb[3][:, 0:WA]), "s_ve")
            wait("tensor", "s_ve", snap("s_ve"))
        def vsync():
            v = marker("vector", "s_ve")
            wait("vector", "s_ve", v)

        def opv(fn):
            op("vector", fn)
            vsync()

        for g in range(TO // 128):
            c0 = g * 128
            op("tensor", lambda e, c0=c0: e.transpose(
                pb[0][:, 0:16], lg_fm[:, c0:c0 + 128],
                s_ident[0:16, 0:16]), "s_mm")
            wait("vector", "s_mm", snap("s_mm"))
            opv(lambda e: e.tensor_copy(tkl[:], pb[0][:, 0:16]))
            opv(lambda e: e.tensor_reduce(
                tkm[:, 0:1], tkl[:], AX.X, ALU.max))
            opv(lambda e: e.tensor_scalar(
                tkl[:], tkl[:], tkm[:, 0:1], None, ALU.subtract))
            opv(lambda e: e.tensor_copy(tkcur[:], tkl[:]))
            for r in range(4):
                opv(lambda e: e.tensor_reduce(
                    tkm[:, 1:2], tkcur[:], AX.X, ALU.max))
                opv(lambda e: e.tensor_scalar(
                    tkeq[:], tkcur[:], tkm[:, 1:2], None, ALU.is_ge))
                opv(lambda e: e.tensor_tensor(
                    tkcand[:], tkeq[:], s_iotamb[:], ALU.mult))
                opv(lambda e: e.tensor_scalar(
                    tkcand[:], tkcand[:], BIG * 10, None, ALU.add))
                opv(lambda e: e.tensor_reduce(
                    tkm[:, 2:3], tkcand[:], AX.X, ALU.min))
                opv(lambda e: e.tensor_scalar(
                    tkeq[:], s_iota[:], tkm[:, 2:3], None, ALU.is_equal))
                if r == 0:
                    opv(lambda e: e.tensor_copy(tksel[:], tkeq[:]))
                else:
                    opv(lambda e: e.tensor_tensor(
                        tksel[:], tksel[:], tkeq[:], ALU.add))
                opv(lambda e: e.scalar_tensor_tensor(
                    tkcur[:], tkeq[:], -BIG, tkcur[:], ALU.mult, ALU.add))
            ve_id = snap("s_ve")
            wait("scalar", "s_ve", ve_id)
            se_id = op("scalar", lambda e: e.activation(
                tkw[:], tkl[:], AF.Exp), "s_se")
            wait("vector", "s_se", se_id)
            opv(lambda e: e.tensor_tensor(
                tkw[:], tkw[:], tksel[:], ALU.mult))
            opv(lambda e: e.tensor_reduce(
                tkm[:, 3:4], tkw[:], AX.X, ALU.add))
            opv(lambda e: e.reciprocal(tkm[:, 0:1], tkm[:, 3:4]))
            ve_id = op("vector", lambda e: e.tensor_scalar(
                tkw[:], tkw[:], tkm[:, 0:1], None, ALU.mult), "s_ve")
            wait("tensor", "s_ve", ve_id)
            op("tensor", lambda e: e.transpose(
                pb[1][0:16, 0:128], tkw[:], s_ident[:]), "s_mm")
            wait("vector", "s_mm", snap("s_mm"))
            op("vector", lambda e, c0=c0: e.tensor_copy(
                comb_fm[:, c0:c0 + 128], pb[1][0:16, 0:128]), "s_ve")
            wait("tensor", "s_ve", snap("s_ve"))
        aii_end = barrier()
        if dbg:
            for s in ("s_mm", "s_ve", "s_se"):
                wait("gpsimd", s, aii_end[s])
            dma("gpsimd", dbg_t[0:1, 0:TA], r_dec[:], "s_sp")
            dma("gpsimd", dbg_t[1:2, 0:TA], r_s[:], "s_sp")
            dma("gpsimd", dbg_t[16:32, 0:TO], cp_buf[:], "s_sp")
            dma("gpsimd", dbg_t[32:48, 0:TO], comb_fm[:], "s_sp")
            dma("gpsimd", dbg_t[48:64, 0:TO], lg_fm[:], "s_sp")
            dma("gpsimd", dbg_t[0:128, TO * 2:TO * 3], dt_lo[:], "s_sp")
            dma("gpsimd", dbg_t[0:128, TO * 3:TO * 4], yrep[:], "s_sp")
            dma("gpsimd", dbg_t[0:128, 100:116], tkl[:], "s_sp")
            dma("gpsimd", dbg_t[0:128, 116:132], tkw[:], "s_sp")
            dma("gpsimd", dbg_t[0:128, 132:148], tksel[:], "s_sp")
            dma("gpsimd", dbg_t[0:128, 148:152], tkm[:, 0:4], "s_sp")
            wait("sync", "s_sp", snap("s_sp"))
        mark("aii")

        emit("scalar", lambda e: _load_act_set(nc, e, SET_SILU))
        wait("sync", "s_mm", aii_end["s_mm"])
        wait("sync", "s_ve", aii_end["s_ve"])
        wait("sync", "s_sp", sp_after_p2)
        dma("sync", s_sel[:], sel16, "s_w")

        exps = [NE] + list(range(NE))
        w13_free = [0, 0]
        w2_free = 0
        last_act = 0
        prev_eo = None
        first = True
        for hf in range(2):
            h0 = hf * TH
            if hf > 0:
                wait("sync", "s_mm", snap("s_mm"))
            for k in range(KC_D):
                dma("sync", xn_h[:, k * TH:(k + 1) * TH],
                    xn_sp[k * 128:(k + 1) * 128, h0:h0 + TH], "s_x")
            xn_ld = snap("s_x")
            wait("tensor", "s_x", xn_ld)
            for ei, ex in enumerate(exps):
                wb = w13_s[ei % 2]
                if not first or ei >= 2:
                    wait("sync", "s_ve", w13_free[ei % 2])
                src = sw13 if ex == NE else w13[ex]
                for k in range(KC_D):
                    dma("sync", wb[:, k * 1024:(k + 1) * 1024],
                        src[k * 128:(k + 1) * 128, :], "s_w")
                wait("sync", "s_mm", w2_free)
                src2 = sw2t if ex == NE else w2t[ex]
                for k in range(4):
                    dma("sync", w2_s[:, k * D:(k + 1) * D],
                        src2[k * 128:(k + 1) * 128, :], "s_w")
                w_ld = snap("s_w")
                if ex != NE:
                    if prev_eo is not None:
                        wait("tensor", "s_ve", prev_eo[1])
                    for tt in range(2):
                        op("tensor", lambda e, ex=ex, h0=h0, tt=tt: e.matmul(
                            pb[4 + tt][:, 0:512],
                            s_sel[:, ex * 128:(ex + 1) * 128],
                            comb_fm[:, h0 + tt * 512:h0 + (tt + 1) * 512],
                            start=True, stop=True),
                            "s_mm" if tt == 1 else None)
                    wait("vector", "s_mm", snap("s_mm"))
                    for tt in range(2):
                        op("vector", lambda e, tt=tt: e.tensor_copy(
                            cbc_b[:, tt * 512:(tt + 1) * 512],
                            pb[4 + tt][:, 0:512]),
                            "s_ve" if tt == 1 else None)
                cbc_ready = cnt["s_ve"]
                wait("tensor", "s_w", w_ld)
                h_free = [0, 0]
                for hh in range(4):
                    for tt in range(2):
                        bank_a = pb[(hh * 2 + tt) % 2]
                        bank_b = pb[2 + (hh * 2 + tt) % 2]
                        if hh >= 1:
                            wait("tensor", "s_ve", h_free[tt])
                        for k in range(KC_D):
                            op("tensor",
                               lambda e, k=k, hh=hh, tt=tt, bank_a=bank_a,
                               wb=wb: e.matmul(
                                   bank_a[:, 0:512],
                                   wb[:, k * 1024 + hh * 128:
                                      k * 1024 + (hh + 1) * 128],
                                   xn_h[:, k * TH + tt * 512:
                                        k * TH + (tt + 1) * 512],
                                   start=(k == 0), stop=(k == KC_D - 1)))
                        for k in range(KC_D):
                            op("tensor",
                               lambda e, k=k, hh=hh, tt=tt, bank_b=bank_b,
                               wb=wb: e.matmul(
                                   bank_b[:, 0:512],
                                   wb[:, k * 1024 + 512 + hh * 128:
                                      k * 1024 + 512 + (hh + 1) * 128],
                                   xn_h[:, k * TH + tt * 512:
                                        k * TH + (tt + 1) * 512],
                                   start=(k == 0), stop=(k == KC_D - 1)),
                               "s_mm" if k == KC_D - 1 else None)
                        mm_done = cnt["s_mm"]
                        wait("scalar", "s_mm", mm_done)
                        if last_act:
                            wait("scalar", "s_ve", last_act)
                        se_id = op("scalar",
                                   lambda e, bank_a=bank_a: e.activation(
                                       silu_b[:], bank_a[:, 0:512], AF.Silu),
                                   "s_se")
                        wait("vector", "s_mm", mm_done)
                        wait("vector", "s_ve", cbc_ready)
                        if ex != NE:
                            op("vector", lambda e, tt=tt, bank_b=bank_b:
                               e.tensor_tensor(
                                   h3c_b[:], bank_b[:, 0:512],
                                   cbc_b[:, tt * 512:(tt + 1) * 512],
                                   ALU.mult))
                        else:
                            op("vector", lambda e, bank_b=bank_b:
                               e.tensor_copy(h3c_b[:], bank_b[:, 0:512]))
                        wait("vector", "s_se", se_id)
                        last_act = op(
                            "vector", lambda e, hh=hh, tt=tt: e.tensor_tensor(
                                act_b[:, hh * TH + tt * 512:
                                      hh * TH + (tt + 1) * 512],
                                silu_b[:], h3c_b[:], ALU.mult), "s_ve")
                        h_free[tt] = last_act
                act_ready = cnt["s_ve"]
                wait("tensor", "s_ve", act_ready)
                eo_free = [0, 0, 0, 0]
                for tg in range(8):
                    for dh in range(2):
                        bank = pb[4 + (tg * 2 + dh) % 4]
                        if tg * 2 + dh >= 4:
                            wait("tensor", "s_ve", eo_free[(tg * 2 + dh) % 4])
                        elif tg * 2 + dh < 2 and ex != NE:
                            wait("tensor", "s_ve", cbc_ready)
                        for k in range(4):
                            op("tensor",
                               lambda e, k=k, tg=tg, dh=dh, bank=bank:
                               e.matmul(
                                   bank[:, 0:512],
                                   act_b[:, k * TH + tg * 128:
                                         k * TH + (tg + 1) * 128],
                                   w2_s[:, k * D + dh * 512:
                                        k * D + (dh + 1) * 512],
                                   start=(k == 0), stop=(k == 3)),
                               "s_mm" if k == 3 else None)
                        wait("vector", "s_mm", snap("s_mm"))
                        if ei == 0:
                            eo_free[(tg * 2 + dh) % 4] = op(
                                "vector", lambda e, tg=tg, dh=dh, bank=bank:
                                e.tensor_copy(
                                    acc[:, tg * D + dh * 512:
                                        tg * D + (dh + 1) * 512],
                                    bank[:, 0:512]), "s_ve")
                        else:
                            eo_free[(tg * 2 + dh) % 4] = op(
                                "vector", lambda e, tg=tg, dh=dh, bank=bank:
                                e.tensor_tensor(
                                    acc[:, tg * D + dh * 512:
                                        tg * D + (dh + 1) * 512],
                                    acc[:, tg * D + dh * 512:
                                        tg * D + (dh + 1) * 512],
                                    bank[:, 0:512], ALU.add), "s_ve")
                w2_free = cnt["s_mm"]
                w13_free[ei % 2] = cnt["s_ve"]
                prev_eo = list(eo_free)
                first = False
            wait("gpsimd", "s_ve", snap("s_ve"))
            for tg in range(8):
                dma("gpsimd", ffn_sp[h0 + tg * 128:h0 + (tg + 1) * 128, :],
                    acc[:, tg * D:(tg + 1) * D], "s_sp")
            wait("vector", "s_sp", snap("s_sp"))
        p3_end = barrier()
        mark("p3")

        wait("sync", "s_mm", p3_end["s_mm"])
        wait("sync", "s_ve", p3_end["s_ve"])
        for k in range(KC_D):
            dma("sync", w_in_h4[:, k * DI:(k + 1) * DI],
                w_in[k * 128:(k + 1) * 128, DI:FZ], "s_w")
        for k in range(KC_I):
            dma("sync", w_out_s[:, k * D:(k + 1) * D],
                w_out[k * 128:(k + 1) * 128, :], "s_w")
        w_p4_ld = snap("s_w")
        wait("tensor", "s_w", w_p4_ld)
        xn_done = [0, 0]
        aux_done = [0, 0]
        for ti in range(TO // WP):
            t0 = ti * WP
            xb = xn_t[ti % 2]
            xtb = xtm_t[ti % 2]
            ftb = ffn_t[ti % 2]
            if ti >= 2:
                wait("sync", "s_mm", xn_done[ti % 2])
                wait("gpsimd", "s_ve", aux_done[ti % 2])
            for k in range(KC_D):
                dma("sync", xb[:, k * WP:(k + 1) * WP],
                    xn_sp[k * 128:(k + 1) * 128, t0:t0 + WP], "s_x")
            dma("gpsimd", xtb[:], x_tm[t0:t0 + 128, :], "s_sp")
            dma("gpsimd", ftb[:], ffn_sp[t0:t0 + 128, :], "s_sp")
            ld_x = snap("s_x")
            ld_aux = snap("s_sp")
            wait("tensor", "s_x", ld_x)
            z_free = [0, 0]
            ab_mark = [0, 0]
            wout_mark = [0, 0]

            def emit_z(m):
                bank = pb[m % 2]
                szm = szb[m % 2]
                if m >= 2:
                    wait("tensor", "s_se", z_free[m % 2])
                for k in range(KC_D):
                    op("tensor", lambda e, m=m, k=k, bank=bank, xb=xb:
                       e.matmul(
                           bank[:, 0:WP],
                           w_in_h4[:, k * DI + m * 128:k * DI + (m + 1) * 128],
                           xb[:, k * WP:(k + 1) * WP],
                           start=(k == 0), stop=(k == KC_D - 1)),
                       "s_mm" if k == KC_D - 1 else None)
                wait("scalar", "s_mm", snap("s_mm"))
                if m >= 2:
                    wait("scalar", "s_ve", ab_mark[m % 2])
                z_free[m % 2] = op(
                    "scalar", lambda e, bank=bank, szm=szm: e.activation(
                        szm[:], bank[:, 0:WP], AF.Silu), "s_se")
                wait("vector", "s_se", z_free[m % 2])
                if m >= 2:
                    wait("vector", "s_mm", wout_mark[m % 2])
                abm = ab[m % 2]
                op("vector", lambda e, m=m, t0=t0, abm=abm: e.tensor_scalar(
                    abm[:], yrep[:, t0:t0 + WP],
                    s_lng[:, m:m + 1], s_lnb[:, m:m + 1],
                    ALU.mult, ALU.add))
                ab_mark[m % 2] = op(
                    "vector", lambda e, abm=abm, szm=szm: e.tensor_tensor(
                        abm[:], abm[:], szm[:], ALU.mult), "s_ve")

            def emit_wout(m):
                abm = ab[m % 2]
                wait("tensor", "s_ve", ab_mark[m % 2])
                for tg in range(2):
                    for dh in range(2):
                        wout_mark[m % 2] = op(
                            "tensor", lambda e, m=m, tg=tg, dh=dh, abm=abm:
                            e.matmul(
                                pb[4 + tg * 2 + dh][:, 0:512],
                                abm[:, tg * 128:(tg + 1) * 128],
                                w_out_s[:, m * D + dh * 512:
                                        m * D + (dh + 1) * 512],
                                start=(m == 0), stop=(m == KC_I - 1)),
                            "s_mm")

            for m in range(KC_I):
                emit_z(m)
                if m >= 1:
                    emit_wout(m - 1)
            emit_wout(KC_I - 1)
            mm_fin = cnt["s_mm"]
            xn_done[ti % 2] = mm_fin
            wait("vector", "s_mm", mm_fin)
            wait("vector", "s_sp", ld_aux)
            for dh in range(2):
                op("vector", lambda e, dh=dh, xtb=xtb: e.tensor_tensor(
                    ob[:, dh * 512:(dh + 1) * 512],
                    pb[4 + dh][:, 0:512],
                    xtb[:, dh * 512:(dh + 1) * 512], ALU.add))
                op("vector", lambda e, dh=dh, ftb=ftb: e.tensor_tensor(
                    ob[:, dh * 512:(dh + 1) * 512],
                    ob[:, dh * 512:(dh + 1) * 512],
                    ftb[:, dh * 512:(dh + 1) * 512], ALU.add),
                    "s_ve" if dh == 1 else None)
            ve1 = cnt["s_ve"]
            wait("gpsimd", "s_ve", ve1)
            dma("gpsimd", out[t0:t0 + 128, :], ob[:, 0:D], "s_sp")
            dma("gpsimd", xtb[:], x_tm[t0 + 128:t0 + 256, :], "s_sp")
            dma("gpsimd", ftb[:], ffn_sp[t0 + 128:t0 + 256, :], "s_sp")
            ld_aux2 = snap("s_sp")
            wait("vector", "s_sp", ld_aux2)
            for dh in range(2):
                op("vector", lambda e, dh=dh, xtb=xtb: e.tensor_tensor(
                    ob[:, D + dh * 512:D + (dh + 1) * 512],
                    pb[6 + dh][:, 0:512],
                    xtb[:, dh * 512:(dh + 1) * 512], ALU.add))
                op("vector", lambda e, dh=dh, ftb=ftb: e.tensor_tensor(
                    ob[:, D + dh * 512:D + (dh + 1) * 512],
                    ob[:, D + dh * 512:D + (dh + 1) * 512],
                    ftb[:, dh * 512:(dh + 1) * 512], ALU.add),
                    "s_ve" if dh == 1 else None)
            ve2 = cnt["s_ve"]
            aux_done[ti % 2] = ve2
            wait("gpsimd", "s_ve", ve2)
            dma("gpsimd", out[t0 + 128:t0 + 256, :], ob[:, D:2 * D], "s_sp")
            wait("tensor", "s_ve", ve2)
        wait("gpsimd", "s_sp", snap("s_sp"))
        wait("sync", "s_sp", snap("s_sp"))

        cutname = {1: "p2", 2: "aii", 3: "p3"}.get(phases)
        envcut = os.environ.get("ATH_CUT")
        if envcut:
            cutname = envcut
        if cutname is not None:
            for e in plan:
                plan[e] = plan[e][:marks[cutname][e]]

        @block.sync
        def _(eng):
            for fn in plan["sync"]:
                fn(eng)

        @block.tensor
        def _(eng):
            for fn in plan["tensor"]:
                fn(eng)

        @block.vector
        def _(eng):
            for fn in plan["vector"]:
                fn(eng)

        @block.scalar
        def _(eng):
            for fn in plan["scalar"]:
                fn(eng)

        @block.gpsimd
        def _(eng):
            for fn in plan["gpsimd"]:
                fn(eng)

    return nc


def _off(nc, handle):
    return nc.lookup_mloc(handle).addr


def _host_inputs(inputs):
    x = np.ascontiguousarray(np.asarray(inputs["L0_attn_x"], np.float32))
    w_in = np.ascontiguousarray(np.asarray(inputs["L0_attn_W_in"], np.float32))
    w_conv = np.asarray(inputs["L0_attn_W_conv"], np.float32)
    w_xproj = np.asarray(inputs["L0_attn_W_xproj"], np.float32)
    w_dt = np.ascontiguousarray(np.asarray(inputs["L0_attn_W_dt"], np.float32))
    b_dt = np.asarray(inputs["L0_attn_b_dt"], np.float32)
    w_out = np.ascontiguousarray(np.asarray(inputs["L0_attn_W_out"], np.float32))
    ln_g = np.asarray(inputs["L0_attn_ln_g"], np.float32)
    ln_b = np.asarray(inputs["L0_attn_ln_b"], np.float32)
    gate_w = np.asarray(inputs["L0_ffn_gate_w"], np.float32)
    w1 = np.asarray(inputs["L0_ffn_w1"], np.float32)
    w3 = np.asarray(inputs["L0_ffn_w3"], np.float32)
    w2 = np.asarray(inputs["L0_ffn_w2"], np.float32)
    sw1 = np.asarray(inputs["L0_ffn_sw1"], np.float32)
    sw3 = np.asarray(inputs["L0_ffn_sw3"], np.float32)
    sw2 = np.asarray(inputs["L0_ffn_sw2"], np.float32)

    conv3 = w_conv.reshape(DI, 3)
    conv_c = np.ascontiguousarray(
        conv3.reshape(KC_I, 128, 3).transpose(1, 0, 2).reshape(128, KC_I * 3))
    wxp_h = np.ascontiguousarray(np.concatenate([
        w_xproj[:, 64:80], w_xproj[:, 80:96], w_xproj[:, 0:64],
        np.full((DI, 1), 1.0 / DI, np.float32)], axis=1))
    b_dt_c = np.ascontiguousarray(b_dt.reshape(KC_I, 128).T)
    ln_g_c = np.ascontiguousarray(ln_g.reshape(KC_I, 128).T)
    ln_b_c = np.ascontiguousarray(ln_b.reshape(KC_I, 128).T)
    gate_wT = np.ascontiguousarray(gate_w.T)
    w13 = np.ascontiguousarray(np.concatenate(
        [w1.transpose(0, 2, 1), w3.transpose(0, 2, 1)], axis=2))
    w2t = np.ascontiguousarray(w2.transpose(0, 2, 1))
    sw13 = np.ascontiguousarray(np.concatenate([sw1.T, sw3.T], axis=1))
    sw2t = np.ascontiguousarray(sw2.T)

    consts = np.zeros((128, 8), np.float32)
    consts[:, 0] = 1.0
    consts[:, 1] = EPS
    ident = np.eye(128, dtype=np.float32)
    iota16 = np.tile(np.arange(NE, dtype=np.float32), (128, 1))
    iotamb = np.ascontiguousarray(iota16 - BIG * 10)
    iota16 = np.ascontiguousarray(iota16)
    p16 = np.zeros((16, 128), np.float32)
    p16[np.arange(128) % 16, np.arange(128)] = 1.0
    sel16 = np.zeros((16, NE * 128), np.float32)
    for e in range(NE):
        sel16[e, e * 128:(e + 1) * 128] = 1.0
    ones_r = np.ones((1, 128), np.float32)

    shared = dict(w_in=w_in, conv_w=conv_c, wxp=wxp_h, w_dt=w_dt,
                  b_dt_c=b_dt_c, w_out=w_out, ln_g_c=ln_g_c, ln_b_c=ln_b_c,
                  gate_wT=gate_wT, w13=w13, w2t=w2t, sw13=sw13, sw2t=sw2t,
                  consts=consts, ident=ident, iota16=iota16, iotamb=iotamb,
                  p16=p16, sel16=sel16, ones_r=ones_r)

    in_maps = []
    for c in range(8):
        seq, half = c // 2, c % 2
        xs = x[seq]
        x_fmc = np.zeros((D, TAL), np.float32)
        x_fmc[:, 2048:4096] = xs[half * 2048:(half + 1) * 2048].T
        if half == 1:
            x_fmc[:, 0:2048] = xs[0:2048].T
        else:
            x_fmc[:, 4096] = xs[2048]
        m = dict(shared)
        m["x_fm"] = np.ascontiguousarray(x_fmc)
        m["x_tm"] = np.ascontiguousarray(xs[half * 2048:(half + 1) * 2048])
        in_maps.append(m)
    return in_maps


def kernel(**inputs):
    in_maps = _host_inputs(inputs)
    nc = build_program()
    res = run_bass_kernel_spmd(nc, in_maps, core_ids=list(range(8)))
    outa = np.zeros((4, 4096, D), np.float32)
    for c in range(8):
        seq, half = c // 2, c % 2
        outa[seq, half * 2048:(half + 1) * 2048, :] = res.results[c]["out"]
    return outa


# version 09
# speedup vs baseline: 1.0467x; 1.0206x over previous
import contextlib
import os
import numpy as np
import concourse.bass as bass
import concourse.mybir as mybir
from concourse.bass_utils import run_bass_kernel_spmd

F32 = mybir.dt.float32
AF = mybir.ActivationFunctionType
ALU = mybir.AluOpType
AX = mybir.AxisListType

D = 1024
DI = 2048
FZ = 4096
HID = 512
NE = 16
TO = 2048
TA = 4096
TAL = 4097
WP = 256
NT2 = TA // WP
WA = 512
NTA = TA // WA
TH = 1024
EPS = 1e-5
BIG = 1.0e5
SET_LNEXP = 6
SET_SILU = 18
KC_D = D // 128
KC_I = DI // 128
WPP = WP + 2


def _load_act_set(nc, eng, idx):
    eng.add_instruction(mybir.InstLoadActFuncSet(
        name=nc.get_next_instruction_name(), act_func_set_id=idx,
        ins=[], outs=[]))


def build_program():
    phases = int(os.environ.get("ATH_PHASES", "4"))
    skips = set(os.environ.get("ATH_SKIP", "").split(","))
    nc = bass.Bass("TRN2", target_bir_lowering=False,
                   detect_race_conditions=False)

    def din(name, shape):
        return nc.dram_tensor(name, shape, F32, kind="ExternalInput").ap()

    x_fm = din("x_fm", [D, TAL])
    x_tm = din("x_tm", [TO, D])
    w_in = din("w_in", [D, FZ])
    conv_w = din("conv_w", [128, KC_I * 3])
    wxp = din("wxp", [DI, 97])
    w_dt = din("w_dt", [64, DI])
    b_dt_c = din("b_dt_c", [128, KC_I])
    w_out = din("w_out", [DI, D])
    ln_g_c = din("ln_g_c", [128, KC_I])
    ln_b_c = din("ln_b_c", [128, KC_I])
    gate_wT = din("gate_wT", [D, NE])
    w13 = din("w13", [NE, D, 1024])
    w2t = din("w2t", [NE, HID, D])
    sw13 = din("sw13", [D, 1024])
    sw2t = din("sw2t", [HID, D])
    consts = din("consts", [128, 8])
    ident = din("ident", [128, 128])
    iota16 = din("iota16", [128, NE])
    iotamb = din("iotamb", [128, NE])
    iotab = din("iotab", [128, NE * 16])
    iotambb = din("iotambb", [128, NE * 16])
    p16 = din("p16", [16, 128])
    sel16 = din("sel16", [16, NE * 128])
    ones_r = din("ones_r", [1, 128])

    dbg = os.environ.get("ATH_DBGOUT", "0") == "1"
    ikind = "ExternalOutput" if dbg else "Internal"
    xn_sp = nc.dram_tensor("xn_sp", [D, TO], F32, kind=ikind).ap()
    ffn_sp = nc.dram_tensor("ffn_sp", [TO, D], F32, kind=ikind).ap()
    dbg_t = (nc.dram_tensor("dbg_t", [128, 8192], F32, kind="ExternalOutput").ap()
             if dbg else None)
    out = nc.dram_tensor("out", [TO, D], F32, kind="ExternalOutput").ap()

    ctx = contextlib.ExitStack()
    with ctx:
        def sbuf(name, shape):
            return ctx.enter_context(nc.sbuf_tensor(name, shape, F32))

        s_consts = sbuf("s_consts", [128, 8])
        s_ident = sbuf("s_ident", [128, 128])
        s_iota = sbuf("s_iota", [128, NE])
        s_iotamb = sbuf("s_iotamb", [128, NE])
        s_p16 = sbuf("s_p16", [16, 128])
        s_ones_r = sbuf("s_ones_r", [1, 128])
        s_convw = sbuf("s_convw", [128, KC_I * 3])
        s_bdt = sbuf("s_bdt", [128, KC_I])
        s_lng = sbuf("s_lng", [128, KC_I])
        s_lnb = sbuf("s_lnb", [128, KC_I])

        SBTOP = nc.SBUF_PARTITION_SIZE_BYTES
        MBASE = 26624
        assert nc.sbuf_base <= MBASE, nc.sbuf_base
        cur = [MBASE]

        def at(name, shape, off=None):
            if off is None:
                off = cur[0]
                nbytes = int(np.prod(shape[1:])) * 4
                cur[0] = off + ((nbytes + 31) // 32) * 32
            h = nc.alloc_sbuf_tensor_at(name, shape, F32, offset=off)
            assert off + int(np.prod(shape[1:])) * 4 <= SBTOP, (name, off)
            return h

        yrep = at("yrep", [128, TO])
        comb_fm = at("comb_fm", [16, TO])
        y16 = at("y16", [16, WA])
        y16b = at("y16b", [16, WA])
        G16 = NE * 16
        PERM_END = cur[0]
        r_dec = at("r_dec", [1, TA])
        r_c = at("r_c", [1, TA])
        dt_lo = at("dt_lo", [128, TO])
        cp_buf = at("cp_buf", [16, TO])
        lg_fm = at("lg_fm", [16, TO])
        MID_END = cur[0]
        r_s = r_c
        w_in_h = at("w_in_h", [128, KC_D * DI])
        w_xp = at("w_xp", [128, KC_I * 97])
        w_gate = at("w_gate", [128, KC_D * NE])
        WPS = 512
        xt = [at(f"xt{i}", [128, KC_D * WPS]) for i in range(2)]
        xzb = at("xzb", [128, WPS])
        xsb = [at(f"xsb{i}", [128, WPS]) for i in range(2)]
        cacc = [at(f"cacc{i}", [128, WPS]) for i in range(2)]
        bpcp = cacc[0]
        r_mu = at("r_mu", [1, WPS])
        r_var = at("r_var", [1, WPS])
        r_inv = at("r_inv", [1, WPS])
        r_xm = at("r_xm", [1, WPS])
        assert cur[0] <= SBTOP, cur[0]
        r_bps = at("r_bps", [1, WPS], off=_off(nc, xzb))
        w_dt_s = at("w_dt_s", [128, DI], off=_off(nc, xt[0]))
        r_dtm = at("r_dtm", [1, WA], off=_off(nc, xt[0]) + 8192)
        xt0o = _off(nc, xt[0]) + 10240
        tkl = at("tkl", [128, G16], off=xt0o)
        tkcur = at("tkcur", [128, G16], off=xt0o + 1024)
        tkeq = at("tkeq", [128, G16], off=xt0o + 2048)
        tkcand = at("tkcand", [128, G16], off=xt0o + 3072)
        tksel = at("tksel", [128, G16], off=xt0o + 4096)
        tkw = at("tkw", [128, G16], off=xt0o + 5120)
        xt1o = _off(nc, xt[1]) + 12288
        tkm = at("tkm", [128, 16], off=xt1o)
        s_iotab = at("s_iotab", [128, G16], off=xt1o + 64)
        s_iotambb = at("s_iotambb", [128, G16], off=xt1o + 1088)
        dtscr = [at(f"dtscr{i}", [128, WA], off=_off(nc, xt[1]) + i * 2048)
                 for i in range(2)]
        r_muy = at("r_muy", [1, TO], off=_off(nc, xt[1]) + 4096)
        r_invy = at("r_invy", [1, TO], off=_off(nc, w_in_h))
        r_tmp = at("r_tmp", [1, TO], off=_off(nc, w_in_h) + 8192)
        p3 = [PERM_END]

        def at3(name, shape):
            off = p3[0]
            nbytes = int(np.prod(shape[1:])) * 4
            p3[0] = off + ((nbytes + 31) // 32) * 32
            h = nc.alloc_sbuf_tensor_at(name, shape, F32, offset=off)
            assert off + nbytes <= SBTOP, (name, off)
            return h

        xn_h = at3("xn_h", [128, KC_D * TH])
        acc = at3("acc", [128, 8 * D])
        w13_s = [at3(f"w13_s{i}", [128, KC_D * 1024]) for i in range(2)]
        w2_s = at3("w2_s", [128, 4 * D])
        act_b = at3("act_b", [128, 4 * TH])
        silu_b = at3("silu_b", [128, 512])
        h3c_b = at3("h3c_b", [128, 512])
        cbc_b = at3("cbc_b", [128, TH])
        s_sel = at3("s_sel", [16, NE * 128])
        assert p3[0] <= SBTOP, p3[0]
        p4 = [PERM_END]

        def at4(name, shape):
            off = p4[0]
            nbytes = int(np.prod(shape[1:])) * 4
            p4[0] = off + ((nbytes + 31) // 32) * 32
            h = nc.alloc_sbuf_tensor_at(name, shape, F32, offset=off)
            assert off + nbytes <= SBTOP, (name, off)
            return h

        w_in_h4 = at4("w_in_h4", [128, KC_D * DI])
        w_out_s = at4("w_out_s", [128, KC_I * D])
        xn_t = [at4(f"xn_t{i}", [128, KC_D * WP]) for i in range(2)]
        xtm_t = [at4(f"xtm_t{i}", [128, D]) for i in range(2)]
        ffn_t = [at4(f"ffn_t{i}", [128, D]) for i in range(2)]
        szb = [at4(f"szb{i}", [128, WP]) for i in range(2)]
        ab = [at4(f"ab{i}", [128, WP]) for i in range(2)]
        ob = at4("ob", [128, 2 * D])
        assert p4[0] <= SBTOP, p4[0]

        pb = [ctx.enter_context(nc.psum_tensor(f"pb{i}", [128, 512], F32))
              for i in range(8)]

        sems = {}
        for sname in ["s_x", "s_w", "s_sp", "s_mm", "s_ve", "s_se"]:
            sems[sname] = ctx.enter_context(nc.semaphore(sname))
        cnt = {k: 0 for k in sems}

        block = ctx.enter_context(nc.Block())
        plan = {e: [] for e in ("sync", "tensor", "vector", "scalar", "gpsimd")}
        marks = {}

        def mark(name):
            marks[name] = {e: len(plan[e]) for e in plan}

        def emit(eng, fn):
            plan[eng].append(fn)

        def dma(q, out_ap, in_ap, sem, slow=False):
            cnt[sem] += 16
            s = sems[sem]
            if slow:
                emit(q, lambda e, o=out_ap, i=in_ap, s=s:
                     e.dma_start(out=o, in_=i, allow_slow_non_contiguous=True)
                     .then_inc(s, 16))
            else:
                emit(q, lambda e, o=out_ap, i=in_ap, s=s:
                     e.dma_start(out=o, in_=i).then_inc(s, 16))
            return cnt[sem]

        def wait(eng, sem, val=None):
            v = cnt[sem] if val is None else val
            s = sems[sem]
            emit(eng, lambda e, s=s, v=v: e.wait_ge(s, v))

        def snap(sem):
            return cnt[sem]

        def op(eng, fn, sem=None):
            if sem is None:
                emit(eng, fn)
                return None
            cnt[sem] += 1
            s = sems[sem]
            emit(eng, lambda e, f=fn, s=s: f(e).then_inc(s, 1))
            return cnt[sem]

        def marker(eng, sem):
            cnt[sem] += 1
            s = sems[sem]
            emit(eng, lambda e, s=s: e.drain().then_inc(s, 1))
            return cnt[sem]

        ENG_SEM = {"tensor": "s_mm", "vector": "s_ve", "scalar": "s_se"}

        def barrier():
            for e in ("tensor", "vector", "scalar"):
                marker(e, ENG_SEM[e])
            tgt = {s: cnt[s] for s in ("s_mm", "s_ve", "s_se")}
            for e in ("tensor", "vector", "scalar", "gpsimd"):
                for s in ("s_mm", "s_ve", "s_se"):
                    wait(e, s, tgt[s])
            return tgt

        for dst, src in [(s_consts, consts), (s_ident, ident),
                         (s_iota, iota16), (s_iotamb, iotamb),
                         (s_p16, p16), (s_ones_r, ones_r),
                         (s_convw, conv_w), (s_bdt, b_dt_c),
                         (s_lng, ln_g_c), (s_lnb, ln_b_c)]:
            dma("sync", dst[:], src, "s_x")
        consts_ld = snap("s_x")
        for k in range(KC_D):
            dma("sync", w_in_h[:, k * DI:(k + 1) * DI],
                w_in[k * 128:(k + 1) * 128, 0:DI], "s_w")
            dma("sync", w_gate[:, k * NE:(k + 1) * NE],
                gate_wT[k * 128:(k + 1) * 128, :], "s_w")
        for k in range(KC_I):
            dma("sync", w_xp[:, k * 97:(k + 1) * 97],
                wxp[k * 128:(k + 1) * 128, :], "s_w")
        w_p2_ld = snap("s_w")
        mark("pre")

        ONE = s_consts[:, 0:1]
        EPS_AP = s_consts[0:1, 1:2]

        wait("tensor", "s_x", consts_ld)
        wait("vector", "s_x", consts_ld)
        wait("scalar", "s_x", consts_ld)
        wait("tensor", "s_w", w_p2_ld)
        xt_done = [0, 0]
        xt_sp = [0, 0]
        sq_done = [0, 0]
        cp_dma_prev = 0
        widths = [410, 410, 410, 410, 408]
        p2_tiles = []
        for h in range(2):
            t = h * TO
            for wdt in widths:
                p2_tiles.append((t, wdt))
                t += wdt
        for ti, (t0, W) in enumerate(p2_tiles):
            own = t0 >= TO
            lo = max(t0 - 1, 0)
            hi = t0 + W + 1
            wlo = t0 - lo
            off = 1 - wlo
            wwin = hi - lo
            buf = xt[ti % 2]
            if ti >= 2:
                wait("sync", "s_ve", xt_done[ti % 2])
                wait("sync", "s_sp", xt_sp[ti % 2])
            for k in range(KC_D):
                dma("sync", buf[:, k * WPS:k * WPS + wwin],
                    x_fm[k * 128:(k + 1) * 128, lo:hi], "s_x")
            ld = snap("s_x")
            wait("tensor", "s_x", ld)
            wait("scalar", "s_x", ld)
            for k in range(KC_D):
                op("tensor", lambda e, k=k, buf=buf, wwin=wwin:
                   e.matmul(pb[0][0:1, 0:wwin], ONE,
                            buf[:, k * WPS:k * WPS + wwin],
                            start=(k == 0), stop=(k == KC_D - 1)))
            for k in range(KC_D):
                sqb = cacc[k % 2]
                if k >= 2:
                    wait("scalar", "s_mm", sq_done[k % 2])
                se_id = op("scalar",
                           lambda e, k=k, buf=buf, wwin=wwin, sqb=sqb:
                           e.activation(sqb[:, 0:wwin],
                                        buf[:, k * WPS:k * WPS + wwin],
                                        AF.Square), "s_se")
                wait("tensor", "s_se", se_id)
                sq_done[k % 2] = op(
                    "tensor", lambda e, k=k, wwin=wwin, sqb=sqb:
                    e.matmul(pb[1][0:1, 0:wwin], ONE, sqb[:, 0:wwin],
                             start=(k == 0), stop=(k == KC_D - 1)), "s_mm")
            wait("vector", "s_mm", snap("s_mm"))
            op("vector", lambda e, wwin=wwin:
               e.tensor_scalar(r_mu[:, 0:wwin], pb[0][0:1, 0:wwin],
                               1.0 / D, None, ALU.mult))
            op("vector", lambda e, wwin=wwin:
               e.tensor_scalar(r_var[:, 0:wwin], pb[1][0:1, 0:wwin],
                               1.0 / D, None, ALU.mult))
            op("vector", lambda e, wwin=wwin: e.tensor_tensor(
                r_inv[:, 0:wwin], r_mu[:, 0:wwin], r_mu[:, 0:wwin],
                ALU.mult))
            ve_id = op("vector", lambda e, wwin=wwin: e.tensor_tensor(
                r_var[:, 0:wwin], r_var[:, 0:wwin], r_inv[:, 0:wwin],
                ALU.subtract), "s_ve")
            wait("scalar", "s_ve", ve_id)
            emit("scalar", lambda e: _load_act_set(nc, e, SET_LNEXP))
            op("scalar", lambda e, wwin=wwin: e.activation(
                r_inv[:, 0:wwin], r_var[:, 0:wwin], AF.Ln, bias=EPS_AP))
            se_id = op("scalar", lambda e, wwin=wwin: e.activation(
                r_inv[:, 0:wwin], r_inv[:, 0:wwin], AF.Exp, scale=-0.5),
                "s_se")
            emit("scalar", lambda e: _load_act_set(nc, e, SET_SILU))
            wait("tensor", "s_se", se_id)
            op("tensor", lambda e, wwin=wwin:
               e.matmul(pb[4][:, 0:wwin], s_ones_r[0:1, :],
                        r_mu[:, 0:wwin], start=True, stop=True))
            op("tensor", lambda e, wwin=wwin:
               e.matmul(pb[5][:, 0:wwin], s_ones_r[0:1, :],
                        r_inv[:, 0:wwin], start=True, stop=True), "s_mm")
            wait("vector", "s_mm", snap("s_mm"))
            for k in range(KC_D):
                op("vector", lambda e, k=k, buf=buf, wwin=wwin:
                   e.tensor_tensor(buf[:, k * WPS:k * WPS + wwin],
                                   buf[:, k * WPS:k * WPS + wwin],
                                   pb[4][:, 0:wwin], ALU.subtract))
                op("vector", lambda e, k=k, buf=buf, wwin=wwin:
                   e.tensor_tensor(buf[:, k * WPS:k * WPS + wwin],
                                   buf[:, k * WPS:k * WPS + wwin],
                                   pb[5][:, 0:wwin], ALU.mult),
                   "s_ve" if k == KC_D - 1 else None)
            xn_ready = cnt["s_ve"]
            if own:
                wait("gpsimd", "s_ve", xn_ready)
                for k in range(KC_D):
                    dma("gpsimd",
                        xn_sp[k * 128:(k + 1) * 128, t0 - TO:t0 - TO + W],
                        buf[:, k * WPS + wlo:k * WPS + wlo + W], "s_sp")
            wait("tensor", "s_ve", xn_ready)
            xz_evac = [0, 0, 0, 0]
            silu_mark = [0, 0]
            xp_mark = [0, 0]

            def emit_xz(m):
                bank = pb[m % 4]
                if m >= 4:
                    wait("tensor", "s_ve", xz_evac[m % 4])
                for k in range(KC_D):
                    op("tensor",
                       lambda e, m=m, k=k, bank=bank, wwin=wwin, buf=buf:
                       e.matmul(bank[:, 0:wwin],
                                w_in_h[:, k * DI + m * 128:k * DI + (m + 1) * 128],
                                buf[:, k * WPS:k * WPS + wwin],
                                start=(k == 0), stop=(k == KC_D - 1)),
                       "s_mm" if k == KC_D - 1 else None)
                wait("vector", "s_mm", snap("s_mm"))
                if ti == 0:
                    op("vector", lambda e: e.tensor_scalar(
                        xzb[:, 0:1], s_consts[:, 0:1], 0.0, None, ALU.mult))
                xz_evac[m % 4] = op(
                    "vector", lambda e, off=off, wwin=wwin, bank=bank:
                    e.tensor_copy(xzb[:, off:off + wwin], bank[:, 0:wwin]),
                    "s_ve")
                cb = cacc[m % 2]
                xb2 = xsb[m % 2]
                if m >= 2:
                    wait("vector", "s_se", silu_mark[m % 2])
                op("vector", lambda e, m=m, cb=cb, W=W: e.tensor_scalar(
                    cb[:, 0:W], xzb[:, 0:W],
                    s_convw[:, m * 3:m * 3 + 1], None, ALU.mult))
                op("vector", lambda e, m=m, cb=cb, W=W: e.scalar_tensor_tensor(
                    cb[:, 0:W], xzb[:, 1:1 + W],
                    s_convw[:, m * 3 + 1:m * 3 + 2], cb[:, 0:W],
                    ALU.mult, ALU.add))
                ve_id = op("vector", lambda e, m=m, cb=cb, W=W:
                           e.scalar_tensor_tensor(
                               cb[:, 0:W], xzb[:, 2:2 + W],
                               s_convw[:, m * 3 + 2:m * 3 + 3], cb[:, 0:W],
                               ALU.mult, ALU.add), "s_ve")
                wait("scalar", "s_ve", ve_id)
                if m >= 2:
                    wait("scalar", "s_mm", xp_mark[m % 2])
                silu_mark[m % 2] = op(
                    "scalar", lambda e, m=m, cb=cb, xb2=xb2, W=W:
                    e.activation(xb2[:, 0:W], cb[:, 0:W], AF.Silu), "s_se")

            def emit_xproj(m):
                xb2 = xsb[m % 2]
                wait("tensor", "s_se", silu_mark[m % 2])
                xp_mark[m % 2] = op(
                    "tensor", lambda e, m=m, xb2=xb2, W=W: e.matmul(
                        pb[6][0:97, 0:W], w_xp[:, m * 97:(m + 1) * 97],
                        xb2[:, 0:W], start=(m == 0), stop=(m == KC_I - 1)),
                    "s_mm")

            for m in range(KC_I):
                emit_xz(m)
                if m >= 1:
                    emit_xproj(m - 1)
            emit_xproj(KC_I - 1)
            wait("vector", "s_mm", snap("s_mm"))
            if own:
                wait("vector", "s_sp", cp_dma_prev)
            hlf = 64 * (t0 // TO)
            tc = t0 % TO
            op("vector", lambda e, W=W: e.tensor_copy(
                bpcp[0:32, 0:W], pb[6][0:32, 0:W]))
            op("vector", lambda e, hlf=hlf, tc=tc, W=W: e.tensor_copy(
                dt_lo[hlf:hlf + 32, tc:tc + W], pb[6][32:64, 0:W]))
            op("vector", lambda e, hlf=hlf, tc=tc, W=W: e.tensor_copy(
                dt_lo[hlf + 32:hlf + 64, tc:tc + W], pb[6][64:96, 0:W]))
            ve_id = op("vector", lambda e, W=W: e.tensor_copy(
                r_xm[:, 0:W], pb[6][96:97, 0:W]), "s_ve")
            wait("tensor", "s_ve", ve_id)
            mm_id = op("tensor", lambda e, W=W: e.matmul(
                pb[7][0:1, 0:W], s_consts[0:16, 0:1], bpcp[0:16, 0:W],
                start=True, stop=True), "s_mm")
            wait("vector", "s_mm", mm_id)
            op("vector", lambda e, W=W: e.tensor_copy(
                r_bps[:, 0:W], pb[7][0:1, 0:W]))
            op("vector", lambda e, t0=t0, W=W: e.tensor_tensor(
                r_c[:, t0:t0 + W], r_xm[:, 0:W], r_bps[:, 0:W],
                ALU.mult), "s_ve")
            if own:
                wait("gpsimd", "s_ve", snap("s_ve"))
                cp_dma_prev = dma("gpsimd", cp_buf[:, tc:tc + W],
                                  bpcp[16:32, 0:W], "s_sp")
                for k in range(KC_D):
                    op("tensor", lambda e, k=k, wlo=wlo, W=W, buf=buf:
                       e.matmul(
                           pb[4][0:16, 0:W],
                           w_gate[:, k * NE:(k + 1) * NE],
                           buf[:, k * WPS + wlo:k * WPS + wlo + W],
                           start=(k == 0), stop=(k == KC_D - 1)),
                       "s_mm" if k == KC_D - 1 else None)
                wait("vector", "s_mm", snap("s_mm"))
                op("vector", lambda e, tc=tc, W=W: e.tensor_copy(
                    lg_fm[:, tc:tc + W], pb[4][0:16, 0:W]), "s_ve")
            bt = barrier()
            xt_done[ti % 2] = bt["s_ve"]
            xt_sp[ti % 2] = snap("s_sp")
            wait("vector", "s_sp", snap("s_sp"))
            wait("scalar", "s_sp", snap("s_sp"))
            mark(f"t{ti}")
        p2_end = barrier()
        sp_after_p2 = snap("s_sp")
        mark("p2")
        emit("scalar", lambda e: _load_act_set(nc, e, SET_LNEXP))
        wait("vector", "s_sp", sp_after_p2)
        wait("sync", "s_mm", p2_end["s_mm"])
        wait("sync", "s_sp", sp_after_p2)
        dma("sync", w_dt_s[0:64, :], w_dt, "s_w")
        dma("sync", w_dt_s[64:128, :], w_dt, "s_w")
        dma("sync", s_iotab[:], iotab, "s_w")
        dma("sync", s_iotambb[:], iotambb, "s_w")
        w_dt_ld = snap("s_w")
        wait("tensor", "s_w", w_dt_ld)
        for ti in range(NTA):
            t0 = ti * WA
            hlf = 64 * (t0 // TO)
            tc = t0 % TO
            dt_sum = [0, 0]
            se_done = [0, 0]

            def emit_dt(m):
                bank = pb[m % 2]
                scr = dtscr[m % 2]
                if m >= 2:
                    wait("tensor", "s_mm", dt_sum[m % 2])
                op("tensor", lambda e, m=m, hlf=hlf, tc=tc, bank=bank:
                   e.matmul(bank[:, 0:WA],
                            w_dt_s[hlf:hlf + 64, m * 128:(m + 1) * 128],
                            dt_lo[hlf:hlf + 64, tc:tc + WA],
                            start=True, stop=True), "s_mm")
                wait("scalar", "s_mm", snap("s_mm"))
                if m >= 2:
                    wait("scalar", "s_mm", dt_sum[m % 2])
                op("scalar", lambda e, m=m, bank=bank, scr=scr: e.activation(
                    scr[:], bank[:, 0:WA], AF.Exp, bias=s_bdt[:, m:m + 1]))
                op("scalar", lambda e, scr=scr: e.activation(
                    scr[:], scr[:], AF.Ln, bias=1.0))
                se_done[m % 2] = marker("scalar", "s_se")

            def emit_dtsum(m):
                scr = dtscr[m % 2]
                wait("tensor", "s_se", se_done[m % 2])
                dt_sum[m % 2] = op("tensor", lambda e, m=m, scr=scr: e.matmul(
                    pb[4][0:1, 0:WA], ONE, scr[:],
                    start=(m == 0), stop=(m == KC_I - 1)), "s_mm")

            for m in range(KC_I):
                emit_dt(m)
                if m >= 1:
                    emit_dtsum(m - 1)
            emit_dtsum(KC_I - 1)
            wait("vector", "s_mm", snap("s_mm"))
            ve_id = op("vector", lambda e: e.tensor_scalar(
                r_dtm[:, 0:WA], pb[4][0:1, 0:WA],
                1.0 / DI, None, ALU.mult), "s_ve")
            wait("scalar", "s_ve", ve_id)
            se_id = op("scalar", lambda e, t0=t0: e.activation(
                r_dec[:, t0:t0 + WA], r_dtm[:, 0:WA], AF.Exp, scale=-1.0),
                "s_se")
            wait("tensor", "s_se", se_id)
        wait("vector", "s_se", snap("s_se"))
        op("vector", lambda e: e.tensor_tensor_scan(
            r_s[:], r_dec[:], r_c[:], 0.0, ALU.mult, ALU.add), "s_ve")
        for ti in range(TO // WA):
            t0 = ti * WA
            wait("tensor", "s_ve", snap("s_ve"))
            op("tensor", lambda e, t0=t0: e.matmul(
                pb[0][0:16, 0:WA], s_ones_r[0:1, 0:16],
                r_s[:, TO + t0:TO + t0 + WA], start=True, stop=True), "s_mm")
            wait("vector", "s_mm", snap("s_mm"))
            op("vector", lambda e, t0=t0: e.tensor_tensor(
                y16[:], cp_buf[:, t0:t0 + WA], pb[0][0:16, 0:WA], ALU.mult),
                "s_ve")
            wait("scalar", "s_ve", snap("s_ve"))
            se_id = op("scalar", lambda e: e.activation(
                y16b[:], y16[:], AF.Square), "s_se")
            wait("tensor", "s_se", se_id)
            op("tensor", lambda e: e.matmul(
                pb[1][0:1, 0:WA], s_consts[0:16, 0:1], y16[:],
                start=True, stop=True))
            op("tensor", lambda e: e.matmul(
                pb[2][0:1, 0:WA], s_consts[0:16, 0:1], y16b[:],
                start=True, stop=True), "s_mm")
            wait("vector", "s_mm", snap("s_mm"))
            op("vector", lambda e, t0=t0: e.tensor_scalar(
                r_muy[:, t0:t0 + WA], pb[1][0:1, 0:WA], 1.0 / 16, None,
                ALU.mult))
            op("vector", lambda e, t0=t0: e.tensor_scalar(
                r_invy[:, t0:t0 + WA], pb[2][0:1, 0:WA], 1.0 / 16, None,
                ALU.mult))
            op("vector", lambda e, t0=t0: e.tensor_tensor(
                r_tmp[:, t0:t0 + WA], r_muy[:, t0:t0 + WA],
                r_muy[:, t0:t0 + WA], ALU.mult))
            ve_id = op("vector", lambda e, t0=t0: e.tensor_tensor(
                r_invy[:, t0:t0 + WA], r_invy[:, t0:t0 + WA],
                r_tmp[:, t0:t0 + WA], ALU.subtract), "s_ve")
            wait("scalar", "s_ve", ve_id)
            op("scalar", lambda e, t0=t0: e.activation(
                r_invy[:, t0:t0 + WA], r_invy[:, t0:t0 + WA],
                AF.Ln, bias=EPS_AP))
            se_id = op("scalar", lambda e, t0=t0: e.activation(
                r_invy[:, t0:t0 + WA], r_invy[:, t0:t0 + WA],
                AF.Exp, scale=-0.5), "s_se")
            wait("tensor", "s_se", se_id)
            op("tensor", lambda e, t0=t0: e.matmul(
                pb[1][0:16, 0:WA], s_ones_r[0:1, 0:16],
                r_muy[:, t0:t0 + WA], start=True, stop=True))
            op("tensor", lambda e, t0=t0: e.matmul(
                pb[2][0:16, 0:WA], s_ones_r[0:1, 0:16],
                r_invy[:, t0:t0 + WA], start=True, stop=True), "s_mm")
            wait("vector", "s_mm", snap("s_mm"))
            op("vector", lambda e: e.tensor_tensor(
                y16[:], y16[:], pb[1][0:16, 0:WA], ALU.subtract))
            ve_id = op("vector", lambda e: e.tensor_tensor(
                y16[:], y16[:], pb[2][0:16, 0:WA], ALU.mult), "s_ve")
            wait("tensor", "s_ve", ve_id)
            op("tensor", lambda e: e.matmul(
                pb[3][:, 0:WA], s_p16[:], y16[:], start=True, stop=True),
                "s_mm")
            wait("vector", "s_mm", snap("s_mm"))
            op("vector", lambda e, t0=t0: e.tensor_copy(
                yrep[:, t0:t0 + WA], pb[3][:, 0:WA]), "s_ve")
            wait("tensor", "s_ve", snap("s_ve"))
        G = 16

        def vsync():
            v = marker("vector", "s_ve")
            wait("vector", "s_ve", v)

        def r3(ap):
            return ap.rearrange("p (g e) -> p g e", g=G)

        def bc(ap):
            return ap[:, 0:G].unsqueeze(2).broadcast_to([128, G, 16])

        wait("vector", "s_w", snap("s_w"))
        tp_cp = [0, 0]
        for g in range(G):
            c0 = g * 128
            if g >= 2:
                wait("tensor", "s_ve", tp_cp[g % 2])
            op("tensor", lambda e, g=g, c0=c0: e.transpose(
                pb[g % 2][:, 0:16], lg_fm[:, c0:c0 + 128],
                s_ident[0:16, 0:16]), "s_mm")
            wait("vector", "s_mm", snap("s_mm"))
            tp_cp[g % 2] = op("vector", lambda e, g=g: e.tensor_copy(
                tkl[:, g * 16:(g + 1) * 16], pb[g % 2][:, 0:16]), "s_ve")
        vsync()
        op("vector", lambda e: e.tensor_reduce(
            tkm[:, 0:G].unsqueeze(2), r3(tkl[:, :]), AX.X, ALU.max))
        vsync()
        op("vector", lambda e: e.tensor_tensor(
            r3(tkl[:, :]), r3(tkl[:, :]), bc(tkm), ALU.subtract))
        op("vector", lambda e: e.tensor_copy(tkcur[:], tkl[:]))
        for r in range(4):
            op("vector", lambda e: e.tensor_reduce(
                tkm[:, 0:G].unsqueeze(2), r3(tkcur[:, :]), AX.X, ALU.max))
            vsync()
            op("vector", lambda e: e.tensor_tensor(
                r3(tkeq[:, :]), r3(tkcur[:, :]), bc(tkm), ALU.is_ge))
            op("vector", lambda e: e.tensor_tensor(
                tkcand[:], tkeq[:], s_iotambb[:], ALU.mult))
            op("vector", lambda e: e.tensor_scalar(
                tkcand[:], tkcand[:], BIG * 10, None, ALU.add))
            op("vector", lambda e: e.tensor_reduce(
                tkm[:, 0:G].unsqueeze(2), r3(tkcand[:, :]), AX.X, ALU.min))
            vsync()
            op("vector", lambda e: e.tensor_tensor(
                r3(tkeq[:, :]), r3(s_iotab[:, :]), bc(tkm), ALU.is_equal))
            if r == 0:
                op("vector", lambda e: e.tensor_copy(tksel[:], tkeq[:]))
            else:
                op("vector", lambda e: e.tensor_tensor(
                    tksel[:], tksel[:], tkeq[:], ALU.add))
            op("vector", lambda e: e.scalar_tensor_tensor(
                tkcur[:], tkeq[:], -BIG, tkcur[:], ALU.mult, ALU.add))
        ve_id = marker("vector", "s_ve")
        wait("scalar", "s_ve", ve_id)
        se_id = op("scalar", lambda e: e.activation(
            tkw[:], tkl[:], AF.Exp), "s_se")
        wait("vector", "s_se", se_id)
        op("vector", lambda e: e.tensor_tensor(
            tkw[:], tkw[:], tksel[:], ALU.mult))
        op("vector", lambda e: e.tensor_reduce(
            tkm[:, 0:G].unsqueeze(2), r3(tkw[:, :]), AX.X, ALU.add))
        vsync()
        op("vector", lambda e: e.reciprocal(tkm[:, 0:G], tkm[:, 0:G]))
        vsync()
        ve_id = op("vector", lambda e: e.tensor_tensor(
            r3(tkw[:, :]), r3(tkw[:, :]), bc(tkm), ALU.mult), "s_ve")
        wait("tensor", "s_ve", ve_id)
        tp_cp = [0, 0]
        for g in range(G):
            c0 = g * 128
            if g >= 2:
                wait("tensor", "s_ve", tp_cp[g % 2])
            op("tensor", lambda e, g=g: e.transpose(
                pb[g % 2][0:16, 0:128], tkw[:, g * 16:(g + 1) * 16],
                s_ident[:]), "s_mm")
            wait("vector", "s_mm", snap("s_mm"))
            tp_cp[g % 2] = op("vector", lambda e, g=g, c0=c0: e.tensor_copy(
                comb_fm[:, c0:c0 + 128], pb[g % 2][0:16, 0:128]), "s_ve")
        wait("tensor", "s_ve", snap("s_ve"))
        aii_end = barrier()
        if dbg:
            for s in ("s_mm", "s_ve", "s_se"):
                wait("gpsimd", s, aii_end[s])
            dma("gpsimd", dbg_t[0:1, 0:TA], r_dec[:], "s_sp")
            dma("gpsimd", dbg_t[1:2, 0:TA], r_s[:], "s_sp")
            dma("gpsimd", dbg_t[16:32, 0:TO], cp_buf[:], "s_sp")
            dma("gpsimd", dbg_t[32:48, 0:TO], comb_fm[:], "s_sp")
            dma("gpsimd", dbg_t[48:64, 0:TO], lg_fm[:], "s_sp")
            dma("gpsimd", dbg_t[0:128, TO * 2:TO * 3], dt_lo[:], "s_sp")
            dma("gpsimd", dbg_t[0:128, TO * 3:TO * 4], yrep[:], "s_sp")
            dma("gpsimd", dbg_t[0:128, 100:116], tkl[:], "s_sp")
            dma("gpsimd", dbg_t[0:128, 116:132], tkw[:], "s_sp")
            dma("gpsimd", dbg_t[0:128, 132:148], tksel[:], "s_sp")
            dma("gpsimd", dbg_t[0:128, 148:152], tkm[:, 0:4], "s_sp")
            wait("sync", "s_sp", snap("s_sp"))
        mark("aii")

        emit("scalar", lambda e: _load_act_set(nc, e, SET_SILU))
        wait("sync", "s_mm", aii_end["s_mm"])
        wait("sync", "s_ve", aii_end["s_ve"])
        wait("sync", "s_sp", sp_after_p2)
        dma("sync", s_sel[:], sel16, "s_w")

        exps = [NE] + list(range(NE))
        w13_free = [0, 0]
        w2_free = 0
        last_act = 0
        prev_eo = None
        first = True
        for hf in range(2):
            h0 = hf * TH
            if hf > 0:
                wait("sync", "s_mm", snap("s_mm"))
            for k in range(KC_D):
                dma("sync", xn_h[:, k * TH:(k + 1) * TH],
                    xn_sp[k * 128:(k + 1) * 128, h0:h0 + TH], "s_x")
            xn_ld = snap("s_x")
            wait("tensor", "s_x", xn_ld)
            for ei, ex in enumerate(exps):
                wb = w13_s[ei % 2]
                if not first or ei >= 2:
                    wait("sync", "s_ve", w13_free[ei % 2])
                src = sw13 if ex == NE else w13[ex]
                for k in range(KC_D):
                    dma("sync", wb[:, k * 1024:(k + 1) * 1024],
                        src[k * 128:(k + 1) * 128, :], "s_w")
                wait("sync", "s_mm", w2_free)
                src2 = sw2t if ex == NE else w2t[ex]
                for k in range(4):
                    dma("sync", w2_s[:, k * D:(k + 1) * D],
                        src2[k * 128:(k + 1) * 128, :], "s_w")
                w_ld = snap("s_w")
                if ex != NE:
                    if prev_eo is not None:
                        wait("tensor", "s_ve", prev_eo[1])
                    for tt in range(2):
                        op("tensor", lambda e, ex=ex, h0=h0, tt=tt: e.matmul(
                            pb[4 + tt][:, 0:512],
                            s_sel[:, ex * 128:(ex + 1) * 128],
                            comb_fm[:, h0 + tt * 512:h0 + (tt + 1) * 512],
                            start=True, stop=True),
                            "s_mm" if tt == 1 else None)
                    wait("vector", "s_mm", snap("s_mm"))
                    for tt in range(2):
                        op("vector", lambda e, tt=tt: e.tensor_copy(
                            cbc_b[:, tt * 512:(tt + 1) * 512],
                            pb[4 + tt][:, 0:512]),
                            "s_ve" if tt == 1 else None)
                cbc_ready = cnt["s_ve"]
                wait("tensor", "s_w", w_ld)
                h_free = [0, 0]
                for hh in range(4):
                    for tt in range(2):
                        bank_a = pb[(hh * 2 + tt) % 2]
                        bank_b = pb[2 + (hh * 2 + tt) % 2]
                        if hh >= 1:
                            wait("tensor", "s_ve", h_free[tt])
                        for k in range(KC_D):
                            op("tensor",
                               lambda e, k=k, hh=hh, tt=tt, bank_a=bank_a,
                               wb=wb: e.matmul(
                                   bank_a[:, 0:512],
                                   wb[:, k * 1024 + hh * 128:
                                      k * 1024 + (hh + 1) * 128],
                                   xn_h[:, k * TH + tt * 512:
                                        k * TH + (tt + 1) * 512],
                                   start=(k == 0), stop=(k == KC_D - 1)))
                        for k in range(KC_D):
                            op("tensor",
                               lambda e, k=k, hh=hh, tt=tt, bank_b=bank_b,
                               wb=wb: e.matmul(
                                   bank_b[:, 0:512],
                                   wb[:, k * 1024 + 512 + hh * 128:
                                      k * 1024 + 512 + (hh + 1) * 128],
                                   xn_h[:, k * TH + tt * 512:
                                        k * TH + (tt + 1) * 512],
                                   start=(k == 0), stop=(k == KC_D - 1)),
                               "s_mm" if k == KC_D - 1 else None)
                        mm_done = cnt["s_mm"]
                        wait("scalar", "s_mm", mm_done)
                        if last_act:
                            wait("scalar", "s_ve", last_act)
                        se_id = op("scalar",
                                   lambda e, bank_a=bank_a: e.activation(
                                       silu_b[:], bank_a[:, 0:512], AF.Silu),
                                   "s_se")
                        wait("vector", "s_mm", mm_done)
                        wait("vector", "s_ve", cbc_ready)
                        if ex != NE:
                            op("vector", lambda e, tt=tt, bank_b=bank_b:
                               e.tensor_tensor(
                                   h3c_b[:], bank_b[:, 0:512],
                                   cbc_b[:, tt * 512:(tt + 1) * 512],
                                   ALU.mult))
                        else:
                            op("vector", lambda e, bank_b=bank_b:
                               e.tensor_copy(h3c_b[:], bank_b[:, 0:512]))
                        wait("vector", "s_se", se_id)
                        last_act = op(
                            "vector", lambda e, hh=hh, tt=tt: e.tensor_tensor(
                                act_b[:, hh * TH + tt * 512:
                                      hh * TH + (tt + 1) * 512],
                                silu_b[:], h3c_b[:], ALU.mult), "s_ve")
                        h_free[tt] = last_act
                act_ready = cnt["s_ve"]
                wait("tensor", "s_ve", act_ready)
                eo_free = [0, 0, 0, 0]
                for tg in range(8):
                    for dh in range(2):
                        bank = pb[4 + (tg * 2 + dh) % 4]
                        if tg * 2 + dh >= 4:
                            wait("tensor", "s_ve", eo_free[(tg * 2 + dh) % 4])
                        elif tg * 2 + dh < 2 and ex != NE:
                            wait("tensor", "s_ve", cbc_ready)
                        for k in range(4):
                            op("tensor",
                               lambda e, k=k, tg=tg, dh=dh, bank=bank:
                               e.matmul(
                                   bank[:, 0:512],
                                   act_b[:, k * TH + tg * 128:
                                         k * TH + (tg + 1) * 128],
                                   w2_s[:, k * D + dh * 512:
                                        k * D + (dh + 1) * 512],
                                   start=(k == 0), stop=(k == 3)),
                               "s_mm" if k == 3 else None)
                        wait("vector", "s_mm", snap("s_mm"))
                        if ei == 0:
                            eo_free[(tg * 2 + dh) % 4] = op(
                                "vector", lambda e, tg=tg, dh=dh, bank=bank:
                                e.tensor_copy(
                                    acc[:, tg * D + dh * 512:
                                        tg * D + (dh + 1) * 512],
                                    bank[:, 0:512]), "s_ve")
                        else:
                            eo_free[(tg * 2 + dh) % 4] = op(
                                "vector", lambda e, tg=tg, dh=dh, bank=bank:
                                e.tensor_tensor(
                                    acc[:, tg * D + dh * 512:
                                        tg * D + (dh + 1) * 512],
                                    acc[:, tg * D + dh * 512:
                                        tg * D + (dh + 1) * 512],
                                    bank[:, 0:512], ALU.add), "s_ve")
                w2_free = cnt["s_mm"]
                w13_free[ei % 2] = cnt["s_ve"]
                prev_eo = list(eo_free)
                first = False
            wait("gpsimd", "s_ve", snap("s_ve"))
            for tg in range(8):
                dma("gpsimd", ffn_sp[h0 + tg * 128:h0 + (tg + 1) * 128, :],
                    acc[:, tg * D:(tg + 1) * D], "s_sp")
            wait("vector", "s_sp", snap("s_sp"))
        p3_end = barrier()
        mark("p3")

        wait("sync", "s_mm", p3_end["s_mm"])
        wait("sync", "s_ve", p3_end["s_ve"])
        for k in range(KC_D):
            dma("sync", w_in_h4[:, k * DI:(k + 1) * DI],
                w_in[k * 128:(k + 1) * 128, DI:FZ], "s_w")
        for k in range(KC_I):
            dma("sync", w_out_s[:, k * D:(k + 1) * D],
                w_out[k * 128:(k + 1) * 128, :], "s_w")
        w_p4_ld = snap("s_w")
        wait("tensor", "s_w", w_p4_ld)
        xn_done = [0, 0]
        aux_done = [0, 0]
        for ti in range(TO // WP):
            t0 = ti * WP
            xb = xn_t[ti % 2]
            xtb = xtm_t[ti % 2]
            ftb = ffn_t[ti % 2]
            if ti >= 2:
                wait("sync", "s_mm", xn_done[ti % 2])
                wait("gpsimd", "s_ve", aux_done[ti % 2])
            for k in range(KC_D):
                dma("sync", xb[:, k * WP:(k + 1) * WP],
                    xn_sp[k * 128:(k + 1) * 128, t0:t0 + WP], "s_x")
            dma("gpsimd", xtb[:], x_tm[t0:t0 + 128, :], "s_sp")
            dma("gpsimd", ftb[:], ffn_sp[t0:t0 + 128, :], "s_sp")
            ld_x = snap("s_x")
            ld_aux = snap("s_sp")
            wait("tensor", "s_x", ld_x)
            z_free = [0, 0]
            ab_mark = [0, 0]
            wout_mark = [0, 0]

            def emit_z(m):
                bank = pb[m % 2]
                szm = szb[m % 2]
                if m >= 2:
                    wait("tensor", "s_se", z_free[m % 2])
                for k in range(KC_D):
                    op("tensor", lambda e, m=m, k=k, bank=bank, xb=xb:
                       e.matmul(
                           bank[:, 0:WP],
                           w_in_h4[:, k * DI + m * 128:k * DI + (m + 1) * 128],
                           xb[:, k * WP:(k + 1) * WP],
                           start=(k == 0), stop=(k == KC_D - 1)),
                       "s_mm" if k == KC_D - 1 else None)
                wait("scalar", "s_mm", snap("s_mm"))
                if m >= 2:
                    wait("scalar", "s_ve", ab_mark[m % 2])
                z_free[m % 2] = op(
                    "scalar", lambda e, bank=bank, szm=szm: e.activation(
                        szm[:], bank[:, 0:WP], AF.Silu), "s_se")
                wait("vector", "s_se", z_free[m % 2])
                if m >= 2:
                    wait("vector", "s_mm", wout_mark[m % 2])
                abm = ab[m % 2]
                op("vector", lambda e, m=m, t0=t0, abm=abm: e.tensor_scalar(
                    abm[:], yrep[:, t0:t0 + WP],
                    s_lng[:, m:m + 1], s_lnb[:, m:m + 1],
                    ALU.mult, ALU.add))
                ab_mark[m % 2] = op(
                    "vector", lambda e, abm=abm, szm=szm: e.tensor_tensor(
                        abm[:], abm[:], szm[:], ALU.mult), "s_ve")

            def emit_wout(m):
                abm = ab[m % 2]
                wait("tensor", "s_ve", ab_mark[m % 2])
                for tg in range(2):
                    for dh in range(2):
                        wout_mark[m % 2] = op(
                            "tensor", lambda e, m=m, tg=tg, dh=dh, abm=abm:
                            e.matmul(
                                pb[4 + tg * 2 + dh][:, 0:512],
                                abm[:, tg * 128:(tg + 1) * 128],
                                w_out_s[:, m * D + dh * 512:
                                        m * D + (dh + 1) * 512],
                                start=(m == 0), stop=(m == KC_I - 1)),
                            "s_mm")

            for m in range(KC_I):
                emit_z(m)
                if m >= 1:
                    emit_wout(m - 1)
            emit_wout(KC_I - 1)
            mm_fin = cnt["s_mm"]
            xn_done[ti % 2] = mm_fin
            wait("vector", "s_mm", mm_fin)
            wait("vector", "s_sp", ld_aux)
            for dh in range(2):
                op("vector", lambda e, dh=dh, xtb=xtb: e.tensor_tensor(
                    ob[:, dh * 512:(dh + 1) * 512],
                    pb[4 + dh][:, 0:512],
                    xtb[:, dh * 512:(dh + 1) * 512], ALU.add))
                op("vector", lambda e, dh=dh, ftb=ftb: e.tensor_tensor(
                    ob[:, dh * 512:(dh + 1) * 512],
                    ob[:, dh * 512:(dh + 1) * 512],
                    ftb[:, dh * 512:(dh + 1) * 512], ALU.add),
                    "s_ve" if dh == 1 else None)
            ve1 = cnt["s_ve"]
            wait("gpsimd", "s_ve", ve1)
            dma("gpsimd", out[t0:t0 + 128, :], ob[:, 0:D], "s_sp")
            dma("gpsimd", xtb[:], x_tm[t0 + 128:t0 + 256, :], "s_sp")
            dma("gpsimd", ftb[:], ffn_sp[t0 + 128:t0 + 256, :], "s_sp")
            ld_aux2 = snap("s_sp")
            wait("vector", "s_sp", ld_aux2)
            for dh in range(2):
                op("vector", lambda e, dh=dh, xtb=xtb: e.tensor_tensor(
                    ob[:, D + dh * 512:D + (dh + 1) * 512],
                    pb[6 + dh][:, 0:512],
                    xtb[:, dh * 512:(dh + 1) * 512], ALU.add))
                op("vector", lambda e, dh=dh, ftb=ftb: e.tensor_tensor(
                    ob[:, D + dh * 512:D + (dh + 1) * 512],
                    ob[:, D + dh * 512:D + (dh + 1) * 512],
                    ftb[:, dh * 512:(dh + 1) * 512], ALU.add),
                    "s_ve" if dh == 1 else None)
            ve2 = cnt["s_ve"]
            aux_done[ti % 2] = ve2
            wait("gpsimd", "s_ve", ve2)
            dma("gpsimd", out[t0 + 128:t0 + 256, :], ob[:, D:2 * D], "s_sp")
            wait("tensor", "s_ve", ve2)
        wait("gpsimd", "s_sp", snap("s_sp"))
        wait("sync", "s_sp", snap("s_sp"))

        cutname = {1: "p2", 2: "aii", 3: "p3"}.get(phases)
        envcut = os.environ.get("ATH_CUT")
        if envcut:
            cutname = envcut
        if cutname is not None:
            for e in plan:
                plan[e] = plan[e][:marks[cutname][e]]

        @block.sync
        def _(eng):
            for fn in plan["sync"]:
                fn(eng)

        @block.tensor
        def _(eng):
            for fn in plan["tensor"]:
                fn(eng)

        @block.vector
        def _(eng):
            for fn in plan["vector"]:
                fn(eng)

        @block.scalar
        def _(eng):
            for fn in plan["scalar"]:
                fn(eng)

        @block.gpsimd
        def _(eng):
            for fn in plan["gpsimd"]:
                fn(eng)

    return nc


def _off(nc, handle):
    return nc.lookup_mloc(handle).addr


def _host_inputs(inputs):
    x = np.ascontiguousarray(np.asarray(inputs["L0_attn_x"], np.float32))
    w_in = np.ascontiguousarray(np.asarray(inputs["L0_attn_W_in"], np.float32))
    w_conv = np.asarray(inputs["L0_attn_W_conv"], np.float32)
    w_xproj = np.asarray(inputs["L0_attn_W_xproj"], np.float32)
    w_dt = np.ascontiguousarray(np.asarray(inputs["L0_attn_W_dt"], np.float32))
    b_dt = np.asarray(inputs["L0_attn_b_dt"], np.float32)
    w_out = np.ascontiguousarray(np.asarray(inputs["L0_attn_W_out"], np.float32))
    ln_g = np.asarray(inputs["L0_attn_ln_g"], np.float32)
    ln_b = np.asarray(inputs["L0_attn_ln_b"], np.float32)
    gate_w = np.asarray(inputs["L0_ffn_gate_w"], np.float32)
    w1 = np.asarray(inputs["L0_ffn_w1"], np.float32)
    w3 = np.asarray(inputs["L0_ffn_w3"], np.float32)
    w2 = np.asarray(inputs["L0_ffn_w2"], np.float32)
    sw1 = np.asarray(inputs["L0_ffn_sw1"], np.float32)
    sw3 = np.asarray(inputs["L0_ffn_sw3"], np.float32)
    sw2 = np.asarray(inputs["L0_ffn_sw2"], np.float32)

    conv3 = w_conv.reshape(DI, 3)
    conv_c = np.ascontiguousarray(
        conv3.reshape(KC_I, 128, 3).transpose(1, 0, 2).reshape(128, KC_I * 3))
    wxp_h = np.ascontiguousarray(np.concatenate([
        w_xproj[:, 64:80], w_xproj[:, 80:96], w_xproj[:, 0:64],
        np.full((DI, 1), 1.0 / DI, np.float32)], axis=1))
    b_dt_c = np.ascontiguousarray(b_dt.reshape(KC_I, 128).T)
    ln_g_c = np.ascontiguousarray(ln_g.reshape(KC_I, 128).T)
    ln_b_c = np.ascontiguousarray(ln_b.reshape(KC_I, 128).T)
    gate_wT = np.ascontiguousarray(gate_w.T)
    w13 = np.ascontiguousarray(np.concatenate(
        [w1.transpose(0, 2, 1), w3.transpose(0, 2, 1)], axis=2))
    w2t = np.ascontiguousarray(w2.transpose(0, 2, 1))
    sw13 = np.ascontiguousarray(np.concatenate([sw1.T, sw3.T], axis=1))
    sw2t = np.ascontiguousarray(sw2.T)

    consts = np.zeros((128, 8), np.float32)
    consts[:, 0] = 1.0
    consts[:, 1] = EPS
    ident = np.eye(128, dtype=np.float32)
    iota16 = np.tile(np.arange(NE, dtype=np.float32), (128, 1))
    iotamb = np.ascontiguousarray(iota16 - BIG * 10)
    iotab = np.ascontiguousarray(np.tile(iota16, (1, 16)))
    iotambb = np.ascontiguousarray(iotab - BIG * 10)
    iota16 = np.ascontiguousarray(iota16)
    p16 = np.zeros((16, 128), np.float32)
    p16[np.arange(128) % 16, np.arange(128)] = 1.0
    sel16 = np.zeros((16, NE * 128), np.float32)
    for e in range(NE):
        sel16[e, e * 128:(e + 1) * 128] = 1.0
    ones_r = np.ones((1, 128), np.float32)

    shared = dict(w_in=w_in, conv_w=conv_c, wxp=wxp_h, w_dt=w_dt,
                  b_dt_c=b_dt_c, w_out=w_out, ln_g_c=ln_g_c, ln_b_c=ln_b_c,
                  gate_wT=gate_wT, w13=w13, w2t=w2t, sw13=sw13, sw2t=sw2t,
                  consts=consts, ident=ident, iota16=iota16, iotamb=iotamb,
                  iotab=iotab, iotambb=iotambb,
                  p16=p16, sel16=sel16, ones_r=ones_r)

    in_maps = []
    for c in range(8):
        seq, half = c // 2, c % 2
        xs = x[seq]
        x_fmc = np.zeros((D, TAL), np.float32)
        x_fmc[:, 2048:4096] = xs[half * 2048:(half + 1) * 2048].T
        if half == 1:
            x_fmc[:, 0:2048] = xs[0:2048].T
        else:
            x_fmc[:, 4096] = xs[2048]
        m = dict(shared)
        m["x_fm"] = np.ascontiguousarray(x_fmc)
        m["x_tm"] = np.ascontiguousarray(xs[half * 2048:(half + 1) * 2048])
        in_maps.append(m)
    return in_maps


def kernel(**inputs):
    in_maps = _host_inputs(inputs)
    nc = build_program()
    res = run_bass_kernel_spmd(nc, in_maps, core_ids=list(range(8)))
    outa = np.zeros((4, 4096, D), np.float32)
    for c in range(8):
        seq, half = c // 2, c % 2
        outa[seq, half * 2048:(half + 1) * 2048, :] = res.results[c]["out"]
    return outa


# version 10
# speedup vs baseline: 1.0593x; 1.0120x over previous
import contextlib
import os
import numpy as np
import concourse.bass as bass
import concourse.mybir as mybir
from concourse.bass_utils import run_bass_kernel_spmd

F32 = mybir.dt.float32
AF = mybir.ActivationFunctionType
ALU = mybir.AluOpType
AX = mybir.AxisListType

D = 1024
DI = 2048
FZ = 4096
HID = 512
NE = 16
TO = 2048
TA = 4096
TAL = 4097
WP = 256
NT2 = TA // WP
WA = 512
NTA = TA // WA
TH = 1024
EPS = 1e-5
BIG = 1.0e5
SET_LNEXP = 6
SET_SILU = 18
KC_D = D // 128
KC_I = DI // 128
WPP = WP + 2


def _load_act_set(nc, eng, idx):
    eng.add_instruction(mybir.InstLoadActFuncSet(
        name=nc.get_next_instruction_name(), act_func_set_id=idx,
        ins=[], outs=[]))


def build_program():
    phases = int(os.environ.get("ATH_PHASES", "4"))
    skips = set(os.environ.get("ATH_SKIP", "").split(","))
    nc = bass.Bass("TRN2", target_bir_lowering=False,
                   detect_race_conditions=False)

    def din(name, shape):
        return nc.dram_tensor(name, shape, F32, kind="ExternalInput").ap()

    x_fm = din("x_fm", [D, TAL])
    x_tm = din("x_tm", [TO, D])
    w_in = din("w_in", [D, FZ])
    conv_w = din("conv_w", [128, KC_I * 3])
    wxp = din("wxp", [DI, 97])
    w_dt = din("w_dt", [64, DI])
    b_dt_c = din("b_dt_c", [128, KC_I])
    w_out = din("w_out", [DI, D])
    ln_g_c = din("ln_g_c", [128, KC_I])
    ln_b_c = din("ln_b_c", [128, KC_I])
    gate_wT = din("gate_wT", [D, NE])
    w13 = din("w13", [NE, D, 1024])
    w2t = din("w2t", [NE, HID, D])
    sw13 = din("sw13", [D, 1024])
    sw2t = din("sw2t", [HID, D])
    consts = din("consts", [128, 8])
    ident = din("ident", [128, 128])
    iota16 = din("iota16", [128, NE])
    iotamb = din("iotamb", [128, NE])
    iotab = din("iotab", [128, NE * 16])
    iotambb = din("iotambb", [128, NE * 16])
    p16 = din("p16", [16, 128])
    sel16 = din("sel16", [16, NE * 128])
    ones_r = din("ones_r", [1, 128])

    dbg = os.environ.get("ATH_DBGOUT", "0") == "1"
    ikind = "ExternalOutput" if dbg else "Internal"
    xn_sp = nc.dram_tensor("xn_sp", [D, TO], F32, kind=ikind).ap()
    ffn_sp = nc.dram_tensor("ffn_sp", [TO, D], F32, kind=ikind).ap()
    dbg_t = (nc.dram_tensor("dbg_t", [128, 8192], F32, kind="ExternalOutput").ap()
             if dbg else None)
    out = nc.dram_tensor("out", [TO, D], F32, kind="ExternalOutput").ap()

    ctx = contextlib.ExitStack()
    with ctx:
        def sbuf(name, shape):
            return ctx.enter_context(nc.sbuf_tensor(name, shape, F32))

        s_consts = sbuf("s_consts", [128, 8])
        s_ident = sbuf("s_ident", [128, 128])
        s_iota = sbuf("s_iota", [128, NE])
        s_iotamb = sbuf("s_iotamb", [128, NE])
        s_p16 = sbuf("s_p16", [16, 128])
        s_ones_r = sbuf("s_ones_r", [1, 128])
        s_convw = sbuf("s_convw", [128, KC_I * 3])
        s_bdt = sbuf("s_bdt", [128, KC_I])
        s_lng = sbuf("s_lng", [128, KC_I])
        s_lnb = sbuf("s_lnb", [128, KC_I])

        SBTOP = nc.SBUF_PARTITION_SIZE_BYTES
        MBASE = 26624
        assert nc.sbuf_base <= MBASE, nc.sbuf_base
        cur = [MBASE]

        def at(name, shape, off=None):
            if off is None:
                off = cur[0]
                nbytes = int(np.prod(shape[1:])) * 4
                cur[0] = off + ((nbytes + 31) // 32) * 32
            h = nc.alloc_sbuf_tensor_at(name, shape, F32, offset=off)
            assert off + int(np.prod(shape[1:])) * 4 <= SBTOP, (name, off)
            return h

        yrep = at("yrep", [128, TO])
        comb_fm = at("comb_fm", [16, TO])
        y16 = at("y16", [16, WA])
        y16b = at("y16b", [16, WA])
        G16 = NE * 16
        PERM_END = cur[0]
        r_dec = at("r_dec", [1, TA])
        r_c = at("r_c", [1, TA])
        dt_lo = at("dt_lo", [128, TO])
        cp_buf = at("cp_buf", [16, TO])
        lg_fm = at("lg_fm", [16, TO])
        MID_END = cur[0]
        r_s = r_c
        w_in_h = at("w_in_h", [128, KC_D * DI])
        w_xp = at("w_xp", [128, KC_I * 97])
        w_gate = at("w_gate", [128, KC_D * NE])
        WPS = 512
        xt = [at(f"xt{i}", [128, KC_D * WPS]) for i in range(2)]
        xzb = at("xzb", [128, WPS])
        xsb = [at(f"xsb{i}", [128, WPS]) for i in range(2)]
        cacc = [at(f"cacc{i}", [128, WPS]) for i in range(2)]
        bpcp = cacc[0]
        r_mu = at("r_mu", [1, WPS])
        r_var = at("r_var", [1, WPS])
        r_inv = at("r_inv", [1, WPS])
        r_xm = at("r_xm", [1, WPS])
        assert cur[0] <= SBTOP, cur[0]
        r_bps = at("r_bps", [1, WPS], off=_off(nc, xzb))
        dtacc = at("dtacc", [128, WA], off=_off(nc, xsb[0]))
        w_dt_s = at("w_dt_s", [128, DI], off=_off(nc, xt[0]))
        r_dtm = at("r_dtm", [1, WA], off=_off(nc, xt[0]) + 8192)
        xt0o = _off(nc, xt[0]) + 10240
        tkl = at("tkl", [128, G16], off=xt0o)
        tkcur = at("tkcur", [128, G16], off=xt0o + 1024)
        tkeq = at("tkeq", [128, G16], off=xt0o + 2048)
        tkcand = at("tkcand", [128, G16], off=xt0o + 3072)
        tksel = at("tksel", [128, G16], off=xt0o + 4096)
        tkw = at("tkw", [128, G16], off=xt0o + 5120)
        xt1o = _off(nc, xt[1]) + 12288
        tkm = at("tkm", [128, 16], off=xt1o)
        s_iotab = at("s_iotab", [128, G16], off=xt1o + 64)
        s_iotambb = at("s_iotambb", [128, G16], off=xt1o + 1088)
        dtscr = [at(f"dtscr{i}", [128, WA], off=_off(nc, xt[1]) + i * 2048)
                 for i in range(2)]
        r_muy = at("r_muy", [1, TO], off=_off(nc, xt[1]) + 4096)
        r_invy = at("r_invy", [1, TO], off=_off(nc, w_in_h))
        r_tmp = at("r_tmp", [1, TO], off=_off(nc, w_in_h) + 8192)
        p3 = [PERM_END]

        def at3(name, shape):
            off = p3[0]
            nbytes = int(np.prod(shape[1:])) * 4
            p3[0] = off + ((nbytes + 31) // 32) * 32
            h = nc.alloc_sbuf_tensor_at(name, shape, F32, offset=off)
            assert off + nbytes <= SBTOP, (name, off)
            return h

        xn_h = at3("xn_h", [128, KC_D * TH])
        acc = at3("acc", [128, 8 * D])
        w13_s = [at3(f"w13_s{i}", [128, KC_D * 1024]) for i in range(2)]
        w2_s = at3("w2_s", [128, 4 * D])
        act_b = at3("act_b", [128, 4 * TH])
        silu_b = at3("silu_b", [128, 512])
        h3c_b = at3("h3c_b", [128, 512])
        cbc_b = at3("cbc_b", [128, TH])
        s_sel = at3("s_sel", [16, NE * 128])
        assert p3[0] <= SBTOP, p3[0]
        p4 = [PERM_END]

        def at4(name, shape):
            off = p4[0]
            nbytes = int(np.prod(shape[1:])) * 4
            p4[0] = off + ((nbytes + 31) // 32) * 32
            h = nc.alloc_sbuf_tensor_at(name, shape, F32, offset=off)
            assert off + nbytes <= SBTOP, (name, off)
            return h

        w_in_h4 = at4("w_in_h4", [128, KC_D * DI])
        w_out_s = at4("w_out_s", [128, KC_I * D])
        xn_t = [at4(f"xn_t{i}", [128, KC_D * WP]) for i in range(2)]
        xtm_t = [at4(f"xtm_t{i}", [128, D]) for i in range(2)]
        ffn_t = [at4(f"ffn_t{i}", [128, D]) for i in range(2)]
        szb = [at4(f"szb{i}", [128, WP]) for i in range(2)]
        ab = [at4(f"ab{i}", [128, WP]) for i in range(2)]
        ob = at4("ob", [128, 2 * D])
        assert p4[0] <= SBTOP, p4[0]

        pb = [ctx.enter_context(nc.psum_tensor(f"pb{i}", [128, 512], F32))
              for i in range(8)]

        sems = {}
        for sname in ["s_x", "s_w", "s_sp", "s_mm", "s_ve", "s_se"]:
            sems[sname] = ctx.enter_context(nc.semaphore(sname))
        cnt = {k: 0 for k in sems}

        block = ctx.enter_context(nc.Block())
        plan = {e: [] for e in ("sync", "tensor", "vector", "scalar", "gpsimd")}
        marks = {}

        def mark(name):
            marks[name] = {e: len(plan[e]) for e in plan}

        def emit(eng, fn):
            plan[eng].append(fn)

        def dma(q, out_ap, in_ap, sem, slow=False):
            cnt[sem] += 16
            s = sems[sem]
            if slow:
                emit(q, lambda e, o=out_ap, i=in_ap, s=s:
                     e.dma_start(out=o, in_=i, allow_slow_non_contiguous=True)
                     .then_inc(s, 16))
            else:
                emit(q, lambda e, o=out_ap, i=in_ap, s=s:
                     e.dma_start(out=o, in_=i).then_inc(s, 16))
            return cnt[sem]

        def wait(eng, sem, val=None):
            v = cnt[sem] if val is None else val
            s = sems[sem]
            emit(eng, lambda e, s=s, v=v: e.wait_ge(s, v))

        def snap(sem):
            return cnt[sem]

        def op(eng, fn, sem=None):
            if sem is None:
                emit(eng, fn)
                return None
            cnt[sem] += 1
            s = sems[sem]
            emit(eng, lambda e, f=fn, s=s: f(e).then_inc(s, 1))
            return cnt[sem]

        def marker(eng, sem):
            cnt[sem] += 1
            s = sems[sem]
            emit(eng, lambda e, s=s: e.drain().then_inc(s, 1))
            return cnt[sem]

        ENG_SEM = {"tensor": "s_mm", "vector": "s_ve", "scalar": "s_se"}

        def barrier():
            for e in ("tensor", "vector", "scalar"):
                marker(e, ENG_SEM[e])
            tgt = {s: cnt[s] for s in ("s_mm", "s_ve", "s_se")}
            for e in ("tensor", "vector", "scalar", "gpsimd"):
                for s in ("s_mm", "s_ve", "s_se"):
                    wait(e, s, tgt[s])
            return tgt

        for dst, src in [(s_consts, consts), (s_ident, ident),
                         (s_iota, iota16), (s_iotamb, iotamb),
                         (s_p16, p16), (s_ones_r, ones_r),
                         (s_convw, conv_w), (s_bdt, b_dt_c),
                         (s_lng, ln_g_c), (s_lnb, ln_b_c)]:
            dma("sync", dst[:], src, "s_x")
        consts_ld = snap("s_x")
        for k in range(KC_D):
            dma("sync", w_in_h[:, k * DI:(k + 1) * DI],
                w_in[k * 128:(k + 1) * 128, 0:DI], "s_w")
            dma("sync", w_gate[:, k * NE:(k + 1) * NE],
                gate_wT[k * 128:(k + 1) * 128, :], "s_w")
        for k in range(KC_I):
            dma("sync", w_xp[:, k * 97:(k + 1) * 97],
                wxp[k * 128:(k + 1) * 128, :], "s_w")
        w_p2_ld = snap("s_w")
        mark("pre")

        ONE = s_consts[:, 0:1]
        EPS_AP = s_consts[0:1, 1:2]

        wait("tensor", "s_x", consts_ld)
        wait("vector", "s_x", consts_ld)
        wait("scalar", "s_x", consts_ld)
        wait("tensor", "s_w", w_p2_ld)
        xt_done = [0, 0]
        xt_sp = [0, 0]
        sq_done = [0, 0]
        cp_dma_prev = 0
        widths = [410, 410, 410, 410, 408]
        p2_tiles = []
        for h in range(2):
            t = h * TO
            for wdt in widths:
                p2_tiles.append((t, wdt))
                t += wdt
        for ti, (t0, W) in enumerate(p2_tiles):
            own = t0 >= TO
            lo = max(t0 - 1, 0)
            hi = t0 + W + 1
            wlo = t0 - lo
            off = 1 - wlo
            wwin = hi - lo
            buf = xt[ti % 2]
            if ti >= 2:
                wait("sync", "s_ve", xt_done[ti % 2])
                wait("sync", "s_sp", xt_sp[ti % 2])
            for k in range(KC_D):
                dma("sync", buf[:, k * WPS:k * WPS + wwin],
                    x_fm[k * 128:(k + 1) * 128, lo:hi], "s_x")
            ld = snap("s_x")
            wait("tensor", "s_x", ld)
            wait("scalar", "s_x", ld)
            for k in range(KC_D):
                op("tensor", lambda e, k=k, buf=buf, wwin=wwin:
                   e.matmul(pb[0][0:1, 0:wwin], ONE,
                            buf[:, k * WPS:k * WPS + wwin],
                            start=(k == 0), stop=(k == KC_D - 1)))
            for k in range(KC_D):
                sqb = cacc[k % 2]
                if k >= 2:
                    wait("scalar", "s_mm", sq_done[k % 2])
                se_id = op("scalar",
                           lambda e, k=k, buf=buf, wwin=wwin, sqb=sqb:
                           e.activation(sqb[:, 0:wwin],
                                        buf[:, k * WPS:k * WPS + wwin],
                                        AF.Square), "s_se")
                wait("tensor", "s_se", se_id)
                sq_done[k % 2] = op(
                    "tensor", lambda e, k=k, wwin=wwin, sqb=sqb:
                    e.matmul(pb[1][0:1, 0:wwin], ONE, sqb[:, 0:wwin],
                             start=(k == 0), stop=(k == KC_D - 1)), "s_mm")
            wait("vector", "s_mm", snap("s_mm"))
            op("vector", lambda e, wwin=wwin:
               e.tensor_scalar(r_mu[:, 0:wwin], pb[0][0:1, 0:wwin],
                               1.0 / D, None, ALU.mult))
            op("vector", lambda e, wwin=wwin:
               e.tensor_scalar(r_var[:, 0:wwin], pb[1][0:1, 0:wwin],
                               1.0 / D, None, ALU.mult))
            op("vector", lambda e, wwin=wwin: e.tensor_tensor(
                r_inv[:, 0:wwin], r_mu[:, 0:wwin], r_mu[:, 0:wwin],
                ALU.mult))
            ve_id = op("vector", lambda e, wwin=wwin: e.tensor_tensor(
                r_var[:, 0:wwin], r_var[:, 0:wwin], r_inv[:, 0:wwin],
                ALU.subtract), "s_ve")
            wait("scalar", "s_ve", ve_id)
            emit("scalar", lambda e: _load_act_set(nc, e, SET_LNEXP))
            op("scalar", lambda e, wwin=wwin: e.activation(
                r_inv[:, 0:wwin], r_var[:, 0:wwin], AF.Ln, bias=EPS_AP))
            se_id = op("scalar", lambda e, wwin=wwin: e.activation(
                r_inv[:, 0:wwin], r_inv[:, 0:wwin], AF.Exp, scale=-0.5),
                "s_se")
            emit("scalar", lambda e: _load_act_set(nc, e, SET_SILU))
            wait("tensor", "s_se", se_id)
            op("tensor", lambda e, wwin=wwin:
               e.matmul(pb[4][:, 0:wwin], s_ones_r[0:1, :],
                        r_mu[:, 0:wwin], start=True, stop=True))
            op("tensor", lambda e, wwin=wwin:
               e.matmul(pb[5][:, 0:wwin], s_ones_r[0:1, :],
                        r_inv[:, 0:wwin], start=True, stop=True), "s_mm")
            wait("vector", "s_mm", snap("s_mm"))
            for k in range(KC_D):
                op("vector", lambda e, k=k, buf=buf, wwin=wwin:
                   e.tensor_tensor(buf[:, k * WPS:k * WPS + wwin],
                                   buf[:, k * WPS:k * WPS + wwin],
                                   pb[4][:, 0:wwin], ALU.subtract))
                op("vector", lambda e, k=k, buf=buf, wwin=wwin:
                   e.tensor_tensor(buf[:, k * WPS:k * WPS + wwin],
                                   buf[:, k * WPS:k * WPS + wwin],
                                   pb[5][:, 0:wwin], ALU.mult),
                   "s_ve" if k == KC_D - 1 else None)
            xn_ready = cnt["s_ve"]
            if own:
                wait("gpsimd", "s_ve", xn_ready)
                for k in range(KC_D):
                    dma("gpsimd",
                        xn_sp[k * 128:(k + 1) * 128, t0 - TO:t0 - TO + W],
                        buf[:, k * WPS + wlo:k * WPS + wlo + W], "s_sp")
            wait("tensor", "s_ve", xn_ready)
            xz_evac = [0, 0, 0, 0]
            silu_mark = [0, 0]
            xp_mark = [0, 0]

            def emit_xz(m):
                bank = pb[m % 4]
                if m >= 4:
                    wait("tensor", "s_ve", xz_evac[m % 4])
                for k in range(KC_D):
                    op("tensor",
                       lambda e, m=m, k=k, bank=bank, wwin=wwin, buf=buf:
                       e.matmul(bank[:, 0:wwin],
                                w_in_h[:, k * DI + m * 128:k * DI + (m + 1) * 128],
                                buf[:, k * WPS:k * WPS + wwin],
                                start=(k == 0), stop=(k == KC_D - 1)),
                       "s_mm" if k == KC_D - 1 else None)
                wait("vector", "s_mm", snap("s_mm"))
                if ti == 0:
                    op("vector", lambda e: e.tensor_scalar(
                        xzb[:, 0:1], s_consts[:, 0:1], 0.0, None, ALU.mult))
                xz_evac[m % 4] = op(
                    "vector", lambda e, off=off, wwin=wwin, bank=bank:
                    e.tensor_copy(xzb[:, off:off + wwin], bank[:, 0:wwin]),
                    "s_ve")
                cb = cacc[m % 2]
                xb2 = xsb[m % 2]
                if m >= 2:
                    wait("vector", "s_se", silu_mark[m % 2])
                op("vector", lambda e, m=m, cb=cb, W=W: e.tensor_scalar(
                    cb[:, 0:W], xzb[:, 0:W],
                    s_convw[:, m * 3:m * 3 + 1], None, ALU.mult))
                op("vector", lambda e, m=m, cb=cb, W=W: e.scalar_tensor_tensor(
                    cb[:, 0:W], xzb[:, 1:1 + W],
                    s_convw[:, m * 3 + 1:m * 3 + 2], cb[:, 0:W],
                    ALU.mult, ALU.add))
                ve_id = op("vector", lambda e, m=m, cb=cb, W=W:
                           e.scalar_tensor_tensor(
                               cb[:, 0:W], xzb[:, 2:2 + W],
                               s_convw[:, m * 3 + 2:m * 3 + 3], cb[:, 0:W],
                               ALU.mult, ALU.add), "s_ve")
                wait("scalar", "s_ve", ve_id)
                if m >= 2:
                    wait("scalar", "s_mm", xp_mark[m % 2])
                silu_mark[m % 2] = op(
                    "scalar", lambda e, m=m, cb=cb, xb2=xb2, W=W:
                    e.activation(xb2[:, 0:W], cb[:, 0:W], AF.Silu), "s_se")

            def emit_xproj(m):
                xb2 = xsb[m % 2]
                wait("tensor", "s_se", silu_mark[m % 2])
                xp_mark[m % 2] = op(
                    "tensor", lambda e, m=m, xb2=xb2, W=W: e.matmul(
                        pb[6][0:97, 0:W], w_xp[:, m * 97:(m + 1) * 97],
                        xb2[:, 0:W], start=(m == 0), stop=(m == KC_I - 1)),
                    "s_mm")

            for m in range(KC_I):
                emit_xz(m)
                if m >= 1:
                    emit_xproj(m - 1)
            emit_xproj(KC_I - 1)
            wait("vector", "s_mm", snap("s_mm"))
            if own:
                wait("vector", "s_sp", cp_dma_prev)
            hlf = 64 * (t0 // TO)
            tc = t0 % TO
            op("vector", lambda e, W=W: e.tensor_copy(
                bpcp[0:32, 0:W], pb[6][0:32, 0:W]))
            op("vector", lambda e, hlf=hlf, tc=tc, W=W: e.tensor_copy(
                dt_lo[hlf:hlf + 32, tc:tc + W], pb[6][32:64, 0:W]))
            op("vector", lambda e, hlf=hlf, tc=tc, W=W: e.tensor_copy(
                dt_lo[hlf + 32:hlf + 64, tc:tc + W], pb[6][64:96, 0:W]))
            ve_id = op("vector", lambda e, W=W: e.tensor_copy(
                r_xm[:, 0:W], pb[6][96:97, 0:W]), "s_ve")
            wait("tensor", "s_ve", ve_id)
            mm_id = op("tensor", lambda e, W=W: e.matmul(
                pb[7][0:1, 0:W], s_consts[0:16, 0:1], bpcp[0:16, 0:W],
                start=True, stop=True), "s_mm")
            wait("vector", "s_mm", mm_id)
            op("vector", lambda e, W=W: e.tensor_copy(
                r_bps[:, 0:W], pb[7][0:1, 0:W]))
            op("vector", lambda e, t0=t0, W=W: e.tensor_tensor(
                r_c[:, t0:t0 + W], r_xm[:, 0:W], r_bps[:, 0:W],
                ALU.mult), "s_ve")
            if own:
                wait("gpsimd", "s_ve", snap("s_ve"))
                cp_dma_prev = dma("gpsimd", cp_buf[:, tc:tc + W],
                                  bpcp[16:32, 0:W], "s_sp")
                for k in range(KC_D):
                    op("tensor", lambda e, k=k, wlo=wlo, W=W, buf=buf:
                       e.matmul(
                           pb[4][0:16, 0:W],
                           w_gate[:, k * NE:(k + 1) * NE],
                           buf[:, k * WPS + wlo:k * WPS + wlo + W],
                           start=(k == 0), stop=(k == KC_D - 1)),
                       "s_mm" if k == KC_D - 1 else None)
                wait("vector", "s_mm", snap("s_mm"))
                op("vector", lambda e, tc=tc, W=W: e.tensor_copy(
                    lg_fm[:, tc:tc + W], pb[4][0:16, 0:W]), "s_ve")
            bt = barrier()
            xt_done[ti % 2] = bt["s_ve"]
            xt_sp[ti % 2] = snap("s_sp")
            wait("vector", "s_sp", snap("s_sp"))
            wait("scalar", "s_sp", snap("s_sp"))
            mark(f"t{ti}")
        p2_end = barrier()
        sp_after_p2 = snap("s_sp")
        mark("p2")
        emit("scalar", lambda e: _load_act_set(nc, e, SET_LNEXP))
        wait("vector", "s_sp", sp_after_p2)
        wait("sync", "s_mm", p2_end["s_mm"])
        wait("sync", "s_sp", sp_after_p2)
        dma("sync", w_dt_s[0:64, :], w_dt, "s_w")
        dma("sync", w_dt_s[64:128, :], w_dt, "s_w")
        dma("sync", s_iotab[:], iotab, "s_w")
        dma("sync", s_iotambb[:], iotambb, "s_w")
        w_dt_ld = snap("s_w")
        wait("tensor", "s_w", w_dt_ld)
        for ti in range(NTA):
            t0 = ti * WA
            hlf = 64 * (t0 // TO)
            tc = t0 % TO
            se_done = [0, 0]
            add_done = [0, 0]

            def emit_dt(m):
                bank = pb[m % 2]
                scr = dtscr[m % 2]
                if m >= 2:
                    wait("tensor", "s_ve", add_done[m % 2])
                op("tensor", lambda e, m=m, hlf=hlf, tc=tc, bank=bank:
                   e.matmul(bank[:, 0:WA],
                            w_dt_s[hlf:hlf + 64, m * 128:(m + 1) * 128],
                            dt_lo[hlf:hlf + 64, tc:tc + WA],
                            start=True, stop=True), "s_mm")
                wait("scalar", "s_mm", snap("s_mm"))
                if m >= 2:
                    wait("scalar", "s_ve", add_done[m % 2])
                op("scalar", lambda e, m=m, bank=bank, scr=scr: e.activation(
                    scr[:], bank[:, 0:WA], AF.Exp, bias=s_bdt[:, m:m + 1]))
                op("scalar", lambda e, scr=scr: e.activation(
                    scr[:], scr[:], AF.Ln, bias=1.0))
                se_done[m % 2] = marker("scalar", "s_se")
                wait("vector", "s_se", se_done[m % 2])
                if m == 0:
                    add_done[0] = op("vector", lambda e, scr=scr:
                                     e.tensor_copy(dtacc[:], scr[:]), "s_ve")
                else:
                    add_done[m % 2] = op(
                        "vector", lambda e, scr=scr: e.tensor_tensor(
                            dtacc[:], dtacc[:], scr[:], ALU.add), "s_ve")

            for m in range(KC_I):
                emit_dt(m)
            wait("tensor", "s_ve", snap("s_ve"))
            op("tensor", lambda e: e.matmul(
                pb[4][0:1, 0:WA], ONE, dtacc[:], start=True, stop=True),
                "s_mm")
            wait("vector", "s_mm", snap("s_mm"))
            ve_id = op("vector", lambda e: e.tensor_scalar(
                r_dtm[:, 0:WA], pb[4][0:1, 0:WA],
                1.0 / DI, None, ALU.mult), "s_ve")
            wait("scalar", "s_ve", ve_id)
            se_id = op("scalar", lambda e, t0=t0: e.activation(
                r_dec[:, t0:t0 + WA], r_dtm[:, 0:WA], AF.Exp, scale=-1.0),
                "s_se")
            wait("tensor", "s_se", se_id)
        wait("vector", "s_se", snap("s_se"))
        op("vector", lambda e: e.tensor_tensor_scan(
            r_s[:], r_dec[:], r_c[:], 0.0, ALU.mult, ALU.add), "s_ve")
        for ti in range(TO // WA):
            t0 = ti * WA
            wait("tensor", "s_ve", snap("s_ve"))
            op("tensor", lambda e, t0=t0: e.matmul(
                pb[0][0:16, 0:WA], s_ones_r[0:1, 0:16],
                r_s[:, TO + t0:TO + t0 + WA], start=True, stop=True), "s_mm")
            wait("vector", "s_mm", snap("s_mm"))
            op("vector", lambda e, t0=t0: e.tensor_tensor(
                y16[:], cp_buf[:, t0:t0 + WA], pb[0][0:16, 0:WA], ALU.mult),
                "s_ve")
            wait("scalar", "s_ve", snap("s_ve"))
            se_id = op("scalar", lambda e: e.activation(
                y16b[:], y16[:], AF.Square), "s_se")
            wait("tensor", "s_se", se_id)
            op("tensor", lambda e: e.matmul(
                pb[1][0:1, 0:WA], s_consts[0:16, 0:1], y16[:],
                start=True, stop=True))
            op("tensor", lambda e: e.matmul(
                pb[2][0:1, 0:WA], s_consts[0:16, 0:1], y16b[:],
                start=True, stop=True), "s_mm")
            wait("vector", "s_mm", snap("s_mm"))
            op("vector", lambda e, t0=t0: e.tensor_scalar(
                r_muy[:, t0:t0 + WA], pb[1][0:1, 0:WA], 1.0 / 16, None,
                ALU.mult))
            op("vector", lambda e, t0=t0: e.tensor_scalar(
                r_invy[:, t0:t0 + WA], pb[2][0:1, 0:WA], 1.0 / 16, None,
                ALU.mult))
            op("vector", lambda e, t0=t0: e.tensor_tensor(
                r_tmp[:, t0:t0 + WA], r_muy[:, t0:t0 + WA],
                r_muy[:, t0:t0 + WA], ALU.mult))
            ve_id = op("vector", lambda e, t0=t0: e.tensor_tensor(
                r_invy[:, t0:t0 + WA], r_invy[:, t0:t0 + WA],
                r_tmp[:, t0:t0 + WA], ALU.subtract), "s_ve")
            wait("scalar", "s_ve", ve_id)
            op("scalar", lambda e, t0=t0: e.activation(
                r_invy[:, t0:t0 + WA], r_invy[:, t0:t0 + WA],
                AF.Ln, bias=EPS_AP))
            se_id = op("scalar", lambda e, t0=t0: e.activation(
                r_invy[:, t0:t0 + WA], r_invy[:, t0:t0 + WA],
                AF.Exp, scale=-0.5), "s_se")
            wait("tensor", "s_se", se_id)
            op("tensor", lambda e, t0=t0: e.matmul(
                pb[1][0:16, 0:WA], s_ones_r[0:1, 0:16],
                r_muy[:, t0:t0 + WA], start=True, stop=True))
            op("tensor", lambda e, t0=t0: e.matmul(
                pb[2][0:16, 0:WA], s_ones_r[0:1, 0:16],
                r_invy[:, t0:t0 + WA], start=True, stop=True), "s_mm")
            wait("vector", "s_mm", snap("s_mm"))
            op("vector", lambda e: e.tensor_tensor(
                y16[:], y16[:], pb[1][0:16, 0:WA], ALU.subtract))
            ve_id = op("vector", lambda e: e.tensor_tensor(
                y16[:], y16[:], pb[2][0:16, 0:WA], ALU.mult), "s_ve")
            wait("tensor", "s_ve", ve_id)
            op("tensor", lambda e: e.matmul(
                pb[3][:, 0:WA], s_p16[:], y16[:], start=True, stop=True),
                "s_mm")
            wait("vector", "s_mm", snap("s_mm"))
            op("vector", lambda e, t0=t0: e.tensor_copy(
                yrep[:, t0:t0 + WA], pb[3][:, 0:WA]), "s_ve")
            wait("tensor", "s_ve", snap("s_ve"))
        G = 16

        def vsync():
            v = marker("vector", "s_ve")
            wait("vector", "s_ve", v)

        def r3(ap):
            return ap.rearrange("p (g e) -> p g e", g=G)

        def bc(ap):
            return ap[:, 0:G].unsqueeze(2).broadcast_to([128, G, 16])

        wait("vector", "s_w", snap("s_w"))
        tp_cp = [0, 0]
        for g in range(G):
            c0 = g * 128
            if g >= 2:
                wait("tensor", "s_ve", tp_cp[g % 2])
            op("tensor", lambda e, g=g, c0=c0: e.transpose(
                pb[g % 2][:, 0:16], lg_fm[:, c0:c0 + 128],
                s_ident[0:16, 0:16]), "s_mm")
            wait("vector", "s_mm", snap("s_mm"))
            tp_cp[g % 2] = op("vector", lambda e, g=g: e.tensor_copy(
                tkl[:, g * 16:(g + 1) * 16], pb[g % 2][:, 0:16]), "s_ve")
        vsync()
        op("vector", lambda e: e.tensor_reduce(
            tkm[:, 0:G].unsqueeze(2), r3(tkl[:, :]), AX.X, ALU.max))
        vsync()
        op("vector", lambda e: e.tensor_tensor(
            r3(tkl[:, :]), r3(tkl[:, :]), bc(tkm), ALU.subtract))
        op("vector", lambda e: e.tensor_copy(tkcur[:], tkl[:]))
        for r in range(4):
            op("vector", lambda e: e.tensor_reduce(
                tkm[:, 0:G].unsqueeze(2), r3(tkcur[:, :]), AX.X, ALU.max))
            vsync()
            op("vector", lambda e: e.tensor_tensor(
                r3(tkeq[:, :]), r3(tkcur[:, :]), bc(tkm), ALU.is_ge))
            op("vector", lambda e: e.tensor_tensor(
                tkcand[:], tkeq[:], s_iotambb[:], ALU.mult))
            op("vector", lambda e: e.tensor_scalar(
                tkcand[:], tkcand[:], BIG * 10, None, ALU.add))
            op("vector", lambda e: e.tensor_reduce(
                tkm[:, 0:G].unsqueeze(2), r3(tkcand[:, :]), AX.X, ALU.min))
            vsync()
            op("vector", lambda e: e.tensor_tensor(
                r3(tkeq[:, :]), r3(s_iotab[:, :]), bc(tkm), ALU.is_equal))
            if r == 0:
                op("vector", lambda e: e.tensor_copy(tksel[:], tkeq[:]))
            else:
                op("vector", lambda e: e.tensor_tensor(
                    tksel[:], tksel[:], tkeq[:], ALU.add))
            op("vector", lambda e: e.scalar_tensor_tensor(
                tkcur[:], tkeq[:], -BIG, tkcur[:], ALU.mult, ALU.add))
        ve_id = marker("vector", "s_ve")
        wait("scalar", "s_ve", ve_id)
        se_id = op("scalar", lambda e: e.activation(
            tkw[:], tkl[:], AF.Exp), "s_se")
        wait("vector", "s_se", se_id)
        op("vector", lambda e: e.tensor_tensor(
            tkw[:], tkw[:], tksel[:], ALU.mult))
        op("vector", lambda e: e.tensor_reduce(
            tkm[:, 0:G].unsqueeze(2), r3(tkw[:, :]), AX.X, ALU.add))
        vsync()
        op("vector", lambda e: e.reciprocal(tkm[:, 0:G], tkm[:, 0:G]))
        vsync()
        ve_id = op("vector", lambda e: e.tensor_tensor(
            r3(tkw[:, :]), r3(tkw[:, :]), bc(tkm), ALU.mult), "s_ve")
        wait("tensor", "s_ve", ve_id)
        tp_cp = [0, 0]
        for g in range(G):
            c0 = g * 128
            if g >= 2:
                wait("tensor", "s_ve", tp_cp[g % 2])
            op("tensor", lambda e, g=g: e.transpose(
                pb[g % 2][0:16, 0:128], tkw[:, g * 16:(g + 1) * 16],
                s_ident[:]), "s_mm")
            wait("vector", "s_mm", snap("s_mm"))
            tp_cp[g % 2] = op("vector", lambda e, g=g, c0=c0: e.tensor_copy(
                comb_fm[:, c0:c0 + 128], pb[g % 2][0:16, 0:128]), "s_ve")
        wait("tensor", "s_ve", snap("s_ve"))
        aii_end = barrier()
        if dbg:
            for s in ("s_mm", "s_ve", "s_se"):
                wait("gpsimd", s, aii_end[s])
            dma("gpsimd", dbg_t[0:1, 0:TA], r_dec[:], "s_sp")
            dma("gpsimd", dbg_t[1:2, 0:TA], r_s[:], "s_sp")
            dma("gpsimd", dbg_t[16:32, 0:TO], cp_buf[:], "s_sp")
            dma("gpsimd", dbg_t[32:48, 0:TO], comb_fm[:], "s_sp")
            dma("gpsimd", dbg_t[48:64, 0:TO], lg_fm[:], "s_sp")
            dma("gpsimd", dbg_t[0:128, TO * 2:TO * 3], dt_lo[:], "s_sp")
            dma("gpsimd", dbg_t[0:128, TO * 3:TO * 4], yrep[:], "s_sp")
            dma("gpsimd", dbg_t[0:128, 100:116], tkl[:], "s_sp")
            dma("gpsimd", dbg_t[0:128, 116:132], tkw[:], "s_sp")
            dma("gpsimd", dbg_t[0:128, 132:148], tksel[:], "s_sp")
            dma("gpsimd", dbg_t[0:128, 148:152], tkm[:, 0:4], "s_sp")
            wait("sync", "s_sp", snap("s_sp"))
        mark("aii")

        emit("scalar", lambda e: _load_act_set(nc, e, SET_SILU))
        wait("sync", "s_mm", aii_end["s_mm"])
        wait("sync", "s_ve", aii_end["s_ve"])
        wait("sync", "s_sp", sp_after_p2)
        dma("sync", s_sel[:], sel16, "s_w")

        exps = [NE] + list(range(NE))
        w13_free = [0, 0]
        w2_free = 0
        last_act = 0
        prev_eo = None
        first = True
        for hf in range(2):
            h0 = hf * TH
            if hf > 0:
                wait("sync", "s_mm", snap("s_mm"))
            for k in range(KC_D):
                dma("sync", xn_h[:, k * TH:(k + 1) * TH],
                    xn_sp[k * 128:(k + 1) * 128, h0:h0 + TH], "s_x")
            xn_ld = snap("s_x")
            wait("tensor", "s_x", xn_ld)
            for ei, ex in enumerate(exps):
                wb = w13_s[ei % 2]
                if not first or ei >= 2:
                    wait("sync", "s_ve", w13_free[ei % 2])
                src = sw13 if ex == NE else w13[ex]
                for k in range(KC_D):
                    dma("sync", wb[:, k * 1024:(k + 1) * 1024],
                        src[k * 128:(k + 1) * 128, :], "s_w")
                wait("sync", "s_mm", w2_free)
                src2 = sw2t if ex == NE else w2t[ex]
                for k in range(4):
                    dma("sync", w2_s[:, k * D:(k + 1) * D],
                        src2[k * 128:(k + 1) * 128, :], "s_w")
                w_ld = snap("s_w")
                if ex != NE:
                    if prev_eo is not None:
                        wait("tensor", "s_ve", prev_eo[1])
                    for tt in range(2):
                        op("tensor", lambda e, ex=ex, h0=h0, tt=tt: e.matmul(
                            pb[4 + tt][:, 0:512],
                            s_sel[:, ex * 128:(ex + 1) * 128],
                            comb_fm[:, h0 + tt * 512:h0 + (tt + 1) * 512],
                            start=True, stop=True),
                            "s_mm" if tt == 1 else None)
                    wait("vector", "s_mm", snap("s_mm"))
                    for tt in range(2):
                        op("vector", lambda e, tt=tt: e.tensor_copy(
                            cbc_b[:, tt * 512:(tt + 1) * 512],
                            pb[4 + tt][:, 0:512]),
                            "s_ve" if tt == 1 else None)
                cbc_ready = cnt["s_ve"]
                wait("tensor", "s_w", w_ld)
                h_free = [0, 0]
                for hh in range(4):
                    for tt in range(2):
                        bank_a = pb[(hh * 2 + tt) % 2]
                        bank_b = pb[2 + (hh * 2 + tt) % 2]
                        if hh >= 1:
                            wait("tensor", "s_ve", h_free[tt])
                        for k in range(KC_D):
                            op("tensor",
                               lambda e, k=k, hh=hh, tt=tt, bank_a=bank_a,
                               wb=wb: e.matmul(
                                   bank_a[:, 0:512],
                                   wb[:, k * 1024 + hh * 128:
                                      k * 1024 + (hh + 1) * 128],
                                   xn_h[:, k * TH + tt * 512:
                                        k * TH + (tt + 1) * 512],
                                   start=(k == 0), stop=(k == KC_D - 1)))
                        for k in range(KC_D):
                            op("tensor",
                               lambda e, k=k, hh=hh, tt=tt, bank_b=bank_b,
                               wb=wb: e.matmul(
                                   bank_b[:, 0:512],
                                   wb[:, k * 1024 + 512 + hh * 128:
                                      k * 1024 + 512 + (hh + 1) * 128],
                                   xn_h[:, k * TH + tt * 512:
                                        k * TH + (tt + 1) * 512],
                                   start=(k == 0), stop=(k == KC_D - 1)),
                               "s_mm" if k == KC_D - 1 else None)
                        mm_done = cnt["s_mm"]
                        wait("scalar", "s_mm", mm_done)
                        if last_act:
                            wait("scalar", "s_ve", last_act)
                        se_id = op("scalar",
                                   lambda e, bank_a=bank_a: e.activation(
                                       silu_b[:], bank_a[:, 0:512], AF.Silu),
                                   "s_se")
                        wait("vector", "s_mm", mm_done)
                        wait("vector", "s_ve", cbc_ready)
                        if ex != NE:
                            op("vector", lambda e, tt=tt, bank_b=bank_b:
                               e.tensor_tensor(
                                   h3c_b[:], bank_b[:, 0:512],
                                   cbc_b[:, tt * 512:(tt + 1) * 512],
                                   ALU.mult))
                        else:
                            op("vector", lambda e, bank_b=bank_b:
                               e.tensor_copy(h3c_b[:], bank_b[:, 0:512]))
                        wait("vector", "s_se", se_id)
                        last_act = op(
                            "vector", lambda e, hh=hh, tt=tt: e.tensor_tensor(
                                act_b[:, hh * TH + tt * 512:
                                      hh * TH + (tt + 1) * 512],
                                silu_b[:], h3c_b[:], ALU.mult), "s_ve")
                        h_free[tt] = last_act
                act_ready = cnt["s_ve"]
                wait("tensor", "s_ve", act_ready)
                eo_free = [0, 0, 0, 0]
                for tg in range(8):
                    for dh in range(2):
                        bank = pb[4 + (tg * 2 + dh) % 4]
                        if tg * 2 + dh >= 4:
                            wait("tensor", "s_ve", eo_free[(tg * 2 + dh) % 4])
                        elif tg * 2 + dh < 2 and ex != NE:
                            wait("tensor", "s_ve", cbc_ready)
                        for k in range(4):
                            op("tensor",
                               lambda e, k=k, tg=tg, dh=dh, bank=bank:
                               e.matmul(
                                   bank[:, 0:512],
                                   act_b[:, k * TH + tg * 128:
                                         k * TH + (tg + 1) * 128],
                                   w2_s[:, k * D + dh * 512:
                                        k * D + (dh + 1) * 512],
                                   start=(k == 0), stop=(k == 3)),
                               "s_mm" if k == 3 else None)
                        wait("vector", "s_mm", snap("s_mm"))
                        if ei == 0:
                            eo_free[(tg * 2 + dh) % 4] = op(
                                "vector", lambda e, tg=tg, dh=dh, bank=bank:
                                e.tensor_copy(
                                    acc[:, tg * D + dh * 512:
                                        tg * D + (dh + 1) * 512],
                                    bank[:, 0:512]), "s_ve")
                        else:
                            eo_free[(tg * 2 + dh) % 4] = op(
                                "vector", lambda e, tg=tg, dh=dh, bank=bank:
                                e.tensor_tensor(
                                    acc[:, tg * D + dh * 512:
                                        tg * D + (dh + 1) * 512],
                                    acc[:, tg * D + dh * 512:
                                        tg * D + (dh + 1) * 512],
                                    bank[:, 0:512], ALU.add), "s_ve")
                w2_free = cnt["s_mm"]
                w13_free[ei % 2] = cnt["s_ve"]
                prev_eo = list(eo_free)
                first = False
            wait("gpsimd", "s_ve", snap("s_ve"))
            for tg in range(8):
                dma("gpsimd", ffn_sp[h0 + tg * 128:h0 + (tg + 1) * 128, :],
                    acc[:, tg * D:(tg + 1) * D], "s_sp")
            wait("vector", "s_sp", snap("s_sp"))
        p3_end = barrier()
        mark("p3")

        wait("sync", "s_mm", p3_end["s_mm"])
        wait("sync", "s_ve", p3_end["s_ve"])
        for k in range(KC_D):
            dma("sync", w_in_h4[:, k * DI:(k + 1) * DI],
                w_in[k * 128:(k + 1) * 128, DI:FZ], "s_w")
        for k in range(KC_I):
            dma("sync", w_out_s[:, k * D:(k + 1) * D],
                w_out[k * 128:(k + 1) * 128, :], "s_w")
        w_p4_ld = snap("s_w")
        wait("tensor", "s_w", w_p4_ld)
        xn_done = [0, 0]
        aux_done = [0, 0]
        for ti in range(TO // WP):
            t0 = ti * WP
            xb = xn_t[ti % 2]
            xtb = xtm_t[ti % 2]
            ftb = ffn_t[ti % 2]
            if ti >= 2:
                wait("sync", "s_mm", xn_done[ti % 2])
                wait("gpsimd", "s_ve", aux_done[ti % 2])
            for k in range(KC_D):
                dma("sync", xb[:, k * WP:(k + 1) * WP],
                    xn_sp[k * 128:(k + 1) * 128, t0:t0 + WP], "s_x")
            dma("gpsimd", xtb[:], x_tm[t0:t0 + 128, :], "s_sp")
            dma("gpsimd", ftb[:], ffn_sp[t0:t0 + 128, :], "s_sp")
            ld_x = snap("s_x")
            ld_aux = snap("s_sp")
            wait("tensor", "s_x", ld_x)
            z_free = [0, 0]
            ab_mark = [0, 0]
            wout_mark = [0, 0]

            def emit_z(m):
                bank = pb[m % 2]
                szm = szb[m % 2]
                if m >= 2:
                    wait("tensor", "s_se", z_free[m % 2])
                for k in range(KC_D):
                    op("tensor", lambda e, m=m, k=k, bank=bank, xb=xb:
                       e.matmul(
                           bank[:, 0:WP],
                           w_in_h4[:, k * DI + m * 128:k * DI + (m + 1) * 128],
                           xb[:, k * WP:(k + 1) * WP],
                           start=(k == 0), stop=(k == KC_D - 1)),
                       "s_mm" if k == KC_D - 1 else None)
                wait("scalar", "s_mm", snap("s_mm"))
                if m >= 2:
                    wait("scalar", "s_ve", ab_mark[m % 2])
                z_free[m % 2] = op(
                    "scalar", lambda e, bank=bank, szm=szm: e.activation(
                        szm[:], bank[:, 0:WP], AF.Silu), "s_se")
                wait("vector", "s_se", z_free[m % 2])
                if m >= 2:
                    wait("vector", "s_mm", wout_mark[m % 2])
                abm = ab[m % 2]
                op("vector", lambda e, m=m, t0=t0, abm=abm: e.tensor_scalar(
                    abm[:], yrep[:, t0:t0 + WP],
                    s_lng[:, m:m + 1], s_lnb[:, m:m + 1],
                    ALU.mult, ALU.add))
                ab_mark[m % 2] = op(
                    "vector", lambda e, abm=abm, szm=szm: e.tensor_tensor(
                        abm[:], abm[:], szm[:], ALU.mult), "s_ve")

            def emit_wout(m):
                abm = ab[m % 2]
                wait("tensor", "s_ve", ab_mark[m % 2])
                for tg in range(2):
                    for dh in range(2):
                        wout_mark[m % 2] = op(
                            "tensor", lambda e, m=m, tg=tg, dh=dh, abm=abm:
                            e.matmul(
                                pb[4 + tg * 2 + dh][:, 0:512],
                                abm[:, tg * 128:(tg + 1) * 128],
                                w_out_s[:, m * D + dh * 512:
                                        m * D + (dh + 1) * 512],
                                start=(m == 0), stop=(m == KC_I - 1)),
                            "s_mm")

            for m in range(KC_I):
                emit_z(m)
                if m >= 1:
                    emit_wout(m - 1)
            emit_wout(KC_I - 1)
            mm_fin = cnt["s_mm"]
            xn_done[ti % 2] = mm_fin
            wait("vector", "s_mm", mm_fin)
            wait("vector", "s_sp", ld_aux)
            for dh in range(2):
                op("vector", lambda e, dh=dh, xtb=xtb: e.tensor_tensor(
                    ob[:, dh * 512:(dh + 1) * 512],
                    pb[4 + dh][:, 0:512],
                    xtb[:, dh * 512:(dh + 1) * 512], ALU.add))
                op("vector", lambda e, dh=dh, ftb=ftb: e.tensor_tensor(
                    ob[:, dh * 512:(dh + 1) * 512],
                    ob[:, dh * 512:(dh + 1) * 512],
                    ftb[:, dh * 512:(dh + 1) * 512], ALU.add),
                    "s_ve" if dh == 1 else None)
            ve1 = cnt["s_ve"]
            wait("gpsimd", "s_ve", ve1)
            dma("gpsimd", out[t0:t0 + 128, :], ob[:, 0:D], "s_sp")
            dma("gpsimd", xtb[:], x_tm[t0 + 128:t0 + 256, :], "s_sp")
            dma("gpsimd", ftb[:], ffn_sp[t0 + 128:t0 + 256, :], "s_sp")
            ld_aux2 = snap("s_sp")
            wait("vector", "s_sp", ld_aux2)
            for dh in range(2):
                op("vector", lambda e, dh=dh, xtb=xtb: e.tensor_tensor(
                    ob[:, D + dh * 512:D + (dh + 1) * 512],
                    pb[6 + dh][:, 0:512],
                    xtb[:, dh * 512:(dh + 1) * 512], ALU.add))
                op("vector", lambda e, dh=dh, ftb=ftb: e.tensor_tensor(
                    ob[:, D + dh * 512:D + (dh + 1) * 512],
                    ob[:, D + dh * 512:D + (dh + 1) * 512],
                    ftb[:, dh * 512:(dh + 1) * 512], ALU.add),
                    "s_ve" if dh == 1 else None)
            ve2 = cnt["s_ve"]
            aux_done[ti % 2] = ve2
            wait("gpsimd", "s_ve", ve2)
            dma("gpsimd", out[t0 + 128:t0 + 256, :], ob[:, D:2 * D], "s_sp")
            wait("tensor", "s_ve", ve2)
        wait("gpsimd", "s_sp", snap("s_sp"))
        wait("sync", "s_sp", snap("s_sp"))

        cutname = {1: "p2", 2: "aii", 3: "p3"}.get(phases)
        envcut = os.environ.get("ATH_CUT")
        if envcut:
            cutname = envcut
        if cutname is not None:
            for e in plan:
                plan[e] = plan[e][:marks[cutname][e]]

        @block.sync
        def _(eng):
            for fn in plan["sync"]:
                fn(eng)

        @block.tensor
        def _(eng):
            for fn in plan["tensor"]:
                fn(eng)

        @block.vector
        def _(eng):
            for fn in plan["vector"]:
                fn(eng)

        @block.scalar
        def _(eng):
            for fn in plan["scalar"]:
                fn(eng)

        @block.gpsimd
        def _(eng):
            for fn in plan["gpsimd"]:
                fn(eng)

    return nc


def _off(nc, handle):
    return nc.lookup_mloc(handle).addr


def _host_inputs(inputs):
    x = np.ascontiguousarray(np.asarray(inputs["L0_attn_x"], np.float32))
    w_in = np.ascontiguousarray(np.asarray(inputs["L0_attn_W_in"], np.float32))
    w_conv = np.asarray(inputs["L0_attn_W_conv"], np.float32)
    w_xproj = np.asarray(inputs["L0_attn_W_xproj"], np.float32)
    w_dt = np.ascontiguousarray(np.asarray(inputs["L0_attn_W_dt"], np.float32))
    b_dt = np.asarray(inputs["L0_attn_b_dt"], np.float32)
    w_out = np.ascontiguousarray(np.asarray(inputs["L0_attn_W_out"], np.float32))
    ln_g = np.asarray(inputs["L0_attn_ln_g"], np.float32)
    ln_b = np.asarray(inputs["L0_attn_ln_b"], np.float32)
    gate_w = np.asarray(inputs["L0_ffn_gate_w"], np.float32)
    w1 = np.asarray(inputs["L0_ffn_w1"], np.float32)
    w3 = np.asarray(inputs["L0_ffn_w3"], np.float32)
    w2 = np.asarray(inputs["L0_ffn_w2"], np.float32)
    sw1 = np.asarray(inputs["L0_ffn_sw1"], np.float32)
    sw3 = np.asarray(inputs["L0_ffn_sw3"], np.float32)
    sw2 = np.asarray(inputs["L0_ffn_sw2"], np.float32)

    conv3 = w_conv.reshape(DI, 3)
    conv_c = np.ascontiguousarray(
        conv3.reshape(KC_I, 128, 3).transpose(1, 0, 2).reshape(128, KC_I * 3))
    wxp_h = np.ascontiguousarray(np.concatenate([
        w_xproj[:, 64:80], w_xproj[:, 80:96], w_xproj[:, 0:64],
        np.full((DI, 1), 1.0 / DI, np.float32)], axis=1))
    b_dt_c = np.ascontiguousarray(b_dt.reshape(KC_I, 128).T)
    ln_g_c = np.ascontiguousarray(ln_g.reshape(KC_I, 128).T)
    ln_b_c = np.ascontiguousarray(ln_b.reshape(KC_I, 128).T)
    gate_wT = np.ascontiguousarray(gate_w.T)
    w13 = np.ascontiguousarray(np.concatenate(
        [w1.transpose(0, 2, 1), w3.transpose(0, 2, 1)], axis=2))
    w2t = np.ascontiguousarray(w2.transpose(0, 2, 1))
    sw13 = np.ascontiguousarray(np.concatenate([sw1.T, sw3.T], axis=1))
    sw2t = np.ascontiguousarray(sw2.T)

    consts = np.zeros((128, 8), np.float32)
    consts[:, 0] = 1.0
    consts[:, 1] = EPS
    ident = np.eye(128, dtype=np.float32)
    iota16 = np.tile(np.arange(NE, dtype=np.float32), (128, 1))
    iotamb = np.ascontiguousarray(iota16 - BIG * 10)
    iotab = np.ascontiguousarray(np.tile(iota16, (1, 16)))
    iotambb = np.ascontiguousarray(iotab - BIG * 10)
    iota16 = np.ascontiguousarray(iota16)
    p16 = np.zeros((16, 128), np.float32)
    p16[np.arange(128) % 16, np.arange(128)] = 1.0
    sel16 = np.zeros((16, NE * 128), np.float32)
    for e in range(NE):
        sel16[e, e * 128:(e + 1) * 128] = 1.0
    ones_r = np.ones((1, 128), np.float32)

    shared = dict(w_in=w_in, conv_w=conv_c, wxp=wxp_h, w_dt=w_dt,
                  b_dt_c=b_dt_c, w_out=w_out, ln_g_c=ln_g_c, ln_b_c=ln_b_c,
                  gate_wT=gate_wT, w13=w13, w2t=w2t, sw13=sw13, sw2t=sw2t,
                  consts=consts, ident=ident, iota16=iota16, iotamb=iotamb,
                  iotab=iotab, iotambb=iotambb,
                  p16=p16, sel16=sel16, ones_r=ones_r)

    in_maps = []
    for c in range(8):
        seq, half = c // 2, c % 2
        xs = x[seq]
        x_fmc = np.zeros((D, TAL), np.float32)
        x_fmc[:, 2048:4096] = xs[half * 2048:(half + 1) * 2048].T
        if half == 1:
            x_fmc[:, 0:2048] = xs[0:2048].T
        else:
            x_fmc[:, 4096] = xs[2048]
        m = dict(shared)
        m["x_fm"] = np.ascontiguousarray(x_fmc)
        m["x_tm"] = np.ascontiguousarray(xs[half * 2048:(half + 1) * 2048])
        in_maps.append(m)
    return in_maps


def kernel(**inputs):
    in_maps = _host_inputs(inputs)
    nc = build_program()
    res = run_bass_kernel_spmd(nc, in_maps, core_ids=list(range(8)))
    outa = np.zeros((4, 4096, D), np.float32)
    for c in range(8):
        seq, half = c // 2, c % 2
        outa[seq, half * 2048:(half + 1) * 2048, :] = res.results[c]["out"]
    return outa


# version 11
# speedup vs baseline: 1.0830x; 1.0224x over previous
import contextlib
import os
import numpy as np
import concourse.bass as bass
import concourse.mybir as mybir
from concourse.bass_utils import run_bass_kernel_spmd

F32 = mybir.dt.float32
AF = mybir.ActivationFunctionType
ALU = mybir.AluOpType
AX = mybir.AxisListType

D = 1024
DI = 2048
FZ = 4096
HID = 512
NE = 16
TO = 2048
TA = 4096
TAL = 4097
WP = 256
NT2 = TA // WP
WA = 512
NTA = TA // WA
TH = 1024
EPS = 1e-5
BIG = 1.0e5
SET_LNEXP = 6
SET_SILU = 18
KC_D = D // 128
KC_I = DI // 128
WPP = WP + 2


def _load_act_set(nc, eng, idx):
    eng.add_instruction(mybir.InstLoadActFuncSet(
        name=nc.get_next_instruction_name(), act_func_set_id=idx,
        ins=[], outs=[]))


def build_program():
    phases = int(os.environ.get("ATH_PHASES", "4"))
    skips = set(os.environ.get("ATH_SKIP", "").split(","))
    nc = bass.Bass("TRN2", target_bir_lowering=False,
                   detect_race_conditions=False)

    def din(name, shape):
        return nc.dram_tensor(name, shape, F32, kind="ExternalInput").ap()

    x_fm = din("x_fm", [D, TAL])
    x_tm = din("x_tm", [TO, D])
    w_in = din("w_in", [D, FZ])
    conv_w = din("conv_w", [128, KC_I * 3])
    wxp = din("wxp", [DI, 97])
    w_dt = din("w_dt", [64, DI])
    b_dt_c = din("b_dt_c", [128, KC_I])
    w_out = din("w_out", [DI, D])
    ln_g_c = din("ln_g_c", [128, KC_I])
    ln_b_c = din("ln_b_c", [128, KC_I])
    gate_wT = din("gate_wT", [D, NE])
    w13 = din("w13", [NE, D, 1024])
    w2t = din("w2t", [NE, HID, D])
    sw13 = din("sw13", [D, 1024])
    sw2t = din("sw2t", [HID, D])
    consts = din("consts", [128, 8])
    ident = din("ident", [128, 128])
    iota16 = din("iota16", [128, NE])
    iotamb = din("iotamb", [128, NE])
    iotab = din("iotab", [128, NE * 16])
    iotambb = din("iotambb", [128, NE * 16])
    p16 = din("p16", [16, 128])
    sel16 = din("sel16", [16, NE * 128])
    ones_r = din("ones_r", [1, 128])

    dbg = os.environ.get("ATH_DBGOUT", "0") == "1"
    ikind = "ExternalOutput" if dbg else "Internal"
    xn_sp = nc.dram_tensor("xn_sp", [D, TO], F32, kind=ikind).ap()
    ffn_sp = nc.dram_tensor("ffn_sp", [TO, D], F32, kind=ikind).ap()
    dbg_t = (nc.dram_tensor("dbg_t", [128, 8192], F32, kind="ExternalOutput").ap()
             if dbg else None)
    out = nc.dram_tensor("out", [TO, D], F32, kind="ExternalOutput").ap()

    ctx = contextlib.ExitStack()
    with ctx:
        def sbuf(name, shape):
            return ctx.enter_context(nc.sbuf_tensor(name, shape, F32))

        s_consts = sbuf("s_consts", [128, 8])
        s_ident = sbuf("s_ident", [128, 128])
        s_iota = sbuf("s_iota", [128, NE])
        s_iotamb = sbuf("s_iotamb", [128, NE])
        s_p16 = sbuf("s_p16", [16, 128])
        s_ones_r = sbuf("s_ones_r", [1, 128])
        s_convw = sbuf("s_convw", [128, KC_I * 3])
        s_bdt = sbuf("s_bdt", [128, KC_I])
        s_lng = sbuf("s_lng", [128, KC_I])
        s_lnb = sbuf("s_lnb", [128, KC_I])

        SBTOP = nc.SBUF_PARTITION_SIZE_BYTES
        MBASE = 26624
        assert nc.sbuf_base <= MBASE, nc.sbuf_base
        cur = [MBASE]

        def at(name, shape, off=None):
            if off is None:
                off = cur[0]
                nbytes = int(np.prod(shape[1:])) * 4
                cur[0] = off + ((nbytes + 31) // 32) * 32
            h = nc.alloc_sbuf_tensor_at(name, shape, F32, offset=off)
            assert off + int(np.prod(shape[1:])) * 4 <= SBTOP, (name, off)
            return h

        yrep = at("yrep", [128, TO])
        comb_fm = at("comb_fm", [16, TO])
        y16 = at("y16", [16, WA])
        y16b = at("y16b", [16, WA])
        G16 = NE * 16
        PERM_END = cur[0]
        r_dec = at("r_dec", [1, TA])
        r_c = at("r_c", [1, TA])
        dt_lo = at("dt_lo", [128, TO])
        cp_buf = at("cp_buf", [16, TO])
        lg_fm = at("lg_fm", [16, TO])
        MID_END = cur[0]
        r_s = r_c
        w_in_h = at("w_in_h", [128, KC_D * DI])
        w_xp = at("w_xp", [128, KC_I * 97])
        w_gate = at("w_gate", [128, KC_D * NE])
        WPS = 512
        xt = [at(f"xt{i}", [128, KC_D * WPS]) for i in range(2)]
        xzb = at("xzb", [128, WPS])
        xsb = [at(f"xsb{i}", [128, WPS]) for i in range(2)]
        cacc = [at(f"cacc{i}", [128, WPS]) for i in range(2)]
        bpcp = cacc[0]
        r_mu = at("r_mu", [1, WPS])
        r_var = at("r_var", [1, WPS])
        r_inv = at("r_inv", [1, WPS])
        r_xm = at("r_xm", [1, WPS])
        assert cur[0] <= SBTOP, cur[0]
        r_bps = at("r_bps", [1, WPS], off=_off(nc, xzb))
        dtacc = at("dtacc", [128, WA], off=_off(nc, xsb[0]))
        w_dt_s = at("w_dt_s", [128, DI], off=_off(nc, xt[0]))
        r_dtm = at("r_dtm", [1, WA], off=_off(nc, xt[0]) + 8192)
        xt0o = _off(nc, xt[0]) + 10240
        tkl = at("tkl", [128, G16], off=xt0o)
        tkcur = at("tkcur", [128, G16], off=xt0o + 1024)
        tkeq = at("tkeq", [128, G16], off=xt0o + 2048)
        tkcand = at("tkcand", [128, G16], off=xt0o + 3072)
        tksel = at("tksel", [128, G16], off=xt0o + 4096)
        tkw = at("tkw", [128, G16], off=xt0o + 5120)
        xt1o = _off(nc, xt[1]) + 12288
        tkm = at("tkm", [128, 16], off=xt1o)
        s_iotab = at("s_iotab", [128, G16], off=xt1o + 64)
        s_iotambb = at("s_iotambb", [128, G16], off=xt1o + 1088)
        dtscr = [at(f"dtscr{i}", [128, WA], off=_off(nc, xt[1]) + i * 2048)
                 for i in range(2)]
        r_muy = at("r_muy", [1, TO], off=_off(nc, xt[1]) + 4096)
        r_invy = at("r_invy", [1, TO], off=_off(nc, w_in_h))
        r_tmp = at("r_tmp", [1, TO], off=_off(nc, w_in_h) + 8192)
        p3 = [PERM_END]

        def at3(name, shape):
            off = p3[0]
            nbytes = int(np.prod(shape[1:])) * 4
            p3[0] = off + ((nbytes + 31) // 32) * 32
            h = nc.alloc_sbuf_tensor_at(name, shape, F32, offset=off)
            assert off + nbytes <= SBTOP, (name, off)
            return h

        xn_h = at3("xn_h", [128, KC_D * TH])
        acc = at3("acc", [128, 8 * D])
        w13_s = [at3(f"w13_s{i}", [128, KC_D * 1024]) for i in range(2)]
        w2_s = at3("w2_s", [128, 4 * D])
        act_b = at3("act_b", [128, 4 * TH])
        silu_b = at3("silu_b", [128, 512])
        h3c_b = at3("h3c_b", [128, 512])
        cbc_b = at3("cbc_b", [128, TH])
        s_sel = at3("s_sel", [16, NE * 128])
        assert p3[0] <= SBTOP, p3[0]
        p4 = [PERM_END]

        def at4(name, shape):
            off = p4[0]
            nbytes = int(np.prod(shape[1:])) * 4
            p4[0] = off + ((nbytes + 31) // 32) * 32
            h = nc.alloc_sbuf_tensor_at(name, shape, F32, offset=off)
            assert off + nbytes <= SBTOP, (name, off)
            return h

        w_in_h4 = at4("w_in_h4", [128, KC_D * DI])
        w_out_s = at4("w_out_s", [128, KC_I * D])
        xn_t = [at4(f"xn_t{i}", [128, KC_D * WP]) for i in range(2)]
        xtm_t = [at4(f"xtm_t{i}", [128, D]) for i in range(2)]
        ffn_t = [at4(f"ffn_t{i}", [128, D]) for i in range(2)]
        szb = [at4(f"szb{i}", [128, WP]) for i in range(2)]
        ab = [at4(f"ab{i}", [128, WP]) for i in range(2)]
        ob = at4("ob", [128, 2 * D])
        assert p4[0] <= SBTOP, p4[0]

        pb = [ctx.enter_context(nc.psum_tensor(f"pb{i}", [128, 512], F32))
              for i in range(8)]

        sems = {}
        for sname in ["s_x", "s_w", "s_sp", "s_mm", "s_ve", "s_se"]:
            sems[sname] = ctx.enter_context(nc.semaphore(sname))
        cnt = {k: 0 for k in sems}

        block = ctx.enter_context(nc.Block())
        plan = {e: [] for e in ("sync", "tensor", "vector", "scalar", "gpsimd")}
        marks = {}

        def mark(name):
            marks[name] = {e: len(plan[e]) for e in plan}

        def emit(eng, fn):
            plan[eng].append(fn)

        def dma(q, out_ap, in_ap, sem, slow=False):
            cnt[sem] += 16
            s = sems[sem]
            if slow:
                emit(q, lambda e, o=out_ap, i=in_ap, s=s:
                     e.dma_start(out=o, in_=i, allow_slow_non_contiguous=True)
                     .then_inc(s, 16))
            else:
                emit(q, lambda e, o=out_ap, i=in_ap, s=s:
                     e.dma_start(out=o, in_=i).then_inc(s, 16))
            return cnt[sem]

        def wait(eng, sem, val=None):
            v = cnt[sem] if val is None else val
            s = sems[sem]
            emit(eng, lambda e, s=s, v=v: e.wait_ge(s, v))

        def snap(sem):
            return cnt[sem]

        def op(eng, fn, sem=None):
            if sem is None:
                emit(eng, fn)
                return None
            cnt[sem] += 1
            s = sems[sem]
            emit(eng, lambda e, f=fn, s=s: f(e).then_inc(s, 1))
            return cnt[sem]

        def marker(eng, sem):
            cnt[sem] += 1
            s = sems[sem]
            emit(eng, lambda e, s=s: e.drain().then_inc(s, 1))
            return cnt[sem]

        ENG_SEM = {"tensor": "s_mm", "vector": "s_ve", "scalar": "s_se"}

        def barrier():
            for e in ("tensor", "vector", "scalar"):
                marker(e, ENG_SEM[e])
            tgt = {s: cnt[s] for s in ("s_mm", "s_ve", "s_se")}
            for e in ("tensor", "vector", "scalar", "gpsimd"):
                for s in ("s_mm", "s_ve", "s_se"):
                    wait(e, s, tgt[s])
            return tgt

        for dst, src in [(s_consts, consts), (s_ident, ident),
                         (s_iota, iota16), (s_iotamb, iotamb),
                         (s_p16, p16), (s_ones_r, ones_r),
                         (s_convw, conv_w), (s_bdt, b_dt_c),
                         (s_lng, ln_g_c), (s_lnb, ln_b_c)]:
            dma("sync", dst[:], src, "s_x")
        consts_ld = snap("s_x")
        for k in range(KC_D):
            dma("sync", w_in_h[:, k * DI:(k + 1) * DI],
                w_in[k * 128:(k + 1) * 128, 0:DI], "s_w")
            dma("sync", w_gate[:, k * NE:(k + 1) * NE],
                gate_wT[k * 128:(k + 1) * 128, :], "s_w")
        for k in range(KC_I):
            dma("sync", w_xp[:, k * 97:(k + 1) * 97],
                wxp[k * 128:(k + 1) * 128, :], "s_w")
        w_p2_ld = snap("s_w")
        mark("pre")

        ONE = s_consts[:, 0:1]
        EPS_AP = s_consts[0:1, 1:2]

        wait("tensor", "s_x", consts_ld)
        wait("vector", "s_x", consts_ld)
        wait("scalar", "s_x", consts_ld)
        wait("tensor", "s_w", w_p2_ld)
        xt_done = [0, 0]
        xt_sp = [0, 0]
        sq_done = [0, 0]
        cp_dma_prev = 0
        widths = [410, 410, 410, 410, 408]
        p2_tiles = []
        for h in range(2):
            t = h * TO
            for wdt in widths:
                p2_tiles.append((t, wdt))
                t += wdt
        for ti, (t0, W) in enumerate(p2_tiles):
            own = t0 >= TO
            lo = max(t0 - 1, 0)
            hi = t0 + W + 1
            wlo = t0 - lo
            off = 1 - wlo
            wwin = hi - lo
            buf = xt[ti % 2]
            if ti >= 2:
                wait("sync", "s_ve", xt_done[ti % 2])
                wait("sync", "s_sp", xt_sp[ti % 2])
            for k in range(KC_D):
                dma("sync", buf[:, k * WPS:k * WPS + wwin],
                    x_fm[k * 128:(k + 1) * 128, lo:hi], "s_x")
            ld = snap("s_x")
            wait("tensor", "s_x", ld)
            wait("scalar", "s_x", ld)
            for k in range(KC_D):
                op("tensor", lambda e, k=k, buf=buf, wwin=wwin:
                   e.matmul(pb[0][0:1, 0:wwin], ONE,
                            buf[:, k * WPS:k * WPS + wwin],
                            start=(k == 0), stop=(k == KC_D - 1)))
            for k in range(KC_D):
                sqb = cacc[k % 2]
                if k >= 2:
                    wait("scalar", "s_mm", sq_done[k % 2])
                se_id = op("scalar",
                           lambda e, k=k, buf=buf, wwin=wwin, sqb=sqb:
                           e.activation(sqb[:, 0:wwin],
                                        buf[:, k * WPS:k * WPS + wwin],
                                        AF.Square), "s_se")
                wait("tensor", "s_se", se_id)
                sq_done[k % 2] = op(
                    "tensor", lambda e, k=k, wwin=wwin, sqb=sqb:
                    e.matmul(pb[1][0:1, 0:wwin], ONE, sqb[:, 0:wwin],
                             start=(k == 0), stop=(k == KC_D - 1)), "s_mm")
            wait("vector", "s_mm", snap("s_mm"))
            op("vector", lambda e, wwin=wwin:
               e.tensor_scalar(r_mu[:, 0:wwin], pb[0][0:1, 0:wwin],
                               1.0 / D, None, ALU.mult))
            op("vector", lambda e, wwin=wwin:
               e.tensor_scalar(r_var[:, 0:wwin], pb[1][0:1, 0:wwin],
                               1.0 / D, None, ALU.mult))
            op("vector", lambda e, wwin=wwin: e.tensor_tensor(
                r_inv[:, 0:wwin], r_mu[:, 0:wwin], r_mu[:, 0:wwin],
                ALU.mult))
            ve_id = op("vector", lambda e, wwin=wwin: e.tensor_tensor(
                r_var[:, 0:wwin], r_var[:, 0:wwin], r_inv[:, 0:wwin],
                ALU.subtract), "s_ve")
            wait("scalar", "s_ve", ve_id)
            emit("scalar", lambda e: _load_act_set(nc, e, SET_LNEXP))
            op("scalar", lambda e, wwin=wwin: e.activation(
                r_inv[:, 0:wwin], r_var[:, 0:wwin], AF.Ln, bias=EPS_AP))
            se_id = op("scalar", lambda e, wwin=wwin: e.activation(
                r_inv[:, 0:wwin], r_inv[:, 0:wwin], AF.Exp, scale=-0.5),
                "s_se")
            emit("scalar", lambda e: _load_act_set(nc, e, SET_SILU))
            wait("tensor", "s_se", se_id)
            op("tensor", lambda e, wwin=wwin:
               e.matmul(pb[4][:, 0:wwin], s_ones_r[0:1, :],
                        r_mu[:, 0:wwin], start=True, stop=True))
            op("tensor", lambda e, wwin=wwin:
               e.matmul(pb[5][:, 0:wwin], s_ones_r[0:1, :],
                        r_inv[:, 0:wwin], start=True, stop=True), "s_mm")
            wait("vector", "s_mm", snap("s_mm"))
            for k in range(KC_D):
                op("vector", lambda e, k=k, buf=buf, wwin=wwin:
                   e.tensor_tensor(buf[:, k * WPS:k * WPS + wwin],
                                   buf[:, k * WPS:k * WPS + wwin],
                                   pb[4][:, 0:wwin], ALU.subtract))
                op("vector", lambda e, k=k, buf=buf, wwin=wwin:
                   e.tensor_tensor(buf[:, k * WPS:k * WPS + wwin],
                                   buf[:, k * WPS:k * WPS + wwin],
                                   pb[5][:, 0:wwin], ALU.mult),
                   "s_ve" if k == KC_D - 1 else None)
            xn_ready = cnt["s_ve"]
            if own:
                wait("gpsimd", "s_ve", xn_ready)
                for k in range(KC_D):
                    dma("gpsimd",
                        xn_sp[k * 128:(k + 1) * 128, t0 - TO:t0 - TO + W],
                        buf[:, k * WPS + wlo:k * WPS + wlo + W], "s_sp")
            wait("tensor", "s_ve", xn_ready)
            xz_evac = [0, 0, 0, 0]
            silu_mark = [0, 0]
            xp_mark = [0, 0]

            def emit_xz(m):
                bank = pb[m % 4]
                if m >= 4:
                    wait("tensor", "s_ve", xz_evac[m % 4])
                for k in range(KC_D):
                    op("tensor",
                       lambda e, m=m, k=k, bank=bank, wwin=wwin, buf=buf:
                       e.matmul(bank[:, 0:wwin],
                                w_in_h[:, k * DI + m * 128:k * DI + (m + 1) * 128],
                                buf[:, k * WPS:k * WPS + wwin],
                                start=(k == 0), stop=(k == KC_D - 1)),
                       "s_mm" if k == KC_D - 1 else None)
                wait("vector", "s_mm", snap("s_mm"))
                if ti == 0:
                    op("vector", lambda e: e.tensor_scalar(
                        xzb[:, 0:1], s_consts[:, 0:1], 0.0, None, ALU.mult))
                xz_evac[m % 4] = op(
                    "vector", lambda e, off=off, wwin=wwin, bank=bank:
                    e.tensor_copy(xzb[:, off:off + wwin], bank[:, 0:wwin]),
                    "s_ve")
                cb = cacc[m % 2]
                xb2 = xsb[m % 2]
                if m >= 2:
                    wait("vector", "s_se", silu_mark[m % 2])
                op("vector", lambda e, m=m, cb=cb, W=W: e.tensor_scalar(
                    cb[:, 0:W], xzb[:, 0:W],
                    s_convw[:, m * 3:m * 3 + 1], None, ALU.mult))
                op("vector", lambda e, m=m, cb=cb, W=W: e.scalar_tensor_tensor(
                    cb[:, 0:W], xzb[:, 1:1 + W],
                    s_convw[:, m * 3 + 1:m * 3 + 2], cb[:, 0:W],
                    ALU.mult, ALU.add))
                ve_id = op("vector", lambda e, m=m, cb=cb, W=W:
                           e.scalar_tensor_tensor(
                               cb[:, 0:W], xzb[:, 2:2 + W],
                               s_convw[:, m * 3 + 2:m * 3 + 3], cb[:, 0:W],
                               ALU.mult, ALU.add), "s_ve")
                wait("scalar", "s_ve", ve_id)
                if m >= 2:
                    wait("scalar", "s_mm", xp_mark[m % 2])
                silu_mark[m % 2] = op(
                    "scalar", lambda e, m=m, cb=cb, xb2=xb2, W=W:
                    e.activation(xb2[:, 0:W], cb[:, 0:W], AF.Silu), "s_se")

            def emit_xproj(m):
                xb2 = xsb[m % 2]
                wait("tensor", "s_se", silu_mark[m % 2])
                xp_mark[m % 2] = op(
                    "tensor", lambda e, m=m, xb2=xb2, W=W: e.matmul(
                        pb[6][0:97, 0:W], w_xp[:, m * 97:(m + 1) * 97],
                        xb2[:, 0:W], start=(m == 0), stop=(m == KC_I - 1)),
                    "s_mm")

            for m in range(KC_I):
                emit_xz(m)
                if m >= 1:
                    emit_xproj(m - 1)
            emit_xproj(KC_I - 1)
            wait("vector", "s_mm", snap("s_mm"))
            if own:
                wait("vector", "s_sp", cp_dma_prev)
            hlf = 64 * (t0 // TO)
            tc = t0 % TO
            op("vector", lambda e, W=W: e.tensor_copy(
                bpcp[0:32, 0:W], pb[6][0:32, 0:W]))
            op("vector", lambda e, hlf=hlf, tc=tc, W=W: e.tensor_copy(
                dt_lo[hlf:hlf + 32, tc:tc + W], pb[6][32:64, 0:W]))
            op("vector", lambda e, hlf=hlf, tc=tc, W=W: e.tensor_copy(
                dt_lo[hlf + 32:hlf + 64, tc:tc + W], pb[6][64:96, 0:W]))
            ve_id = op("vector", lambda e, W=W: e.tensor_copy(
                r_xm[:, 0:W], pb[6][96:97, 0:W]), "s_ve")
            wait("tensor", "s_ve", ve_id)
            mm_id = op("tensor", lambda e, W=W: e.matmul(
                pb[7][0:1, 0:W], s_consts[0:16, 0:1], bpcp[0:16, 0:W],
                start=True, stop=True), "s_mm")
            wait("vector", "s_mm", mm_id)
            op("vector", lambda e, W=W: e.tensor_copy(
                r_bps[:, 0:W], pb[7][0:1, 0:W]))
            op("vector", lambda e, t0=t0, W=W: e.tensor_tensor(
                r_c[:, t0:t0 + W], r_xm[:, 0:W], r_bps[:, 0:W],
                ALU.mult), "s_ve")
            if own:
                wait("gpsimd", "s_ve", snap("s_ve"))
                cp_dma_prev = dma("gpsimd", cp_buf[:, tc:tc + W],
                                  bpcp[16:32, 0:W], "s_sp")
                for k in range(KC_D):
                    op("tensor", lambda e, k=k, wlo=wlo, W=W, buf=buf:
                       e.matmul(
                           pb[4][0:16, 0:W],
                           w_gate[:, k * NE:(k + 1) * NE],
                           buf[:, k * WPS + wlo:k * WPS + wlo + W],
                           start=(k == 0), stop=(k == KC_D - 1)),
                       "s_mm" if k == KC_D - 1 else None)
                wait("vector", "s_mm", snap("s_mm"))
                op("vector", lambda e, tc=tc, W=W: e.tensor_copy(
                    lg_fm[:, tc:tc + W], pb[4][0:16, 0:W]), "s_ve")
            bt = barrier()
            xt_done[ti % 2] = bt["s_ve"]
            xt_sp[ti % 2] = snap("s_sp")
            wait("vector", "s_sp", snap("s_sp"))
            wait("scalar", "s_sp", snap("s_sp"))
            mark(f"t{ti}")
        p2_end = barrier()
        sp_after_p2 = snap("s_sp")
        mark("p2")
        emit("scalar", lambda e: _load_act_set(nc, e, SET_LNEXP))
        wait("vector", "s_sp", sp_after_p2)
        wait("sync", "s_mm", p2_end["s_mm"])
        wait("sync", "s_sp", sp_after_p2)
        dma("sync", w_dt_s[0:64, :], w_dt, "s_w")
        dma("sync", w_dt_s[64:128, :], w_dt, "s_w")
        dma("sync", s_iotab[:], iotab, "s_w")
        dma("sync", s_iotambb[:], iotambb, "s_w")
        w_dt_ld = snap("s_w")
        wait("tensor", "s_w", w_dt_ld)
        for ti in range(NTA):
            t0 = ti * WA
            hlf = 64 * (t0 // TO)
            tc = t0 % TO
            se_done = [0, 0]
            add_done = [0, 0]

            def emit_dt(m):
                bank = pb[m % 2]
                scr = dtscr[m % 2]
                if m >= 2:
                    wait("tensor", "s_ve", add_done[m % 2])
                op("tensor", lambda e, m=m, hlf=hlf, tc=tc, bank=bank:
                   e.matmul(bank[:, 0:WA],
                            w_dt_s[hlf:hlf + 64, m * 128:(m + 1) * 128],
                            dt_lo[hlf:hlf + 64, tc:tc + WA],
                            start=True, stop=True), "s_mm")
                wait("scalar", "s_mm", snap("s_mm"))
                if m >= 2:
                    wait("scalar", "s_ve", add_done[m % 2])
                op("scalar", lambda e, m=m, bank=bank, scr=scr: e.activation(
                    scr[:], bank[:, 0:WA], AF.Exp, bias=s_bdt[:, m:m + 1]))
                op("scalar", lambda e, scr=scr: e.activation(
                    scr[:], scr[:], AF.Ln, bias=1.0))
                se_done[m % 2] = marker("scalar", "s_se")
                wait("vector", "s_se", se_done[m % 2])
                if m == 0:
                    add_done[0] = op("vector", lambda e, scr=scr:
                                     e.tensor_copy(dtacc[:], scr[:]), "s_ve")
                else:
                    add_done[m % 2] = op(
                        "vector", lambda e, scr=scr: e.tensor_tensor(
                            dtacc[:], dtacc[:], scr[:], ALU.add), "s_ve")

            for m in range(KC_I):
                emit_dt(m)
            wait("tensor", "s_ve", snap("s_ve"))
            op("tensor", lambda e: e.matmul(
                pb[4][0:1, 0:WA], ONE, dtacc[:], start=True, stop=True),
                "s_mm")
            wait("vector", "s_mm", snap("s_mm"))
            ve_id = op("vector", lambda e: e.tensor_scalar(
                r_dtm[:, 0:WA], pb[4][0:1, 0:WA],
                1.0 / DI, None, ALU.mult), "s_ve")
            wait("scalar", "s_ve", ve_id)
            se_id = op("scalar", lambda e, t0=t0: e.activation(
                r_dec[:, t0:t0 + WA], r_dtm[:, 0:WA], AF.Exp, scale=-1.0),
                "s_se")
            wait("tensor", "s_se", se_id)
        wait("vector", "s_se", snap("s_se"))
        op("vector", lambda e: e.tensor_scalar(
            r_c[:, 0:TO], r_c[:, 0:TO], s_consts[0:1, 2:3], None, ALU.mult),
            "s_ve")
        op("vector", lambda e: e.tensor_tensor_scan(
            r_s[:], r_dec[:], r_c[:], 0.0, ALU.mult, ALU.add), "s_ve")
        for ti in range(TO // WA):
            t0 = ti * WA
            wait("tensor", "s_ve", snap("s_ve"))
            op("tensor", lambda e, t0=t0: e.matmul(
                pb[0][0:16, 0:WA], s_ones_r[0:1, 0:16],
                r_s[:, TO + t0:TO + t0 + WA], start=True, stop=True), "s_mm")
            wait("vector", "s_mm", snap("s_mm"))
            op("vector", lambda e, t0=t0: e.tensor_tensor(
                y16[:], cp_buf[:, t0:t0 + WA], pb[0][0:16, 0:WA], ALU.mult),
                "s_ve")
            wait("scalar", "s_ve", snap("s_ve"))
            se_id = op("scalar", lambda e: e.activation(
                y16b[:], y16[:], AF.Square), "s_se")
            wait("tensor", "s_se", se_id)
            op("tensor", lambda e: e.matmul(
                pb[1][0:1, 0:WA], s_consts[0:16, 0:1], y16[:],
                start=True, stop=True))
            op("tensor", lambda e: e.matmul(
                pb[2][0:1, 0:WA], s_consts[0:16, 0:1], y16b[:],
                start=True, stop=True), "s_mm")
            wait("vector", "s_mm", snap("s_mm"))
            op("vector", lambda e, t0=t0: e.tensor_scalar(
                r_muy[:, t0:t0 + WA], pb[1][0:1, 0:WA], 1.0 / 16, None,
                ALU.mult))
            op("vector", lambda e, t0=t0: e.tensor_scalar(
                r_invy[:, t0:t0 + WA], pb[2][0:1, 0:WA], 1.0 / 16, None,
                ALU.mult))
            op("vector", lambda e, t0=t0: e.tensor_tensor(
                r_tmp[:, t0:t0 + WA], r_muy[:, t0:t0 + WA],
                r_muy[:, t0:t0 + WA], ALU.mult))
            ve_id = op("vector", lambda e, t0=t0: e.tensor_tensor(
                r_invy[:, t0:t0 + WA], r_invy[:, t0:t0 + WA],
                r_tmp[:, t0:t0 + WA], ALU.subtract), "s_ve")
            wait("scalar", "s_ve", ve_id)
            op("scalar", lambda e, t0=t0: e.activation(
                r_invy[:, t0:t0 + WA], r_invy[:, t0:t0 + WA],
                AF.Ln, bias=EPS_AP))
            se_id = op("scalar", lambda e, t0=t0: e.activation(
                r_invy[:, t0:t0 + WA], r_invy[:, t0:t0 + WA],
                AF.Exp, scale=-0.5), "s_se")
            wait("tensor", "s_se", se_id)
            op("tensor", lambda e, t0=t0: e.matmul(
                pb[1][0:16, 0:WA], s_ones_r[0:1, 0:16],
                r_muy[:, t0:t0 + WA], start=True, stop=True))
            op("tensor", lambda e, t0=t0: e.matmul(
                pb[2][0:16, 0:WA], s_ones_r[0:1, 0:16],
                r_invy[:, t0:t0 + WA], start=True, stop=True), "s_mm")
            wait("vector", "s_mm", snap("s_mm"))
            op("vector", lambda e: e.tensor_tensor(
                y16[:], y16[:], pb[1][0:16, 0:WA], ALU.subtract))
            ve_id = op("vector", lambda e: e.tensor_tensor(
                y16[:], y16[:], pb[2][0:16, 0:WA], ALU.mult), "s_ve")
            wait("tensor", "s_ve", ve_id)
            op("tensor", lambda e: e.matmul(
                pb[3][:, 0:WA], s_p16[:], y16[:], start=True, stop=True),
                "s_mm")
            wait("vector", "s_mm", snap("s_mm"))
            op("vector", lambda e, t0=t0: e.tensor_copy(
                yrep[:, t0:t0 + WA], pb[3][:, 0:WA]), "s_ve")
            wait("tensor", "s_ve", snap("s_ve"))
        G = 16

        def vsync():
            v = marker("vector", "s_ve")
            wait("vector", "s_ve", v)

        def r3(ap):
            return ap.rearrange("p (g e) -> p g e", g=G)

        def bc(ap):
            return ap[:, 0:G].unsqueeze(2).broadcast_to([128, G, 16])

        wait("vector", "s_w", snap("s_w"))
        tp_cp = [0, 0]
        for g in range(G):
            c0 = g * 128
            if g >= 2:
                wait("tensor", "s_ve", tp_cp[g % 2])
            op("tensor", lambda e, g=g, c0=c0: e.transpose(
                pb[g % 2][:, 0:16], lg_fm[:, c0:c0 + 128],
                s_ident[0:16, 0:16]), "s_mm")
            wait("vector", "s_mm", snap("s_mm"))
            tp_cp[g % 2] = op("vector", lambda e, g=g: e.tensor_copy(
                tkl[:, g * 16:(g + 1) * 16], pb[g % 2][:, 0:16]), "s_ve")
        vsync()
        op("vector", lambda e: e.tensor_reduce(
            tkm[:, 0:G].unsqueeze(2), r3(tkl[:, :]), AX.X, ALU.max))
        vsync()
        op("vector", lambda e: e.tensor_tensor(
            r3(tkl[:, :]), r3(tkl[:, :]), bc(tkm), ALU.subtract))
        op("vector", lambda e: e.tensor_copy(tkcur[:], tkl[:]))
        for r in range(4):
            op("vector", lambda e: e.tensor_reduce(
                tkm[:, 0:G].unsqueeze(2), r3(tkcur[:, :]), AX.X, ALU.max))
            vsync()
            op("vector", lambda e: e.tensor_tensor(
                r3(tkeq[:, :]), r3(tkcur[:, :]), bc(tkm), ALU.is_ge))
            op("vector", lambda e: e.tensor_tensor(
                tkcand[:], tkeq[:], s_iotambb[:], ALU.mult))
            op("vector", lambda e: e.tensor_scalar(
                tkcand[:], tkcand[:], BIG * 10, None, ALU.add))
            op("vector", lambda e: e.tensor_reduce(
                tkm[:, 0:G].unsqueeze(2), r3(tkcand[:, :]), AX.X, ALU.min))
            vsync()
            op("vector", lambda e: e.tensor_tensor(
                r3(tkeq[:, :]), r3(s_iotab[:, :]), bc(tkm), ALU.is_equal))
            if r == 0:
                op("vector", lambda e: e.tensor_copy(tksel[:], tkeq[:]))
            else:
                op("vector", lambda e: e.tensor_tensor(
                    tksel[:], tksel[:], tkeq[:], ALU.add))
            op("vector", lambda e: e.scalar_tensor_tensor(
                tkcur[:], tkeq[:], -BIG, tkcur[:], ALU.mult, ALU.add))
        ve_id = marker("vector", "s_ve")
        wait("scalar", "s_ve", ve_id)
        se_id = op("scalar", lambda e: e.activation(
            tkw[:], tkl[:], AF.Exp), "s_se")
        wait("vector", "s_se", se_id)
        op("vector", lambda e: e.tensor_tensor(
            tkw[:], tkw[:], tksel[:], ALU.mult))
        op("vector", lambda e: e.tensor_reduce(
            tkm[:, 0:G].unsqueeze(2), r3(tkw[:, :]), AX.X, ALU.add))
        vsync()
        op("vector", lambda e: e.reciprocal(tkm[:, 0:G], tkm[:, 0:G]))
        vsync()
        ve_id = op("vector", lambda e: e.tensor_tensor(
            r3(tkw[:, :]), r3(tkw[:, :]), bc(tkm), ALU.mult), "s_ve")
        wait("tensor", "s_ve", ve_id)
        tp_cp = [0, 0]
        for g in range(G):
            c0 = g * 128
            if g >= 2:
                wait("tensor", "s_ve", tp_cp[g % 2])
            op("tensor", lambda e, g=g: e.transpose(
                pb[g % 2][0:16, 0:128], tkw[:, g * 16:(g + 1) * 16],
                s_ident[:]), "s_mm")
            wait("vector", "s_mm", snap("s_mm"))
            tp_cp[g % 2] = op("vector", lambda e, g=g, c0=c0: e.tensor_copy(
                comb_fm[:, c0:c0 + 128], pb[g % 2][0:16, 0:128]), "s_ve")
        wait("tensor", "s_ve", snap("s_ve"))
        aii_end = barrier()
        if dbg:
            for s in ("s_mm", "s_ve", "s_se"):
                wait("gpsimd", s, aii_end[s])
            dma("gpsimd", dbg_t[0:1, 0:TA], r_dec[:], "s_sp")
            dma("gpsimd", dbg_t[1:2, 0:TA], r_s[:], "s_sp")
            dma("gpsimd", dbg_t[16:32, 0:TO], cp_buf[:], "s_sp")
            dma("gpsimd", dbg_t[32:48, 0:TO], comb_fm[:], "s_sp")
            dma("gpsimd", dbg_t[48:64, 0:TO], lg_fm[:], "s_sp")
            dma("gpsimd", dbg_t[0:128, TO * 2:TO * 3], dt_lo[:], "s_sp")
            dma("gpsimd", dbg_t[0:128, TO * 3:TO * 4], yrep[:], "s_sp")
            wait("sync", "s_sp", snap("s_sp"))
        mark("aii")

        emit("scalar", lambda e: _load_act_set(nc, e, SET_SILU))
        wait("sync", "s_mm", aii_end["s_mm"])
        wait("sync", "s_ve", aii_end["s_ve"])
        wait("sync", "s_sp", sp_after_p2)
        dma("sync", s_sel[:], sel16, "s_w")

        exps = [NE] + list(range(NE))
        w13_free = [0, 0]
        w2_free = 0
        last_act = 0
        prev_eo = None
        first = True
        for hf in range(2):
            h0 = hf * TH
            if hf > 0:
                wait("sync", "s_mm", snap("s_mm"))
            for k in range(KC_D):
                dma("sync", xn_h[:, k * TH:(k + 1) * TH],
                    xn_sp[k * 128:(k + 1) * 128, h0:h0 + TH], "s_x")
            xn_ld = snap("s_x")
            wait("tensor", "s_x", xn_ld)
            for ei, ex in enumerate(exps):
                wb = w13_s[ei % 2]
                if not first or ei >= 2:
                    wait("sync", "s_ve", w13_free[ei % 2])
                src = sw13 if ex == NE else w13[ex]
                for k in range(KC_D):
                    dma("sync", wb[:, k * 1024:(k + 1) * 1024],
                        src[k * 128:(k + 1) * 128, :], "s_w")
                wait("sync", "s_mm", w2_free)
                src2 = sw2t if ex == NE else w2t[ex]
                for k in range(4):
                    dma("sync", w2_s[:, k * D:(k + 1) * D],
                        src2[k * 128:(k + 1) * 128, :], "s_w")
                w_ld = snap("s_w")
                if ex != NE:
                    if prev_eo is not None:
                        wait("tensor", "s_ve", prev_eo[1])
                    for tt in range(2):
                        op("tensor", lambda e, ex=ex, h0=h0, tt=tt: e.matmul(
                            pb[4 + tt][:, 0:512],
                            s_sel[:, ex * 128:(ex + 1) * 128],
                            comb_fm[:, h0 + tt * 512:h0 + (tt + 1) * 512],
                            start=True, stop=True),
                            "s_mm" if tt == 1 else None)
                    wait("vector", "s_mm", snap("s_mm"))
                    for tt in range(2):
                        op("vector", lambda e, tt=tt: e.tensor_copy(
                            cbc_b[:, tt * 512:(tt + 1) * 512],
                            pb[4 + tt][:, 0:512]),
                            "s_ve" if tt == 1 else None)
                cbc_ready = cnt["s_ve"]
                wait("tensor", "s_w", w_ld)
                h_free = [0, 0]
                for hh in range(4):
                    for tt in range(2):
                        bank_a = pb[(hh * 2 + tt) % 2]
                        bank_b = pb[2 + (hh * 2 + tt) % 2]
                        if hh >= 1:
                            wait("tensor", "s_ve", h_free[tt])
                        for k in range(KC_D):
                            op("tensor",
                               lambda e, k=k, hh=hh, tt=tt, bank_a=bank_a,
                               wb=wb: e.matmul(
                                   bank_a[:, 0:512],
                                   wb[:, k * 1024 + hh * 128:
                                      k * 1024 + (hh + 1) * 128],
                                   xn_h[:, k * TH + tt * 512:
                                        k * TH + (tt + 1) * 512],
                                   start=(k == 0), stop=(k == KC_D - 1)))
                        for k in range(KC_D):
                            op("tensor",
                               lambda e, k=k, hh=hh, tt=tt, bank_b=bank_b,
                               wb=wb: e.matmul(
                                   bank_b[:, 0:512],
                                   wb[:, k * 1024 + 512 + hh * 128:
                                      k * 1024 + 512 + (hh + 1) * 128],
                                   xn_h[:, k * TH + tt * 512:
                                        k * TH + (tt + 1) * 512],
                                   start=(k == 0), stop=(k == KC_D - 1)),
                               "s_mm" if k == KC_D - 1 else None)
                        mm_done = cnt["s_mm"]
                        wait("scalar", "s_mm", mm_done)
                        if last_act:
                            wait("scalar", "s_ve", last_act)
                        se_id = op("scalar",
                                   lambda e, bank_a=bank_a: e.activation(
                                       silu_b[:], bank_a[:, 0:512], AF.Silu),
                                   "s_se")
                        wait("vector", "s_mm", mm_done)
                        wait("vector", "s_ve", cbc_ready)
                        if ex != NE:
                            op("vector", lambda e, tt=tt, bank_b=bank_b:
                               e.tensor_tensor(
                                   h3c_b[:], bank_b[:, 0:512],
                                   cbc_b[:, tt * 512:(tt + 1) * 512],
                                   ALU.mult))
                        else:
                            op("vector", lambda e, bank_b=bank_b:
                               e.tensor_copy(h3c_b[:], bank_b[:, 0:512]))
                        wait("vector", "s_se", se_id)
                        last_act = op(
                            "vector", lambda e, hh=hh, tt=tt: e.tensor_tensor(
                                act_b[:, hh * TH + tt * 512:
                                      hh * TH + (tt + 1) * 512],
                                silu_b[:], h3c_b[:], ALU.mult), "s_ve")
                        h_free[tt] = last_act
                act_ready = cnt["s_ve"]
                wait("tensor", "s_ve", act_ready)
                eo_free = [0, 0, 0, 0]
                for tg in range(8):
                    for dh in range(2):
                        bank = pb[4 + (tg * 2 + dh) % 4]
                        if tg * 2 + dh >= 4:
                            wait("tensor", "s_ve", eo_free[(tg * 2 + dh) % 4])
                        elif tg * 2 + dh < 2 and ex != NE:
                            wait("tensor", "s_ve", cbc_ready)
                        for k in range(4):
                            op("tensor",
                               lambda e, k=k, tg=tg, dh=dh, bank=bank:
                               e.matmul(
                                   bank[:, 0:512],
                                   act_b[:, k * TH + tg * 128:
                                         k * TH + (tg + 1) * 128],
                                   w2_s[:, k * D + dh * 512:
                                        k * D + (dh + 1) * 512],
                                   start=(k == 0), stop=(k == 3)),
                               "s_mm" if k == 3 else None)
                        wait("vector", "s_mm", snap("s_mm"))
                        if ei == 0:
                            eo_free[(tg * 2 + dh) % 4] = op(
                                "vector", lambda e, tg=tg, dh=dh, bank=bank:
                                e.tensor_copy(
                                    acc[:, tg * D + dh * 512:
                                        tg * D + (dh + 1) * 512],
                                    bank[:, 0:512]), "s_ve")
                        else:
                            eo_free[(tg * 2 + dh) % 4] = op(
                                "vector", lambda e, tg=tg, dh=dh, bank=bank:
                                e.tensor_tensor(
                                    acc[:, tg * D + dh * 512:
                                        tg * D + (dh + 1) * 512],
                                    acc[:, tg * D + dh * 512:
                                        tg * D + (dh + 1) * 512],
                                    bank[:, 0:512], ALU.add), "s_ve")
                w2_free = cnt["s_mm"]
                w13_free[ei % 2] = cnt["s_ve"]
                prev_eo = list(eo_free)
                first = False
            wait("gpsimd", "s_ve", snap("s_ve"))
            for tg in range(8):
                dma("gpsimd", ffn_sp[h0 + tg * 128:h0 + (tg + 1) * 128, :],
                    acc[:, tg * D:(tg + 1) * D], "s_sp")
            wait("vector", "s_sp", snap("s_sp"))
        p3_end = barrier()
        mark("p3")

        wait("sync", "s_mm", p3_end["s_mm"])
        wait("sync", "s_ve", p3_end["s_ve"])
        for k in range(KC_D):
            dma("sync", w_in_h4[:, k * DI:(k + 1) * DI],
                w_in[k * 128:(k + 1) * 128, DI:FZ], "s_w")
        for k in range(KC_I):
            dma("sync", w_out_s[:, k * D:(k + 1) * D],
                w_out[k * 128:(k + 1) * 128, :], "s_w")
        w_p4_ld = snap("s_w")
        wait("tensor", "s_w", w_p4_ld)
        xn_done = [0, 0]
        aux_done = [0, 0]
        for ti in range(TO // WP):
            t0 = ti * WP
            xb = xn_t[ti % 2]
            xtb = xtm_t[ti % 2]
            ftb = ffn_t[ti % 2]
            if ti >= 2:
                wait("sync", "s_mm", xn_done[ti % 2])
                wait("gpsimd", "s_ve", aux_done[ti % 2])
            for k in range(KC_D):
                dma("sync", xb[:, k * WP:(k + 1) * WP],
                    xn_sp[k * 128:(k + 1) * 128, t0:t0 + WP], "s_x")
            dma("gpsimd", xtb[:], x_tm[t0:t0 + 128, :], "s_sp")
            dma("gpsimd", ftb[:], ffn_sp[t0:t0 + 128, :], "s_sp")
            ld_x = snap("s_x")
            ld_aux = snap("s_sp")
            wait("tensor", "s_x", ld_x)
            z_free = [0, 0]
            ab_mark = [0, 0]
            wout_mark = [0, 0]

            def emit_z(m):
                bank = pb[m % 2]
                szm = szb[m % 2]
                if m >= 2:
                    wait("tensor", "s_se", z_free[m % 2])
                for k in range(KC_D):
                    op("tensor", lambda e, m=m, k=k, bank=bank, xb=xb:
                       e.matmul(
                           bank[:, 0:WP],
                           w_in_h4[:, k * DI + m * 128:k * DI + (m + 1) * 128],
                           xb[:, k * WP:(k + 1) * WP],
                           start=(k == 0), stop=(k == KC_D - 1)),
                       "s_mm" if k == KC_D - 1 else None)
                wait("scalar", "s_mm", snap("s_mm"))
                if m >= 2:
                    wait("scalar", "s_ve", ab_mark[m % 2])
                z_free[m % 2] = op(
                    "scalar", lambda e, bank=bank, szm=szm: e.activation(
                        szm[:], bank[:, 0:WP], AF.Silu), "s_se")
                wait("vector", "s_se", z_free[m % 2])
                if m >= 2:
                    wait("vector", "s_mm", wout_mark[m % 2])
                abm = ab[m % 2]
                op("vector", lambda e, m=m, t0=t0, abm=abm: e.tensor_scalar(
                    abm[:], yrep[:, t0:t0 + WP],
                    s_lng[:, m:m + 1], s_lnb[:, m:m + 1],
                    ALU.mult, ALU.add))
                ab_mark[m % 2] = op(
                    "vector", lambda e, abm=abm, szm=szm: e.tensor_tensor(
                        abm[:], abm[:], szm[:], ALU.mult), "s_ve")

            def emit_wout(m):
                abm = ab[m % 2]
                wait("tensor", "s_ve", ab_mark[m % 2])
                for tg in range(2):
                    for dh in range(2):
                        wout_mark[m % 2] = op(
                            "tensor", lambda e, m=m, tg=tg, dh=dh, abm=abm:
                            e.matmul(
                                pb[4 + tg * 2 + dh][:, 0:512],
                                abm[:, tg * 128:(tg + 1) * 128],
                                w_out_s[:, m * D + dh * 512:
                                        m * D + (dh + 1) * 512],
                                start=(m == 0), stop=(m == KC_I - 1)),
                            "s_mm")

            for m in range(KC_I):
                emit_z(m)
                if m >= 1:
                    emit_wout(m - 1)
            emit_wout(KC_I - 1)
            mm_fin = cnt["s_mm"]
            xn_done[ti % 2] = mm_fin
            wait("vector", "s_mm", mm_fin)
            wait("vector", "s_sp", ld_aux)
            for dh in range(2):
                op("vector", lambda e, dh=dh, xtb=xtb: e.tensor_tensor(
                    ob[:, dh * 512:(dh + 1) * 512],
                    pb[4 + dh][:, 0:512],
                    xtb[:, dh * 512:(dh + 1) * 512], ALU.add))
                op("vector", lambda e, dh=dh, ftb=ftb: e.tensor_tensor(
                    ob[:, dh * 512:(dh + 1) * 512],
                    ob[:, dh * 512:(dh + 1) * 512],
                    ftb[:, dh * 512:(dh + 1) * 512], ALU.add),
                    "s_ve" if dh == 1 else None)
            ve1 = cnt["s_ve"]
            wait("gpsimd", "s_ve", ve1)
            dma("gpsimd", out[t0:t0 + 128, :], ob[:, 0:D], "s_sp")
            dma("gpsimd", xtb[:], x_tm[t0 + 128:t0 + 256, :], "s_sp")
            dma("gpsimd", ftb[:], ffn_sp[t0 + 128:t0 + 256, :], "s_sp")
            ld_aux2 = snap("s_sp")
            wait("vector", "s_sp", ld_aux2)
            for dh in range(2):
                op("vector", lambda e, dh=dh, xtb=xtb: e.tensor_tensor(
                    ob[:, D + dh * 512:D + (dh + 1) * 512],
                    pb[6 + dh][:, 0:512],
                    xtb[:, dh * 512:(dh + 1) * 512], ALU.add))
                op("vector", lambda e, dh=dh, ftb=ftb: e.tensor_tensor(
                    ob[:, D + dh * 512:D + (dh + 1) * 512],
                    ob[:, D + dh * 512:D + (dh + 1) * 512],
                    ftb[:, dh * 512:(dh + 1) * 512], ALU.add),
                    "s_ve" if dh == 1 else None)
            ve2 = cnt["s_ve"]
            aux_done[ti % 2] = ve2
            wait("gpsimd", "s_ve", ve2)
            dma("gpsimd", out[t0 + 128:t0 + 256, :], ob[:, D:2 * D], "s_sp")
            wait("tensor", "s_ve", ve2)
        wait("gpsimd", "s_sp", snap("s_sp"))
        wait("sync", "s_sp", snap("s_sp"))

        cutname = {1: "p2", 2: "aii", 3: "p3"}.get(phases)
        envcut = os.environ.get("ATH_CUT")
        if envcut:
            cutname = envcut
        if cutname is not None:
            for e in plan:
                plan[e] = plan[e][:marks[cutname][e]]

        @block.sync
        def _(eng):
            for fn in plan["sync"]:
                fn(eng)

        @block.tensor
        def _(eng):
            for fn in plan["tensor"]:
                fn(eng)

        @block.vector
        def _(eng):
            for fn in plan["vector"]:
                fn(eng)

        @block.scalar
        def _(eng):
            for fn in plan["scalar"]:
                fn(eng)

        @block.gpsimd
        def _(eng):
            for fn in plan["gpsimd"]:
                fn(eng)

    return nc


def _off(nc, handle):
    return nc.lookup_mloc(handle).addr


def _host_inputs(inputs):
    x = np.ascontiguousarray(np.asarray(inputs["L0_attn_x"], np.float32))
    w_in = np.ascontiguousarray(np.asarray(inputs["L0_attn_W_in"], np.float32))
    w_conv = np.asarray(inputs["L0_attn_W_conv"], np.float32)
    w_xproj = np.asarray(inputs["L0_attn_W_xproj"], np.float32)
    w_dt = np.ascontiguousarray(np.asarray(inputs["L0_attn_W_dt"], np.float32))
    b_dt = np.asarray(inputs["L0_attn_b_dt"], np.float32)
    w_out = np.ascontiguousarray(np.asarray(inputs["L0_attn_W_out"], np.float32))
    ln_g = np.asarray(inputs["L0_attn_ln_g"], np.float32)
    ln_b = np.asarray(inputs["L0_attn_ln_b"], np.float32)
    gate_w = np.asarray(inputs["L0_ffn_gate_w"], np.float32)
    w1 = np.asarray(inputs["L0_ffn_w1"], np.float32)
    w3 = np.asarray(inputs["L0_ffn_w3"], np.float32)
    w2 = np.asarray(inputs["L0_ffn_w2"], np.float32)
    sw1 = np.asarray(inputs["L0_ffn_sw1"], np.float32)
    sw3 = np.asarray(inputs["L0_ffn_sw3"], np.float32)
    sw2 = np.asarray(inputs["L0_ffn_sw2"], np.float32)

    conv3 = w_conv.reshape(DI, 3)
    conv_c = np.ascontiguousarray(
        conv3.reshape(KC_I, 128, 3).transpose(1, 0, 2).reshape(128, KC_I * 3))
    wxp_h = np.ascontiguousarray(np.concatenate([
        w_xproj[:, 64:80], w_xproj[:, 80:96], w_xproj[:, 0:64],
        np.full((DI, 1), 1.0 / DI, np.float32)], axis=1))
    b_dt_c = np.ascontiguousarray(b_dt.reshape(KC_I, 128).T)
    ln_g_c = np.ascontiguousarray(ln_g.reshape(KC_I, 128).T)
    ln_b_c = np.ascontiguousarray(ln_b.reshape(KC_I, 128).T)
    gate_wT = np.ascontiguousarray(gate_w.T)
    w13 = np.ascontiguousarray(np.concatenate(
        [w1.transpose(0, 2, 1), w3.transpose(0, 2, 1)], axis=2))
    w2t = np.ascontiguousarray(w2.transpose(0, 2, 1))
    sw13 = np.ascontiguousarray(np.concatenate([sw1.T, sw3.T], axis=1))
    sw2t = np.ascontiguousarray(sw2.T)

    consts = np.zeros((128, 8), np.float32)
    consts[:, 0] = 1.0
    consts[:, 1] = EPS
    ident = np.eye(128, dtype=np.float32)
    iota16 = np.tile(np.arange(NE, dtype=np.float32), (128, 1))
    iotamb = np.ascontiguousarray(iota16 - BIG * 10)
    iotab = np.ascontiguousarray(np.tile(iota16, (1, 16)))
    iotambb = np.ascontiguousarray(iotab - BIG * 10)
    iota16 = np.ascontiguousarray(iota16)
    p16 = np.zeros((16, 128), np.float32)
    p16[np.arange(128) % 16, np.arange(128)] = 1.0
    sel16 = np.zeros((16, NE * 128), np.float32)
    for e in range(NE):
        sel16[e, e * 128:(e + 1) * 128] = 1.0
    ones_r = np.ones((1, 128), np.float32)

    shared = dict(w_in=w_in, conv_w=conv_c, wxp=wxp_h, w_dt=w_dt,
                  b_dt_c=b_dt_c, w_out=w_out, ln_g_c=ln_g_c, ln_b_c=ln_b_c,
                  gate_wT=gate_wT, w13=w13, w2t=w2t, sw13=sw13, sw2t=sw2t,
                  consts=consts, ident=ident, iota16=iota16, iotamb=iotamb,
                  iotab=iotab, iotambb=iotambb,
                  p16=p16, sel16=sel16, ones_r=ones_r)

    in_maps = []
    for c in range(8):
        seq, half = c // 2, c % 2
        xs = x[seq]
        x_fmc = np.zeros((D, TAL), np.float32)
        x_fmc[:, 2048:4096] = xs[half * 2048:(half + 1) * 2048].T
        if half == 1:
            x_fmc[:, 0:2048] = xs[0:2048].T
        else:
            x_fmc[:, 4096] = xs[2048]
        m = dict(shared)
        cst = consts.copy()
        cst[:, 2] = float(half)
        m["consts"] = cst
        m["x_fm"] = np.ascontiguousarray(x_fmc)
        m["x_tm"] = np.ascontiguousarray(xs[half * 2048:(half + 1) * 2048])
        in_maps.append(m)
    return in_maps


def kernel(**inputs):
    in_maps = _host_inputs(inputs)
    nc = build_program()
    res = run_bass_kernel_spmd(nc, in_maps, core_ids=list(range(8)))
    outa = np.zeros((4, 4096, D), np.float32)
    for c in range(8):
        seq, half = c // 2, c % 2
        outa[seq, half * 2048:(half + 1) * 2048, :] = res.results[c]["out"]
    return outa


# version 12 (final)
# speedup vs baseline: 1.2208x; 1.1272x over previous
import contextlib
import os
import numpy as np
import concourse.bass as bass
import concourse.mybir as mybir
from concourse.bass_utils import run_bass_kernel_spmd

F32 = mybir.dt.float32
AF = mybir.ActivationFunctionType
ALU = mybir.AluOpType
AX = mybir.AxisListType

D = 1024
DI = 2048
FZ = 4096
HID = 512
NE = 16
TO = 2048
TA = 4096
TAL = 4097
WP = 256
NT2 = TA // WP
WA = 512
NTA = TA // WA
TH = 1024
EPS = 1e-5
BIG = 1.0e5
SET_LNEXP = 6
SET_SILU = 18
KC_D = D // 128
KC_I = DI // 128
WPP = WP + 2


def _load_act_set(nc, eng, idx):
    eng.add_instruction(mybir.InstLoadActFuncSet(
        name=nc.get_next_instruction_name(), act_func_set_id=idx,
        ins=[], outs=[]))


def build_program():
    phases = int(os.environ.get("ATH_PHASES", "4"))
    skips = set(os.environ.get("ATH_SKIP", "").split(","))
    nc = bass.Bass("TRN2", target_bir_lowering=False,
                   detect_race_conditions=False)

    def din(name, shape):
        return nc.dram_tensor(name, shape, F32, kind="ExternalInput").ap()

    x_fm = din("x_fm", [D, TAL])
    x_tm = din("x_tm", [TO, D])
    w_in = din("w_in", [D, FZ])
    conv_w = din("conv_w", [128, KC_I * 3])
    wxp = din("wxp", [DI, 97])
    w_dt = din("w_dt", [64, DI])
    b_dt_c = din("b_dt_c", [128, KC_I])
    w_out = din("w_out", [DI, D])
    ln_g_c = din("ln_g_c", [128, KC_I])
    ln_b_c = din("ln_b_c", [128, KC_I])
    gate_wT = din("gate_wT", [D, NE])
    w13 = din("w13", [NE, D, 1024])
    w2t = din("w2t", [NE, HID, D])
    sw13 = din("sw13", [D, 1024])
    sw2t = din("sw2t", [HID, D])
    consts = din("consts", [128, 8])
    ident = din("ident", [128, 128])
    iota16 = din("iota16", [128, NE])
    iotamb = din("iotamb", [128, NE])
    iotab = din("iotab", [128, NE * 16])
    iotambb = din("iotambb", [128, NE * 16])
    p16 = din("p16", [16, 128])
    sel16 = din("sel16", [16, NE * 128])
    ones_r = din("ones_r", [1, 128])

    dbg = os.environ.get("ATH_DBGOUT", "0") == "1"
    ikind = "ExternalOutput" if dbg else "Internal"
    xn_sp = nc.dram_tensor("xn_sp", [D, TO], F32, kind=ikind).ap()
    ffn_sp = nc.dram_tensor("ffn_sp", [TO, D], F32, kind=ikind).ap()
    dbg_t = (nc.dram_tensor("dbg_t", [128, 8192], F32, kind="ExternalOutput").ap()
             if dbg else None)
    out = nc.dram_tensor("out", [TO, D], F32, kind="ExternalOutput").ap()

    ctx = contextlib.ExitStack()
    with ctx:
        def sbuf(name, shape):
            return ctx.enter_context(nc.sbuf_tensor(name, shape, F32))

        s_consts = sbuf("s_consts", [128, 8])
        s_ident = sbuf("s_ident", [128, 128])
        s_iota = sbuf("s_iota", [128, NE])
        s_iotamb = sbuf("s_iotamb", [128, NE])
        s_p16 = sbuf("s_p16", [16, 128])
        s_ones_r = sbuf("s_ones_r", [1, 128])
        s_convw = sbuf("s_convw", [128, KC_I * 3])
        s_bdt = sbuf("s_bdt", [128, KC_I])
        s_lng = sbuf("s_lng", [128, KC_I])
        s_lnb = sbuf("s_lnb", [128, KC_I])

        SBTOP = nc.SBUF_PARTITION_SIZE_BYTES
        MBASE = 26624
        assert nc.sbuf_base <= MBASE, nc.sbuf_base
        cur = [MBASE]

        def at(name, shape, off=None):
            if off is None:
                off = cur[0]
                nbytes = int(np.prod(shape[1:])) * 4
                cur[0] = off + ((nbytes + 31) // 32) * 32
            h = nc.alloc_sbuf_tensor_at(name, shape, F32, offset=off)
            assert off + int(np.prod(shape[1:])) * 4 <= SBTOP, (name, off)
            return h

        yrep = at("yrep", [128, TO])
        comb_fm = at("comb_fm", [16, TO])
        y16 = at("y16", [16, WA])
        y16b = at("y16b", [16, WA])
        G16 = NE * 16
        PERM_END = cur[0]
        r_dec = at("r_dec", [1, TA])
        r_c = at("r_c", [1, TA])
        dt_lo = at("dt_lo", [128, TO])
        cp_buf = at("cp_buf", [16, TO])
        lg_fm = at("lg_fm", [16, TO])
        MID_END = cur[0]
        r_s = r_c
        w_in_h = at("w_in_h", [128, KC_D * DI])
        w_xp = at("w_xp", [128, KC_I * 97])
        w_gate = at("w_gate", [128, KC_D * NE])
        WPS = 512
        xt = [at(f"xt{i}", [128, KC_D * WPS]) for i in range(2)]
        xzb = at("xzb", [128, WPS])
        xsb = [at(f"xsb{i}", [128, WPS]) for i in range(2)]
        cacc = [at(f"cacc{i}", [128, WPS]) for i in range(2)]
        bpcp = cacc[0]
        r_mu = at("r_mu", [1, WPS])
        r_var = at("r_var", [1, WPS])
        r_inv = at("r_inv", [1, WPS])
        r_xm = at("r_xm", [1, WPS])
        assert cur[0] <= SBTOP, cur[0]
        r_bps = at("r_bps", [1, WPS], off=_off(nc, xzb))
        dtacc = at("dtacc", [128, WA], off=_off(nc, xsb[0]))
        w_dt_s = at("w_dt_s", [128, DI], off=_off(nc, xt[0]))
        r_dtm = at("r_dtm", [1, WA], off=_off(nc, xt[0]) + 8192)
        xt0o = _off(nc, xt[0]) + 10240
        tkl = at("tkl", [128, G16], off=xt0o)
        tkcur = at("tkcur", [128, G16], off=xt0o + 1024)
        tkeq = at("tkeq", [128, G16], off=xt0o + 2048)
        tkcand = at("tkcand", [128, G16], off=xt0o + 3072)
        tksel = at("tksel", [128, G16], off=xt0o + 4096)
        tkw = at("tkw", [128, G16], off=xt0o + 5120)
        xt1o = _off(nc, xt[1]) + 12288
        tkm = at("tkm", [128, 16], off=xt1o)
        s_iotab = at("s_iotab", [128, G16], off=xt1o + 64)
        s_iotambb = at("s_iotambb", [128, G16], off=xt1o + 1088)
        dtscr = [at(f"dtscr{i}", [128, WA], off=_off(nc, xt[1]) + i * 2048)
                 for i in range(2)]
        r_muy = at("r_muy", [1, TO], off=_off(nc, xt[1]) + 4096)
        r_invy = at("r_invy", [1, TO], off=_off(nc, w_in_h))
        r_tmp = at("r_tmp", [1, TO], off=_off(nc, w_in_h) + 8192)
        p3 = [PERM_END]

        def at3(name, shape):
            off = p3[0]
            nbytes = int(np.prod(shape[1:])) * 4
            p3[0] = off + ((nbytes + 31) // 32) * 32
            h = nc.alloc_sbuf_tensor_at(name, shape, F32, offset=off)
            assert off + nbytes <= SBTOP, (name, off)
            return h

        xn_h = at3("xn_h", [128, KC_D * TH])
        acc = at3("acc", [128, 8 * D])
        w13_s = [at3(f"w13_s{i}", [128, KC_D * 1024]) for i in range(2)]
        w2_s = at3("w2_s", [128, 4 * D])
        act_b = at3("act_b", [128, 4 * TH])
        silu_b = at3("silu_b", [128, 512])
        h3c_b = at3("h3c_b", [128, 512])
        cbc_b = at3("cbc_b", [128, TH])
        s_sel = at3("s_sel", [16, NE * 128])
        assert p3[0] <= SBTOP, p3[0]
        p4 = [PERM_END]

        def at4(name, shape):
            off = p4[0]
            nbytes = int(np.prod(shape[1:])) * 4
            p4[0] = off + ((nbytes + 31) // 32) * 32
            h = nc.alloc_sbuf_tensor_at(name, shape, F32, offset=off)
            assert off + nbytes <= SBTOP, (name, off)
            return h

        w_in_h4 = at4("w_in_h4", [128, KC_D * DI])
        w_out_s = at4("w_out_s", [128, KC_I * D])
        xn_t = [at4(f"xn_t{i}", [128, KC_D * WP]) for i in range(2)]
        xtm_t = [at4(f"xtm_t{i}", [128, D]) for i in range(2)]
        ffn_t = [at4(f"ffn_t{i}", [128, D]) for i in range(2)]
        szb = [at4(f"szb{i}", [128, WP]) for i in range(2)]
        ab = [at4(f"ab{i}", [128, WP]) for i in range(2)]
        ob = at4("ob", [128, 2 * D])
        assert p4[0] <= SBTOP, p4[0]

        pb = [ctx.enter_context(nc.psum_tensor(f"pb{i}", [128, 512], F32))
              for i in range(8)]

        sems = {}
        for sname in ["s_x", "s_w", "s_sp", "s_mm", "s_ve", "s_se"]:
            sems[sname] = ctx.enter_context(nc.semaphore(sname))
        cnt = {k: 0 for k in sems}

        block = ctx.enter_context(nc.Block())
        plan = {e: [] for e in ("sync", "tensor", "vector", "scalar", "gpsimd")}
        marks = {}

        def mark(name):
            marks[name] = {e: len(plan[e]) for e in plan}

        def emit(eng, fn):
            plan[eng].append(fn)

        def dma(q, out_ap, in_ap, sem, slow=False):
            cnt[sem] += 16
            s = sems[sem]
            if slow:
                emit(q, lambda e, o=out_ap, i=in_ap, s=s:
                     e.dma_start(out=o, in_=i, allow_slow_non_contiguous=True)
                     .then_inc(s, 16))
            else:
                emit(q, lambda e, o=out_ap, i=in_ap, s=s:
                     e.dma_start(out=o, in_=i).then_inc(s, 16))
            return cnt[sem]

        def wait(eng, sem, val=None):
            v = cnt[sem] if val is None else val
            s = sems[sem]
            emit(eng, lambda e, s=s, v=v: e.wait_ge(s, v))

        def snap(sem):
            return cnt[sem]

        def op(eng, fn, sem=None):
            if sem is None:
                emit(eng, fn)
                return None
            cnt[sem] += 1
            s = sems[sem]
            emit(eng, lambda e, f=fn, s=s: f(e).then_inc(s, 1))
            return cnt[sem]

        def marker(eng, sem):
            cnt[sem] += 1
            s = sems[sem]
            emit(eng, lambda e, s=s: e.drain().then_inc(s, 1))
            return cnt[sem]

        ENG_SEM = {"tensor": "s_mm", "vector": "s_ve", "scalar": "s_se"}

        def barrier():
            for e in ("tensor", "vector", "scalar"):
                marker(e, ENG_SEM[e])
            tgt = {s: cnt[s] for s in ("s_mm", "s_ve", "s_se")}
            for e in ("tensor", "vector", "scalar", "gpsimd"):
                for s in ("s_mm", "s_ve", "s_se"):
                    wait(e, s, tgt[s])
            return tgt

        for dst, src in [(s_consts, consts), (s_ident, ident),
                         (s_iota, iota16), (s_iotamb, iotamb),
                         (s_p16, p16), (s_ones_r, ones_r),
                         (s_convw, conv_w), (s_bdt, b_dt_c),
                         (s_lng, ln_g_c), (s_lnb, ln_b_c)]:
            dma("sync", dst[:], src, "s_x")
        consts_ld = snap("s_x")
        for k in range(KC_D):
            dma("sync", w_in_h[:, k * DI:(k + 1) * DI],
                w_in[k * 128:(k + 1) * 128, 0:DI], "s_w")
            dma("sync", w_gate[:, k * NE:(k + 1) * NE],
                gate_wT[k * 128:(k + 1) * 128, :], "s_w")
        for k in range(KC_I):
            dma("sync", w_xp[:, k * 97:(k + 1) * 97],
                wxp[k * 128:(k + 1) * 128, :], "s_w")
        w_p2_ld = snap("s_w")
        mark("pre")

        ONE = s_consts[:, 0:1]
        EPS_AP = s_consts[0:1, 1:2]

        wait("tensor", "s_x", consts_ld)
        wait("vector", "s_x", consts_ld)
        wait("scalar", "s_x", consts_ld)
        wait("tensor", "s_w", w_p2_ld)
        xt_done = [0, 0]
        xt_sp = [0, 0]
        sq_done = [0, 0]
        cp_dma_prev = 0
        widths = [410, 410, 410, 410, 408]
        p2_tiles = []
        for h in range(2):
            t = h * TO
            for wdt in widths:
                p2_tiles.append((t, wdt))
                t += wdt
        for ti, (t0, W) in enumerate(p2_tiles):
            own = t0 >= TO
            lo = max(t0 - 1, 0)
            hi = t0 + W + 1
            wlo = t0 - lo
            off = 1 - wlo
            wwin = hi - lo
            buf = xt[ti % 2]
            if ti >= 2:
                wait("sync", "s_ve", xt_done[ti % 2])
                wait("sync", "s_sp", xt_sp[ti % 2])
            for k in range(KC_D):
                dma("sync", buf[:, k * WPS:k * WPS + wwin],
                    x_fm[k * 128:(k + 1) * 128, lo:hi], "s_x")
            ld = snap("s_x")
            wait("tensor", "s_x", ld)
            wait("scalar", "s_x", ld)
            for k in range(KC_D):
                op("tensor", lambda e, k=k, buf=buf, wwin=wwin:
                   e.matmul(pb[0][0:1, 0:wwin], ONE,
                            buf[:, k * WPS:k * WPS + wwin],
                            start=(k == 0), stop=(k == KC_D - 1)))
            for k in range(KC_D):
                sqb = cacc[k % 2]
                if k >= 2:
                    wait("scalar", "s_mm", sq_done[k % 2])
                se_id = op("scalar",
                           lambda e, k=k, buf=buf, wwin=wwin, sqb=sqb:
                           e.activation(sqb[:, 0:wwin],
                                        buf[:, k * WPS:k * WPS + wwin],
                                        AF.Square), "s_se")
                wait("tensor", "s_se", se_id)
                sq_done[k % 2] = op(
                    "tensor", lambda e, k=k, wwin=wwin, sqb=sqb:
                    e.matmul(pb[1][0:1, 0:wwin], ONE, sqb[:, 0:wwin],
                             start=(k == 0), stop=(k == KC_D - 1)), "s_mm")
            wait("vector", "s_mm", snap("s_mm"))
            op("vector", lambda e, wwin=wwin:
               e.tensor_scalar(r_mu[:, 0:wwin], pb[0][0:1, 0:wwin],
                               1.0 / D, None, ALU.mult))
            op("vector", lambda e, wwin=wwin:
               e.tensor_scalar(r_var[:, 0:wwin], pb[1][0:1, 0:wwin],
                               1.0 / D, None, ALU.mult))
            op("vector", lambda e, wwin=wwin: e.tensor_tensor(
                r_inv[:, 0:wwin], r_mu[:, 0:wwin], r_mu[:, 0:wwin],
                ALU.mult))
            ve_id = op("vector", lambda e, wwin=wwin: e.tensor_tensor(
                r_var[:, 0:wwin], r_var[:, 0:wwin], r_inv[:, 0:wwin],
                ALU.subtract), "s_ve")
            wait("scalar", "s_ve", ve_id)
            emit("scalar", lambda e: _load_act_set(nc, e, SET_LNEXP))
            op("scalar", lambda e, wwin=wwin: e.activation(
                r_inv[:, 0:wwin], r_var[:, 0:wwin], AF.Ln, bias=EPS_AP))
            se_id = op("scalar", lambda e, wwin=wwin: e.activation(
                r_inv[:, 0:wwin], r_inv[:, 0:wwin], AF.Exp, scale=-0.5),
                "s_se")
            emit("scalar", lambda e: _load_act_set(nc, e, SET_SILU))
            wait("tensor", "s_se", se_id)
            op("tensor", lambda e, wwin=wwin:
               e.matmul(pb[4][:, 0:wwin], s_ones_r[0:1, :],
                        r_mu[:, 0:wwin], start=True, stop=True))
            op("tensor", lambda e, wwin=wwin:
               e.matmul(pb[5][:, 0:wwin], s_ones_r[0:1, :],
                        r_inv[:, 0:wwin], start=True, stop=True), "s_mm")
            wait("vector", "s_mm", snap("s_mm"))
            for k in range(KC_D):
                op("vector", lambda e, k=k, buf=buf, wwin=wwin:
                   e.tensor_tensor(buf[:, k * WPS:k * WPS + wwin],
                                   buf[:, k * WPS:k * WPS + wwin],
                                   pb[4][:, 0:wwin], ALU.subtract))
                op("vector", lambda e, k=k, buf=buf, wwin=wwin:
                   e.tensor_tensor(buf[:, k * WPS:k * WPS + wwin],
                                   buf[:, k * WPS:k * WPS + wwin],
                                   pb[5][:, 0:wwin], ALU.mult),
                   "s_ve" if k == KC_D - 1 else None)
            xn_ready = cnt["s_ve"]
            if own:
                wait("gpsimd", "s_ve", xn_ready)
                for k in range(KC_D):
                    dma("gpsimd",
                        xn_sp[k * 128:(k + 1) * 128, t0 - TO:t0 - TO + W],
                        buf[:, k * WPS + wlo:k * WPS + wlo + W], "s_sp")
            wait("tensor", "s_ve", xn_ready)
            xz_evac = [0, 0, 0, 0]
            silu_mark = [0, 0]
            xp_mark = [0, 0]

            def emit_xz(m):
                bank = pb[m % 4]
                if m >= 4:
                    wait("tensor", "s_ve", xz_evac[m % 4])
                for k in range(KC_D):
                    op("tensor",
                       lambda e, m=m, k=k, bank=bank, wwin=wwin, buf=buf:
                       e.matmul(bank[:, 0:wwin],
                                w_in_h[:, k * DI + m * 128:k * DI + (m + 1) * 128],
                                buf[:, k * WPS:k * WPS + wwin],
                                start=(k == 0), stop=(k == KC_D - 1)),
                       "s_mm" if k == KC_D - 1 else None)
                wait("vector", "s_mm", snap("s_mm"))
                if ti == 0:
                    op("vector", lambda e: e.tensor_scalar(
                        xzb[:, 0:1], s_consts[:, 0:1], 0.0, None, ALU.mult))
                xz_evac[m % 4] = op(
                    "vector", lambda e, off=off, wwin=wwin, bank=bank:
                    e.tensor_copy(xzb[:, off:off + wwin], bank[:, 0:wwin]),
                    "s_ve")
                cb = cacc[m % 2]
                xb2 = xsb[m % 2]
                if m >= 2:
                    wait("vector", "s_se", silu_mark[m % 2])
                op("vector", lambda e, m=m, cb=cb, W=W: e.tensor_scalar(
                    cb[:, 0:W], xzb[:, 0:W],
                    s_convw[:, m * 3:m * 3 + 1], None, ALU.mult))
                op("vector", lambda e, m=m, cb=cb, W=W: e.scalar_tensor_tensor(
                    cb[:, 0:W], xzb[:, 1:1 + W],
                    s_convw[:, m * 3 + 1:m * 3 + 2], cb[:, 0:W],
                    ALU.mult, ALU.add))
                ve_id = op("vector", lambda e, m=m, cb=cb, W=W:
                           e.scalar_tensor_tensor(
                               cb[:, 0:W], xzb[:, 2:2 + W],
                               s_convw[:, m * 3 + 2:m * 3 + 3], cb[:, 0:W],
                               ALU.mult, ALU.add), "s_ve")
                wait("scalar", "s_ve", ve_id)
                if m >= 2:
                    wait("scalar", "s_mm", xp_mark[m % 2])
                silu_mark[m % 2] = op(
                    "scalar", lambda e, m=m, cb=cb, xb2=xb2, W=W:
                    e.activation(xb2[:, 0:W], cb[:, 0:W], AF.Silu), "s_se")

            def emit_xproj(m):
                xb2 = xsb[m % 2]
                wait("tensor", "s_se", silu_mark[m % 2])
                xp_mark[m % 2] = op(
                    "tensor", lambda e, m=m, xb2=xb2, W=W: e.matmul(
                        pb[6][0:97, 0:W], w_xp[:, m * 97:(m + 1) * 97],
                        xb2[:, 0:W], start=(m == 0), stop=(m == KC_I - 1)),
                    "s_mm")

            for m in range(KC_I):
                emit_xz(m)
                if m >= 1:
                    emit_xproj(m - 1)
            emit_xproj(KC_I - 1)
            wait("vector", "s_mm", snap("s_mm"))
            if own:
                wait("vector", "s_sp", cp_dma_prev)
            hlf = 64 * (t0 // TO)
            tc = t0 % TO
            op("vector", lambda e, W=W: e.tensor_copy(
                bpcp[0:32, 0:W], pb[6][0:32, 0:W]))
            op("vector", lambda e, hlf=hlf, tc=tc, W=W: e.tensor_copy(
                dt_lo[hlf:hlf + 32, tc:tc + W], pb[6][32:64, 0:W]))
            op("vector", lambda e, hlf=hlf, tc=tc, W=W: e.tensor_copy(
                dt_lo[hlf + 32:hlf + 64, tc:tc + W], pb[6][64:96, 0:W]))
            ve_id = op("vector", lambda e, W=W: e.tensor_copy(
                r_xm[:, 0:W], pb[6][96:97, 0:W]), "s_ve")
            wait("tensor", "s_ve", ve_id)
            mm_id = op("tensor", lambda e, W=W: e.matmul(
                pb[7][0:1, 0:W], s_consts[0:16, 0:1], bpcp[0:16, 0:W],
                start=True, stop=True), "s_mm")
            wait("vector", "s_mm", mm_id)
            op("vector", lambda e, W=W: e.tensor_copy(
                r_bps[:, 0:W], pb[7][0:1, 0:W]))
            op("vector", lambda e, t0=t0, W=W: e.tensor_tensor(
                r_c[:, t0:t0 + W], r_xm[:, 0:W], r_bps[:, 0:W],
                ALU.mult), "s_ve")
            if own:
                wait("gpsimd", "s_ve", snap("s_ve"))
                cp_dma_prev = dma("gpsimd", cp_buf[:, tc:tc + W],
                                  bpcp[16:32, 0:W], "s_sp")
                for k in range(KC_D):
                    op("tensor", lambda e, k=k, wlo=wlo, W=W, buf=buf:
                       e.matmul(
                           pb[4][0:16, 0:W],
                           w_gate[:, k * NE:(k + 1) * NE],
                           buf[:, k * WPS + wlo:k * WPS + wlo + W],
                           start=(k == 0), stop=(k == KC_D - 1)),
                       "s_mm" if k == KC_D - 1 else None)
                wait("vector", "s_mm", snap("s_mm"))
                op("vector", lambda e, tc=tc, W=W: e.tensor_copy(
                    lg_fm[:, tc:tc + W], pb[4][0:16, 0:W]), "s_ve")
            bt = barrier()
            xt_done[ti % 2] = bt["s_ve"]
            xt_sp[ti % 2] = snap("s_sp")
            wait("vector", "s_sp", snap("s_sp"))
            wait("scalar", "s_sp", snap("s_sp"))
            mark(f"t{ti}")
        p2_end = barrier()
        sp_after_p2 = snap("s_sp")
        mark("p2")
        emit("scalar", lambda e: _load_act_set(nc, e, SET_LNEXP))
        wait("vector", "s_sp", sp_after_p2)
        wait("sync", "s_mm", p2_end["s_mm"])
        wait("sync", "s_sp", sp_after_p2)
        dma("sync", w_dt_s[0:64, :], w_dt, "s_w")
        dma("sync", w_dt_s[64:128, :], w_dt, "s_w")
        dma("sync", s_iotab[:], iotab, "s_w")
        dma("sync", s_iotambb[:], iotambb, "s_w")
        w_dt_ld = snap("s_w")
        wait("tensor", "s_w", w_dt_ld)
        for ti in range(NTA):
            t0 = ti * WA
            hlf = 64 * (t0 // TO)
            tc = t0 % TO
            se_done = [0, 0]
            add_done = [0, 0]

            def emit_dt(m):
                bank = pb[m % 2]
                scr = dtscr[m % 2]
                if m >= 2:
                    wait("tensor", "s_ve", add_done[m % 2])
                op("tensor", lambda e, m=m, hlf=hlf, tc=tc, bank=bank:
                   e.matmul(bank[:, 0:WA],
                            w_dt_s[hlf:hlf + 64, m * 128:(m + 1) * 128],
                            dt_lo[hlf:hlf + 64, tc:tc + WA],
                            start=True, stop=True), "s_mm")
                wait("scalar", "s_mm", snap("s_mm"))
                if m >= 2:
                    wait("scalar", "s_ve", add_done[m % 2])
                op("scalar", lambda e, m=m, bank=bank, scr=scr: e.activation(
                    scr[:], bank[:, 0:WA], AF.Exp, bias=s_bdt[:, m:m + 1]))
                op("scalar", lambda e, scr=scr: e.activation(
                    scr[:], scr[:], AF.Ln, bias=1.0))
                se_done[m % 2] = marker("scalar", "s_se")
                wait("vector", "s_se", se_done[m % 2])
                if m == 0:
                    add_done[0] = op("vector", lambda e, scr=scr:
                                     e.tensor_copy(dtacc[:], scr[:]), "s_ve")
                else:
                    add_done[m % 2] = op(
                        "vector", lambda e, scr=scr: e.tensor_tensor(
                            dtacc[:], dtacc[:], scr[:], ALU.add), "s_ve")

            for m in range(KC_I):
                emit_dt(m)
            wait("tensor", "s_ve", snap("s_ve"))
            op("tensor", lambda e: e.matmul(
                pb[4][0:1, 0:WA], ONE, dtacc[:], start=True, stop=True),
                "s_mm")
            wait("vector", "s_mm", snap("s_mm"))
            ve_id = op("vector", lambda e: e.tensor_scalar(
                r_dtm[:, 0:WA], pb[4][0:1, 0:WA],
                1.0 / DI, None, ALU.mult), "s_ve")
            wait("scalar", "s_ve", ve_id)
            se_id = op("scalar", lambda e, t0=t0: e.activation(
                r_dec[:, t0:t0 + WA], r_dtm[:, 0:WA], AF.Exp, scale=-1.0),
                "s_se")
            wait("tensor", "s_se", se_id)
        wait("vector", "s_se", snap("s_se"))
        op("vector", lambda e: e.tensor_scalar(
            r_c[:, 0:TO], r_c[:, 0:TO], s_consts[0:1, 2:3], None, ALU.mult),
            "s_ve")
        op("vector", lambda e: e.tensor_tensor_scan(
            r_s[:], r_dec[:], r_c[:], 0.0, ALU.mult, ALU.add), "s_ve")
        for ti in range(TO // WA):
            t0 = ti * WA
            wait("tensor", "s_ve", snap("s_ve"))
            op("tensor", lambda e, t0=t0: e.matmul(
                pb[0][0:16, 0:WA], s_ones_r[0:1, 0:16],
                r_s[:, TO + t0:TO + t0 + WA], start=True, stop=True), "s_mm")
            wait("vector", "s_mm", snap("s_mm"))
            op("vector", lambda e, t0=t0: e.tensor_tensor(
                y16[:], cp_buf[:, t0:t0 + WA], pb[0][0:16, 0:WA], ALU.mult),
                "s_ve")
            wait("scalar", "s_ve", snap("s_ve"))
            se_id = op("scalar", lambda e: e.activation(
                y16b[:], y16[:], AF.Square), "s_se")
            wait("tensor", "s_se", se_id)
            op("tensor", lambda e: e.matmul(
                pb[1][0:1, 0:WA], s_consts[0:16, 0:1], y16[:],
                start=True, stop=True))
            op("tensor", lambda e: e.matmul(
                pb[2][0:1, 0:WA], s_consts[0:16, 0:1], y16b[:],
                start=True, stop=True), "s_mm")
            wait("vector", "s_mm", snap("s_mm"))
            op("vector", lambda e, t0=t0: e.tensor_scalar(
                r_muy[:, t0:t0 + WA], pb[1][0:1, 0:WA], 1.0 / 16, None,
                ALU.mult))
            op("vector", lambda e, t0=t0: e.tensor_scalar(
                r_invy[:, t0:t0 + WA], pb[2][0:1, 0:WA], 1.0 / 16, None,
                ALU.mult))
            op("vector", lambda e, t0=t0: e.tensor_tensor(
                r_tmp[:, t0:t0 + WA], r_muy[:, t0:t0 + WA],
                r_muy[:, t0:t0 + WA], ALU.mult))
            ve_id = op("vector", lambda e, t0=t0: e.tensor_tensor(
                r_invy[:, t0:t0 + WA], r_invy[:, t0:t0 + WA],
                r_tmp[:, t0:t0 + WA], ALU.subtract), "s_ve")
            wait("scalar", "s_ve", ve_id)
            op("scalar", lambda e, t0=t0: e.activation(
                r_invy[:, t0:t0 + WA], r_invy[:, t0:t0 + WA],
                AF.Ln, bias=EPS_AP))
            se_id = op("scalar", lambda e, t0=t0: e.activation(
                r_invy[:, t0:t0 + WA], r_invy[:, t0:t0 + WA],
                AF.Exp, scale=-0.5), "s_se")
            wait("tensor", "s_se", se_id)
            op("tensor", lambda e, t0=t0: e.matmul(
                pb[1][0:16, 0:WA], s_ones_r[0:1, 0:16],
                r_muy[:, t0:t0 + WA], start=True, stop=True))
            op("tensor", lambda e, t0=t0: e.matmul(
                pb[2][0:16, 0:WA], s_ones_r[0:1, 0:16],
                r_invy[:, t0:t0 + WA], start=True, stop=True), "s_mm")
            wait("vector", "s_mm", snap("s_mm"))
            op("vector", lambda e: e.tensor_tensor(
                y16[:], y16[:], pb[1][0:16, 0:WA], ALU.subtract))
            ve_id = op("vector", lambda e: e.tensor_tensor(
                y16[:], y16[:], pb[2][0:16, 0:WA], ALU.mult), "s_ve")
            wait("tensor", "s_ve", ve_id)
            op("tensor", lambda e: e.matmul(
                pb[3][:, 0:WA], s_p16[:], y16[:], start=True, stop=True),
                "s_mm")
            wait("vector", "s_mm", snap("s_mm"))
            op("vector", lambda e, t0=t0: e.tensor_copy(
                yrep[:, t0:t0 + WA], pb[3][:, 0:WA]), "s_ve")
            wait("tensor", "s_ve", snap("s_ve"))
        G = 16

        def vsync():
            v = marker("vector", "s_ve")
            wait("vector", "s_ve", v)

        def r3(ap):
            return ap.rearrange("p (g e) -> p g e", g=G)

        def bc(ap):
            return ap[:, 0:G].unsqueeze(2).broadcast_to([128, G, 16])

        wait("vector", "s_w", snap("s_w"))
        tp_cp = [0, 0]
        for g in range(G):
            c0 = g * 128
            if g >= 2:
                wait("tensor", "s_ve", tp_cp[g % 2])
            op("tensor", lambda e, g=g, c0=c0: e.transpose(
                pb[g % 2][:, 0:16], lg_fm[:, c0:c0 + 128],
                s_ident[0:16, 0:16]), "s_mm")
            wait("vector", "s_mm", snap("s_mm"))
            tp_cp[g % 2] = op("vector", lambda e, g=g: e.tensor_copy(
                tkl[:, g * 16:(g + 1) * 16], pb[g % 2][:, 0:16]), "s_ve")
        vsync()
        op("vector", lambda e: e.tensor_reduce(
            tkm[:, 0:G].unsqueeze(2), r3(tkl[:, :]), AX.X, ALU.max))
        vsync()
        op("vector", lambda e: e.tensor_tensor(
            r3(tkl[:, :]), r3(tkl[:, :]), bc(tkm), ALU.subtract))
        op("vector", lambda e: e.tensor_copy(tkcur[:], tkl[:]))
        for r in range(4):
            op("vector", lambda e: e.tensor_reduce(
                tkm[:, 0:G].unsqueeze(2), r3(tkcur[:, :]), AX.X, ALU.max))
            vsync()
            op("vector", lambda e: e.tensor_tensor(
                r3(tkeq[:, :]), r3(tkcur[:, :]), bc(tkm), ALU.is_ge))
            op("vector", lambda e: e.tensor_tensor(
                tkcand[:], tkeq[:], s_iotambb[:], ALU.mult))
            op("vector", lambda e: e.tensor_scalar(
                tkcand[:], tkcand[:], BIG * 10, None, ALU.add))
            op("vector", lambda e: e.tensor_reduce(
                tkm[:, 0:G].unsqueeze(2), r3(tkcand[:, :]), AX.X, ALU.min))
            vsync()
            op("vector", lambda e: e.tensor_tensor(
                r3(tkeq[:, :]), r3(s_iotab[:, :]), bc(tkm), ALU.is_equal))
            if r == 0:
                op("vector", lambda e: e.tensor_copy(tksel[:], tkeq[:]))
            else:
                op("vector", lambda e: e.tensor_tensor(
                    tksel[:], tksel[:], tkeq[:], ALU.add))
            op("vector", lambda e: e.scalar_tensor_tensor(
                tkcur[:], tkeq[:], -BIG, tkcur[:], ALU.mult, ALU.add))
        ve_id = marker("vector", "s_ve")
        wait("scalar", "s_ve", ve_id)
        se_id = op("scalar", lambda e: e.activation(
            tkw[:], tkl[:], AF.Exp), "s_se")
        wait("vector", "s_se", se_id)
        op("vector", lambda e: e.tensor_tensor(
            tkw[:], tkw[:], tksel[:], ALU.mult))
        op("vector", lambda e: e.tensor_reduce(
            tkm[:, 0:G].unsqueeze(2), r3(tkw[:, :]), AX.X, ALU.add))
        vsync()
        op("vector", lambda e: e.reciprocal(tkm[:, 0:G], tkm[:, 0:G]))
        vsync()
        ve_id = op("vector", lambda e: e.tensor_tensor(
            r3(tkw[:, :]), r3(tkw[:, :]), bc(tkm), ALU.mult), "s_ve")
        wait("tensor", "s_ve", ve_id)
        tp_cp = [0, 0]
        for g in range(G):
            c0 = g * 128
            if g >= 2:
                wait("tensor", "s_ve", tp_cp[g % 2])
            op("tensor", lambda e, g=g: e.transpose(
                pb[g % 2][0:16, 0:128], tkw[:, g * 16:(g + 1) * 16],
                s_ident[:]), "s_mm")
            wait("vector", "s_mm", snap("s_mm"))
            tp_cp[g % 2] = op("vector", lambda e, g=g, c0=c0: e.tensor_copy(
                comb_fm[:, c0:c0 + 128], pb[g % 2][0:16, 0:128]), "s_ve")
        wait("tensor", "s_ve", snap("s_ve"))
        aii_end = barrier()
        if dbg:
            for s in ("s_mm", "s_ve", "s_se"):
                wait("gpsimd", s, aii_end[s])
            dma("gpsimd", dbg_t[0:1, 0:TA], r_dec[:], "s_sp")
            dma("gpsimd", dbg_t[1:2, 0:TA], r_s[:], "s_sp")
            dma("gpsimd", dbg_t[16:32, 0:TO], cp_buf[:], "s_sp")
            dma("gpsimd", dbg_t[32:48, 0:TO], comb_fm[:], "s_sp")
            dma("gpsimd", dbg_t[48:64, 0:TO], lg_fm[:], "s_sp")
            dma("gpsimd", dbg_t[0:128, TO * 2:TO * 3], dt_lo[:], "s_sp")
            dma("gpsimd", dbg_t[0:128, TO * 3:TO * 4], yrep[:], "s_sp")
            wait("sync", "s_sp", snap("s_sp"))
        mark("aii")

        emit("scalar", lambda e: _load_act_set(nc, e, SET_SILU))
        wait("sync", "s_mm", aii_end["s_mm"])
        wait("sync", "s_ve", aii_end["s_ve"])
        wait("sync", "s_sp", sp_after_p2)
        dma("sync", s_sel[:], sel16, "s_w")

        exps = [NE] + list(range(NE))
        w13_free = [0, 0]
        w2_free = 0
        last_act = 0
        prev_eo = None
        first = True
        for hf in range(2):
            h0 = hf * TH
            if hf > 0:
                wait("sync", "s_mm", snap("s_mm"))
            for k in range(KC_D):
                dma("sync", xn_h[:, k * TH:(k + 1) * TH],
                    xn_sp[k * 128:(k + 1) * 128, h0:h0 + TH], "s_x")
            xn_ld = snap("s_x")
            wait("tensor", "s_x", xn_ld)
            for ei, ex in enumerate(exps):
                wb = w13_s[ei % 2]
                if not first or ei >= 2:
                    wait("sync", "s_ve", w13_free[ei % 2])
                src = sw13 if ex == NE else w13[ex]
                for k in range(KC_D):
                    dma("sync", wb[:, k * 1024:(k + 1) * 1024],
                        src[k * 128:(k + 1) * 128, :], "s_w")
                wait("sync", "s_mm", w2_free)
                src2 = sw2t if ex == NE else w2t[ex]
                for k in range(4):
                    dma("sync", w2_s[:, k * D:(k + 1) * D],
                        src2[k * 128:(k + 1) * 128, :], "s_w")
                w_ld = snap("s_w")
                if ex != NE:
                    if prev_eo is not None:
                        wait("tensor", "s_ve", prev_eo[1])
                    for tt in range(2):
                        op("tensor", lambda e, ex=ex, h0=h0, tt=tt: e.matmul(
                            pb[4 + tt][:, 0:512],
                            s_sel[:, ex * 128:(ex + 1) * 128],
                            comb_fm[:, h0 + tt * 512:h0 + (tt + 1) * 512],
                            start=True, stop=True),
                            "s_mm" if tt == 1 else None)
                    wait("vector", "s_mm", snap("s_mm"))
                    for tt in range(2):
                        op("vector", lambda e, tt=tt: e.tensor_copy(
                            cbc_b[:, tt * 512:(tt + 1) * 512],
                            pb[4 + tt][:, 0:512]),
                            "s_ve" if tt == 1 else None)
                cbc_ready = cnt["s_ve"]
                wait("tensor", "s_w", w_ld)
                h_free = [0, 0]
                for hh in range(4):
                    for tt in range(2):
                        bank_a = pb[(hh * 2 + tt) % 2]
                        bank_b = pb[2 + (hh * 2 + tt) % 2]
                        if hh >= 1:
                            wait("tensor", "s_ve", h_free[tt])
                        for k in range(KC_D):
                            op("tensor",
                               lambda e, k=k, hh=hh, tt=tt, bank_a=bank_a,
                               wb=wb: e.matmul(
                                   bank_a[:, 0:512],
                                   wb[:, k * 1024 + hh * 128:
                                      k * 1024 + (hh + 1) * 128],
                                   xn_h[:, k * TH + tt * 512:
                                        k * TH + (tt + 1) * 512],
                                   start=(k == 0), stop=(k == KC_D - 1)))
                        for k in range(KC_D):
                            op("tensor",
                               lambda e, k=k, hh=hh, tt=tt, bank_b=bank_b,
                               wb=wb: e.matmul(
                                   bank_b[:, 0:512],
                                   wb[:, k * 1024 + 512 + hh * 128:
                                      k * 1024 + 512 + (hh + 1) * 128],
                                   xn_h[:, k * TH + tt * 512:
                                        k * TH + (tt + 1) * 512],
                                   start=(k == 0), stop=(k == KC_D - 1)),
                               "s_mm" if k == KC_D - 1 else None)
                        mm_done = cnt["s_mm"]
                        wait("scalar", "s_mm", mm_done)
                        if last_act:
                            wait("scalar", "s_ve", last_act)
                        se_id = op("scalar",
                                   lambda e, bank_a=bank_a: e.activation(
                                       silu_b[:], bank_a[:, 0:512], AF.Silu),
                                   "s_se")
                        wait("vector", "s_mm", mm_done)
                        wait("vector", "s_ve", cbc_ready)
                        if ex != NE:
                            op("vector", lambda e, tt=tt, bank_b=bank_b:
                               e.tensor_tensor(
                                   h3c_b[:], bank_b[:, 0:512],
                                   cbc_b[:, tt * 512:(tt + 1) * 512],
                                   ALU.mult))
                        else:
                            op("vector", lambda e, bank_b=bank_b:
                               e.tensor_copy(h3c_b[:], bank_b[:, 0:512]))
                        wait("vector", "s_se", se_id)
                        last_act = op(
                            "vector", lambda e, hh=hh, tt=tt: e.tensor_tensor(
                                act_b[:, hh * TH + tt * 512:
                                      hh * TH + (tt + 1) * 512],
                                silu_b[:], h3c_b[:], ALU.mult), "s_ve")
                        h_free[tt] = last_act
                act_ready = cnt["s_ve"]
                wait("tensor", "s_ve", act_ready)
                eo_free = [0, 0, 0, 0]
                for tg in range(8):
                    for dh in range(2):
                        bank = pb[4 + (tg * 2 + dh) % 4]
                        if tg * 2 + dh >= 4:
                            wait("tensor", "s_ve", eo_free[(tg * 2 + dh) % 4])
                        elif tg * 2 + dh < 2 and ex != NE:
                            wait("tensor", "s_ve", cbc_ready)
                        for k in range(4):
                            op("tensor",
                               lambda e, k=k, tg=tg, dh=dh, bank=bank:
                               e.matmul(
                                   bank[:, 0:512],
                                   act_b[:, k * TH + tg * 128:
                                         k * TH + (tg + 1) * 128],
                                   w2_s[:, k * D + dh * 512:
                                        k * D + (dh + 1) * 512],
                                   start=(k == 0), stop=(k == 3)),
                               "s_mm" if k == 3 else None)
                        wait("vector", "s_mm", snap("s_mm"))
                        if ei == 0:
                            eo_free[(tg * 2 + dh) % 4] = op(
                                "vector", lambda e, tg=tg, dh=dh, bank=bank:
                                e.tensor_copy(
                                    acc[:, tg * D + dh * 512:
                                        tg * D + (dh + 1) * 512],
                                    bank[:, 0:512]), "s_ve")
                        else:
                            eo_free[(tg * 2 + dh) % 4] = op(
                                "vector", lambda e, tg=tg, dh=dh, bank=bank:
                                e.tensor_tensor(
                                    acc[:, tg * D + dh * 512:
                                        tg * D + (dh + 1) * 512],
                                    acc[:, tg * D + dh * 512:
                                        tg * D + (dh + 1) * 512],
                                    bank[:, 0:512], ALU.add), "s_ve")
                w2_free = cnt["s_mm"]
                w13_free[ei % 2] = cnt["s_ve"]
                prev_eo = list(eo_free)
                first = False
            wait("gpsimd", "s_ve", snap("s_ve"))
            for tg in range(8):
                dma("gpsimd", ffn_sp[h0 + tg * 128:h0 + (tg + 1) * 128, :],
                    acc[:, tg * D:(tg + 1) * D], "s_sp")
            wait("vector", "s_sp", snap("s_sp"))
        p3_end = barrier()
        mark("p3")

        wait("sync", "s_mm", p3_end["s_mm"])
        wait("sync", "s_ve", p3_end["s_ve"])
        for k in range(KC_D):
            dma("sync", w_in_h4[:, k * DI:(k + 1) * DI],
                w_in[k * 128:(k + 1) * 128, DI:FZ], "s_w")
        w_in_ld = snap("s_w")
        wait("tensor", "s_w", w_in_ld)
        xn_done = [0, 0]
        aux_done = [0, 0]
        for ti in range(TO // WP):
            t0 = ti * WP
            xb = xn_t[ti % 2]
            xtb = xtm_t[ti % 2]
            ftb = ffn_t[ti % 2]
            if ti >= 2:
                wait("sync", "s_mm", xn_done[ti % 2])
                wait("gpsimd", "s_ve", aux_done[ti % 2])
            for k in range(KC_D):
                dma("sync", xb[:, k * WP:(k + 1) * WP],
                    xn_sp[k * 128:(k + 1) * 128, t0:t0 + WP], "s_x")
            dma("gpsimd", xtb[:], x_tm[t0:t0 + 128, :], "s_sp")
            dma("gpsimd", ftb[:], ffn_sp[t0:t0 + 128, :], "s_sp")
            ld_x = snap("s_x")
            ld_aux = snap("s_sp")
            if ti == 0:
                for k in range(KC_I):
                    dma("sync", w_out_s[:, k * D:(k + 1) * D],
                        w_out[k * 128:(k + 1) * 128, :], "s_w")
                snap("s_w")
            wait("tensor", "s_x", ld_x)
            z_free = [0, 0]
            ab_mark = [0, 0]
            wout_mark = [0, 0]

            def emit_z(m):
                bank = pb[m % 2]
                szm = szb[m % 2]
                if m >= 2:
                    wait("tensor", "s_se", z_free[m % 2])
                for k in range(KC_D):
                    op("tensor", lambda e, m=m, k=k, bank=bank, xb=xb:
                       e.matmul(
                           bank[:, 0:WP],
                           w_in_h4[:, k * DI + m * 128:k * DI + (m + 1) * 128],
                           xb[:, k * WP:(k + 1) * WP],
                           start=(k == 0), stop=(k == KC_D - 1)),
                       "s_mm" if k == KC_D - 1 else None)
                wait("scalar", "s_mm", snap("s_mm"))
                if m >= 2:
                    wait("scalar", "s_ve", ab_mark[m % 2])
                z_free[m % 2] = op(
                    "scalar", lambda e, bank=bank, szm=szm: e.activation(
                        szm[:], bank[:, 0:WP], AF.Silu), "s_se")
                wait("vector", "s_se", z_free[m % 2])
                if m >= 2:
                    wait("vector", "s_mm", wout_mark[m % 2])
                abm = ab[m % 2]
                op("vector", lambda e, m=m, t0=t0, abm=abm: e.tensor_scalar(
                    abm[:], yrep[:, t0:t0 + WP],
                    s_lng[:, m:m + 1], s_lnb[:, m:m + 1],
                    ALU.mult, ALU.add))
                ab_mark[m % 2] = op(
                    "vector", lambda e, abm=abm, szm=szm: e.tensor_tensor(
                        abm[:], abm[:], szm[:], ALU.mult), "s_ve")

            def emit_wout(m):
                abm = ab[m % 2]
                if ti == 0:
                    wait("tensor", "s_w", w_in_ld + (m + 1) * 16)
                wait("tensor", "s_ve", ab_mark[m % 2])
                for tg in range(2):
                    for dh in range(2):
                        wout_mark[m % 2] = op(
                            "tensor", lambda e, m=m, tg=tg, dh=dh, abm=abm:
                            e.matmul(
                                pb[4 + tg * 2 + dh][:, 0:512],
                                abm[:, tg * 128:(tg + 1) * 128],
                                w_out_s[:, m * D + dh * 512:
                                        m * D + (dh + 1) * 512],
                                start=(m == 0), stop=(m == KC_I - 1)),
                            "s_mm")

            for m in range(KC_I):
                emit_z(m)
                if m >= 1:
                    emit_wout(m - 1)
            emit_wout(KC_I - 1)
            mm_fin = cnt["s_mm"]
            xn_done[ti % 2] = mm_fin
            wait("vector", "s_mm", mm_fin)
            wait("vector", "s_sp", ld_aux)
            for dh in range(2):
                op("vector", lambda e, dh=dh, xtb=xtb: e.tensor_tensor(
                    ob[:, dh * 512:(dh + 1) * 512],
                    pb[4 + dh][:, 0:512],
                    xtb[:, dh * 512:(dh + 1) * 512], ALU.add))
                op("vector", lambda e, dh=dh, ftb=ftb: e.tensor_tensor(
                    ob[:, dh * 512:(dh + 1) * 512],
                    ob[:, dh * 512:(dh + 1) * 512],
                    ftb[:, dh * 512:(dh + 1) * 512], ALU.add),
                    "s_ve" if dh == 1 else None)
            ve1 = cnt["s_ve"]
            wait("gpsimd", "s_ve", ve1)
            dma("gpsimd", out[t0:t0 + 128, :], ob[:, 0:D], "s_sp")
            dma("gpsimd", xtb[:], x_tm[t0 + 128:t0 + 256, :], "s_sp")
            dma("gpsimd", ftb[:], ffn_sp[t0 + 128:t0 + 256, :], "s_sp")
            ld_aux2 = snap("s_sp")
            wait("vector", "s_sp", ld_aux2)
            for dh in range(2):
                op("vector", lambda e, dh=dh, xtb=xtb: e.tensor_tensor(
                    ob[:, D + dh * 512:D + (dh + 1) * 512],
                    pb[6 + dh][:, 0:512],
                    xtb[:, dh * 512:(dh + 1) * 512], ALU.add))
                op("vector", lambda e, dh=dh, ftb=ftb: e.tensor_tensor(
                    ob[:, D + dh * 512:D + (dh + 1) * 512],
                    ob[:, D + dh * 512:D + (dh + 1) * 512],
                    ftb[:, dh * 512:(dh + 1) * 512], ALU.add),
                    "s_ve" if dh == 1 else None)
            ve2 = cnt["s_ve"]
            aux_done[ti % 2] = ve2
            wait("gpsimd", "s_ve", ve2)
            dma("gpsimd", out[t0 + 128:t0 + 256, :], ob[:, D:2 * D], "s_sp")
            wait("tensor", "s_ve", ve2)
        wait("gpsimd", "s_sp", snap("s_sp"))
        wait("sync", "s_sp", snap("s_sp"))

        cutname = {1: "p2", 2: "aii", 3: "p3"}.get(phases)
        envcut = os.environ.get("ATH_CUT")
        if envcut:
            cutname = envcut
        if cutname is not None:
            for e in plan:
                plan[e] = plan[e][:marks[cutname][e]]

        @block.sync
        def _(eng):
            for fn in plan["sync"]:
                fn(eng)

        @block.tensor
        def _(eng):
            for fn in plan["tensor"]:
                fn(eng)

        @block.vector
        def _(eng):
            for fn in plan["vector"]:
                fn(eng)

        @block.scalar
        def _(eng):
            for fn in plan["scalar"]:
                fn(eng)

        @block.gpsimd
        def _(eng):
            for fn in plan["gpsimd"]:
                fn(eng)

    return nc


def _off(nc, handle):
    return nc.lookup_mloc(handle).addr


def _host_inputs(inputs):
    x = np.ascontiguousarray(np.asarray(inputs["L0_attn_x"], np.float32))
    w_in = np.ascontiguousarray(np.asarray(inputs["L0_attn_W_in"], np.float32))
    w_conv = np.asarray(inputs["L0_attn_W_conv"], np.float32)
    w_xproj = np.asarray(inputs["L0_attn_W_xproj"], np.float32)
    w_dt = np.ascontiguousarray(np.asarray(inputs["L0_attn_W_dt"], np.float32))
    b_dt = np.asarray(inputs["L0_attn_b_dt"], np.float32)
    w_out = np.ascontiguousarray(np.asarray(inputs["L0_attn_W_out"], np.float32))
    ln_g = np.asarray(inputs["L0_attn_ln_g"], np.float32)
    ln_b = np.asarray(inputs["L0_attn_ln_b"], np.float32)
    gate_w = np.asarray(inputs["L0_ffn_gate_w"], np.float32)
    w1 = np.asarray(inputs["L0_ffn_w1"], np.float32)
    w3 = np.asarray(inputs["L0_ffn_w3"], np.float32)
    w2 = np.asarray(inputs["L0_ffn_w2"], np.float32)
    sw1 = np.asarray(inputs["L0_ffn_sw1"], np.float32)
    sw3 = np.asarray(inputs["L0_ffn_sw3"], np.float32)
    sw2 = np.asarray(inputs["L0_ffn_sw2"], np.float32)

    conv3 = w_conv.reshape(DI, 3)
    conv_c = np.ascontiguousarray(
        conv3.reshape(KC_I, 128, 3).transpose(1, 0, 2).reshape(128, KC_I * 3))
    wxp_h = np.ascontiguousarray(np.concatenate([
        w_xproj[:, 64:80], w_xproj[:, 80:96], w_xproj[:, 0:64],
        np.full((DI, 1), 1.0 / DI, np.float32)], axis=1))
    b_dt_c = np.ascontiguousarray(b_dt.reshape(KC_I, 128).T)
    ln_g_c = np.ascontiguousarray(ln_g.reshape(KC_I, 128).T)
    ln_b_c = np.ascontiguousarray(ln_b.reshape(KC_I, 128).T)
    gate_wT = np.ascontiguousarray(gate_w.T)
    w13 = np.ascontiguousarray(np.concatenate(
        [w1.transpose(0, 2, 1), w3.transpose(0, 2, 1)], axis=2))
    w2t = np.ascontiguousarray(w2.transpose(0, 2, 1))
    sw13 = np.ascontiguousarray(np.concatenate([sw1.T, sw3.T], axis=1))
    sw2t = np.ascontiguousarray(sw2.T)

    consts = np.zeros((128, 8), np.float32)
    consts[:, 0] = 1.0
    consts[:, 1] = EPS
    ident = np.eye(128, dtype=np.float32)
    iota16 = np.tile(np.arange(NE, dtype=np.float32), (128, 1))
    iotamb = np.ascontiguousarray(iota16 - BIG * 10)
    iotab = np.ascontiguousarray(np.tile(iota16, (1, 16)))
    iotambb = np.ascontiguousarray(iotab - BIG * 10)
    iota16 = np.ascontiguousarray(iota16)
    p16 = np.zeros((16, 128), np.float32)
    p16[np.arange(128) % 16, np.arange(128)] = 1.0
    sel16 = np.zeros((16, NE * 128), np.float32)
    for e in range(NE):
        sel16[e, e * 128:(e + 1) * 128] = 1.0
    ones_r = np.ones((1, 128), np.float32)

    shared = dict(w_in=w_in, conv_w=conv_c, wxp=wxp_h, w_dt=w_dt,
                  b_dt_c=b_dt_c, w_out=w_out, ln_g_c=ln_g_c, ln_b_c=ln_b_c,
                  gate_wT=gate_wT, w13=w13, w2t=w2t, sw13=sw13, sw2t=sw2t,
                  consts=consts, ident=ident, iota16=iota16, iotamb=iotamb,
                  iotab=iotab, iotambb=iotambb,
                  p16=p16, sel16=sel16, ones_r=ones_r)

    in_maps = []
    for c in range(8):
        seq, half = c // 2, c % 2
        xs = x[seq]
        x_fmc = np.zeros((D, TAL), np.float32)
        x_fmc[:, 2048:4096] = xs[half * 2048:(half + 1) * 2048].T
        if half == 1:
            x_fmc[:, 0:2048] = xs[0:2048].T
        else:
            x_fmc[:, 4096] = xs[2048]
        m = dict(shared)
        cst = consts.copy()
        cst[:, 2] = float(half)
        m["consts"] = cst
        m["x_fm"] = np.ascontiguousarray(x_fmc)
        m["x_tm"] = np.ascontiguousarray(xs[half * 2048:(half + 1) * 2048])
        in_maps.append(m)
    return in_maps


def kernel(**inputs):
    in_maps = _host_inputs(inputs)
    nc = build_program()
    res = run_bass_kernel_spmd(nc, in_maps, core_ids=list(range(8)))
    outa = np.zeros((4, 4096, D), np.float32)
    for c in range(8):
        seq, half = c // 2, c % 2
        outa[seq, half * 2048:(half + 1) * 2048, :] = res.results[c]["out"]
    return outa
